# version 01
# baseline (speedup 1.0000x reference)
import numpy as np
from contextlib import ExitStack

import concourse.bass as bass
import concourse.mybir as mybir

F32 = mybir.dt.float32
I32 = mybir.dt.int32
AF = mybir.ActivationFunctionType
ALU = mybir.AluOpType
AX = mybir.AxisListType

D = 1024
T = 2048
NB = 2
DC = D // 128
TQ = T // 128
NH = 16
DH = 64
FF = 4096
FC = FF // 128
KSEL = 512
KQ = KSEL // 128
LN_EPS = 1e-5
RMS_EPS = 1e-6
NEG_BIG = -1e30
CW = (6.28125, 0.0019350051879882812, 3.019916050561733e-07)
LOG1E4_32 = float(np.log(10000.0) / 32.0)

ENGS = ("tensor", "vector", "scalar", "gpsimd", "sync")


def _mtb_view(mtb, mi):
    import dataclasses
    base = mtb[:, 0:1]
    pitch = base.ap[0][0]
    return dataclasses.replace(
        base, offset=base.offset + mi * 128,
        ap=[[pitch, 128], [1024, 4], [1, 128]])


def host_prep_weight(w):
    K, M = w.shape
    kc, mc = K // 128, M // 128
    return np.ascontiguousarray(
        w.reshape(kc, 128, mc, 128).transpose(1, 2, 0, 3).reshape(128, mc * kc * 128))


def host_consts():
    ident = np.eye(128, dtype=np.float32)
    causal = np.where(np.tril(np.ones((128, 128), bool)), 0.0, NEG_BIG).astype(np.float32)
    tri_up = np.triu(np.ones((128, 128), np.float32), 1)
    perm = np.zeros((128, 128), np.float32)
    for m in range(128):
        perm[m ^ 32, m] = 1.0
    return ident, causal, tri_up, perm


class Arena:
    def __init__(self, tensor, n):
        self.t, self.n, self.off = tensor, n, 0

    def alloc(self, cols):
        assert self.off + cols <= self.n, f"arena overflow {self.off}+{cols}>{self.n}"
        ap = self.t[:, self.off:self.off + cols]
        self.off += cols
        return ap

    def mark(self):
        return self.off

    def reset(self, mark=0):
        self.off = mark


class MK:
    SERIAL = False

    def __init__(self, nc, es, block, dbg_names=()):
        self.nc, self.es, self.block = nc, es, block
        self.dbg_names = set(dbg_names)
        self.dbg = {}
        self._last_tok = None

        dt = nc.dram_tensor
        self.x_in = dt("x_in", [NB * T, D], F32, kind="ExternalInput").ap()
        self.out = dt("out", [NB * T, D], F32, kind="ExternalOutput").ap()

        self.w = {}
        for name, K, M in [
            ("w_delta", D, D), ("w_select", D, D), ("w_in", D, D),
            ("w_gate", D, D), ("w_out", D, D),
            ("wq1", D, D), ("wk1", D, D), ("wv1", D, D), ("wo1", D, D),
            ("wg1", D, FF), ("wu1", D, FF), ("wd1", FF, D),
            ("wq2", D, D), ("wk2", D, D), ("wv2", D, D), ("wo2", D, D),
            ("wg2", D, FF), ("wu2", D, FF), ("wd2", FF, D),
            ("prior_w", D, 2 * D),
        ]:
            self.w[name] = dt(name, [128, (M // 128) * (K // 128) * 128], F32,
                              kind="ExternalInput").ap()
        for name in ["log_a", "ln1_1", "ln2_1", "ln1_2", "ln2_2", "prior_nw"]:
            self.w[name] = dt(name, [1, D], F32, kind="ExternalInput").ap()
        self.ident_in = dt("ident_in", [128, 128], F32, kind="ExternalInput").ap()
        self.causal_in = dt("causal_in", [128, 128], F32, kind="ExternalInput").ap()
        self.tri_in = dt("tri_in", [128, 128], F32, kind="ExternalInput").ap()
        self.perm_in = dt("perm_in", [128, 128], F32, kind="ExternalInput").ap()

        self.s_x1 = dt("s_x1", [NB * T, D], F32, kind="Internal").ap()
        self.s_h = dt("s_h", [NB * T, D], F32, kind="Internal").ap()
        self.s_x2 = dt("s_x2", [NB * T, D], F32, kind="Internal").ap()
        self.s_proc = dt("s_proc", [NB * T, D], F32, kind="Internal").ap()
        self.s_a = dt("s_a", [NB * 128, DC * T], F32, kind="Internal").ap()
        self.s_q = dt("s_q", [NB * 128, DC * T], F32, kind="Internal").ap()
        self.s_v = dt("s_v", [NB * T, D], F32, kind="Internal").ap()
        self.s_o = dt("s_o", [NB * T, D], F32, kind="Internal").ap()
        self.s_selg = dt("s_selg", [NB * KSEL, D], F32, kind="Internal").ap()
        self.s_ropeg = dt("s_ropeg", [NB * KSEL, 128], F32, kind="Internal").ap()
        self.s_e = dt("s_e", [NB * KSEL, D], F32, kind="Internal").ap()
        self.s_bc = dt("s_bc", [1, T], F32, kind="Internal").ap()
        self.s_bc2 = dt("s_bc2", [1, T], F32, kind="Internal").ap()
        self.cc_in_t = dt("cc_in", [1, 512], F32, kind="Internal")
        self.cc_out_t = dt("cc_out", [1, 512], F32, kind="Internal")

        sb = lambda n, s, dtt=F32: es.enter_context(nc.sbuf_tensor(n, s, dtt))
        self.ident = sb("ident", [128, 128])
        self.causal = sb("causal", [128, 128])
        self.tri_up = sb("tri_up", [128, 128])
        self.perm = sb("perm", [128, 128])
        self.cosP = sb("cosP", [128, T])
        self.ssgnP = sb("ssgnP", [128, T])
        self.cos_tm = sb("cos_tm", [128, TQ * 32])
        self.sin_tm = sb("sin_tm", [128, TQ * 32])
        self.gam = {n: sb("gam_" + n, [128, D]) for n in
                    ["ln1_1", "ln2_1", "ln1_2", "ln2_2", "prior_nw"]}
        self.a_sig = sb("a_sig", [128, DC])
        self.iota512 = sb("iota512", [128, 512])
        self.consts = sb("consts", [128, 8])
        self.s_all = sb("s_all", [128, NB * TQ])
        self.sel_all = sb("sel_all", [128, NB * TQ])
        self.rank_all = sb("rank_all", [128, NB * TQ])
        self.r_all = sb("r_all", [128, NB * TQ])
        self.dsum = sb("dsum", [128, 8])
        self.negshift = sb("negshift", [128, 1])
        self.sc_sel = sb("sc_sel", [128, NB * KQ])
        self.s_sel = sb("s_sel", [128, NB * KQ])
        self.rowbuf = sb("rowbuf", [1, T])
        self.smalls = sb("smalls", [128, 64])
        self.stat = sb("stat", [128, 24])
        self.wbuf = sb("wbuf", [128, 8 * 1024])
        self.pbuf = sb("pbuf", [128, 2048])
        self.iscr = sb("iscr", [128, 64], I32)
        self.ptb = sb("ptb", [128, 2048])
        ARENA_COLS = 24576
        self.arena_t = sb("arena", [128, ARENA_COLS])
        self.arena = Arena(self.arena_t, ARENA_COLS)

        ps = lambda n, c: es.enter_context(nc.psum_tensor(n, [128, c], F32))
        self.ps_s = ps("ps_s", 2048)
        self.ps_a = ps("ps_a", 512)
        self.ps_b = ps("ps_b", 512)
        self.ps_c = ps("ps_c", 512)
        self.ps_d = ps("ps_d", 512)

        self.sem = {}
        for e in ENGS:
            self.sem[e] = [es.enter_context(nc.semaphore("p_" + e)), 0]
        self.streams = {}

        for nm, shape in [
            ("d_tab", [128, 2 * T + 1024]), ("d_dbg2", [128, 160]),
            ("d_stat", [128, 1048]),
            ("d_xn", [128, DC * 512]), ("d_gs", [128, DC * 512]),
            ("d_dec", [128, 2048]),
            ("d_x1", [NB * T, D]), ("d_h", [NB * T, D]),
            ("d_a", [NB * 128, DC * T]), ("d_k", [NB * 128, DC * T]),
            ("d_x2", [NB * T, D]), ("d_proc", [NB * T, D]),
            ("d_s", [128, NB * TQ]), ("d_sel", [128, NB * TQ]),
            ("d_r", [128, NB * TQ]),
            ("d_selg", [NB * KSEL, D]), ("d_rope", [NB * KSEL, 2 * 32]),
            ("d_y2", [NB * KSEL, D]), ("d_u", [NB * KSEL, D]),
            ("d_sc", [128, NB * KQ]),
        ]:
            if nm in self.dbg_names:
                self.dbg[nm] = dt(nm, shape, F32, kind="ExternalOutput").ap()

    def run(self, eng, deps, fn):
        if self.SERIAL:
            deps = list(deps) + ([self._last_tok] if self._last_tok else [])
        sem, _ = self.sem[eng]
        self.sem[eng][1] += 1
        target = self.sem[eng][1]

        def f(e):
            for (s, c) in deps:
                if c > 0:
                    e.wait_ge(s, c)
            last = fn(e)
            assert last is not None, f"section on {eng} returned no instruction"
            last.then_inc(sem, 1)
        getattr(self.block, eng)(f)
        if self.SERIAL:
            self._last_tok = (sem, target)
        return (sem, target)

    def _stream(self, name, q):
        if name not in self.streams:
            self.streams[name] = [
                self.es.enter_context(self.nc.semaphore("st_" + name)), 0, q]
        st = self.streams[name]
        assert st[2] == q, f"stream {name} queue mismatch"
        return st

    def dma(self, name, out, in_, deps=(), q="sync", slow=False):
        return self.dma_many(name, [(out, in_)], deps, q, slow)

    def dma_many(self, name, pairs, deps=(), q="sync", slow=False):
        if self.SERIAL:
            deps = list(deps) + ([self._last_tok] if self._last_tok else [])
        st = self._stream(name, q)
        sem = st[0]
        st[1] += 16 * len(pairs)
        target = st[1]

        def f(e):
            for (s, c) in deps:
                if c > 0:
                    e.wait_ge(s, c)
            for out, in_ in pairs:
                kw = {"allow_slow_non_contiguous": True} if slow else {}
                e.dma_start(out=out, in_=in_, **kw).then_inc(sem, 16)
        getattr(self.block, q)(f)
        if self.SERIAL:
            self._last_tok = (sem, target)
        return (sem, target)

    def dump(self, name, pairs, deps):
        if name not in self.dbg:
            return None
        d = self.dbg[name]
        return self.dma_many("gpsimd", [(fn(d), src) for fn, src in pairs], deps)

    @staticmethod
    def fmc(ap, c, cols=T, width=1):
        return ap[:, c * cols:(c + width) * cols]

    def setup(self):
        self.arena.reset(0)
        tb = self.arena.alloc(T)
        pairs = [
            (self.ident[:], self.ident_in), (self.causal[:], self.causal_in),
            (self.tri_up[:], self.tri_in), (self.perm[:], self.perm_in)]
        pairs += [(self.gam[n][:], self.w[n].to_broadcast([128, D]))
                  for n in ["ln1_1", "ln2_1", "ln1_2", "ln2_2", "prior_nw"]]
        pairs += [(self.smalls[:, c:c + 1],
                   self.w["log_a"][0, c * 128:(c + 1) * 128]
                   .rearrange("(p one) -> p one", one=1)) for c in range(DC)]
        t_init = self.dma_many("init", pairs)

        def g1(g):
            g.iota(self.iota512[:], pattern=[[1, 512]], base=0,
                   channel_multiplier=0, allow_small_or_imprecise_dtypes=True)
            g.iota(tb, pattern=[[1, T]], base=0,
                   channel_multiplier=0, allow_small_or_imprecise_dtypes=True)
            g.iota(self.ptb[:, 0:32], pattern=[[1, 32]], base=0,
                   channel_multiplier=0, allow_small_or_imprecise_dtypes=True)
            return g.iota(self.smalls[:, 8:9], pattern=[[0, 1]], base=0,
                          channel_multiplier=1, allow_small_or_imprecise_dtypes=True)
        tg1 = self.run("gpsimd", [t_init], g1)

        def g_tcols(g):
            last = None
            for qt in range(TQ):
                last = g.tensor_scalar(self.smalls[:, 12 + qt:13 + qt],
                                       self.smalls[:, 8:9], float(qt * 128), None,
                                       op0=ALU.add)
            return last
        tg1 = self.run("gpsimd", [tg1], g_tcols)

        ts_asig = self.run("scalar", [t_init], lambda s: s.activation(
            self.a_sig[:], self.smalls[:, 0:DC], AF.Sigmoid))

        c = float(np.log(10000.0) / 32.0)
        c_hi = float(np.float32(np.float64(c)))
        u = np.float32(c_hi).view(np.uint32) if False else None
        ch = np.frombuffer(np.float32(c).tobytes(), dtype=np.uint32)[0]
        ch12 = np.frombuffer((ch & np.uint32(0xFFFFF000)).tobytes(),
                             dtype=np.float32)[0]
        c_lo = float(np.float32(np.float64(c) - np.float64(ch12)))
        c_hi = float(ch12)

        def v_inv(v):
            j = self.ptb[:, 0:32]
            e1 = self.pbuf[:, 0:32]
            e2 = self.pbuf[:, 32:64]
            v.tensor_scalar(e1, j, -c_hi, None, op0=ALU.mult)
            v.tensor_scalar(e2, j, -c_lo, None, op0=ALU.mult)
            v.drain()
            return v.tensor_tensor(e1, e1, e2, ALU.add)
        tvi = self.run("vector", [tg1], v_inv)
        ts_inv = self.run("scalar", [tvi], lambda s: s.activation(
            self.ptb[:, 32:64], self.pbuf[:, 0:32], AF.Exp))

        def memc(v):
            v.memset(self.consts[:, 0:1], -0.5)
            v.memset(self.consts[:, 1:2], 1.0 / D)
            return v.memset(self.consts[:, 2:3], 0.0)
        tvc = self.run("vector", [], memc)

        CLAMP = 3.1415925
        inv_bc = self.ptb[:, 32:64]

        def reduce_block(v, ang, kq, ki):
            w = ang.shape[-1]
            v.tensor_scalar(kq, ang, 1.0 / (2 * np.pi), None, op0=ALU.mult)
            v.drain()
            v.tensor_copy(self.iscr[:, 0:w], kq)
            v.drain()
            v.tensor_copy(kq, self.iscr[:, 0:w])
            v.drain()
            for cc in CW:
                v.scalar_tensor_tensor(ang, kq, -cc, ang, op0=ALU.mult, op1=ALU.add)
                v.drain()
            return v.tensor_scalar(ang, ang, CLAMP, -CLAMP, op0=ALU.min, op1=ALU.max)

        def build_tm(dst, shift, deps):
            def vv(v):
                last = None
                for qt in range(TQ):
                    tcol = self.smalls[:, 12 + qt:13 + qt]
                    ang = dst[:, qt * 32:(qt + 1) * 32]
                    if shift == 0.0:
                        v.tensor_scalar(ang, inv_bc, tcol, None, op0=ALU.mult)
                    else:
                        v.tensor_scalar(ang, inv_bc, tcol, shift,
                                        op0=ALU.mult, op1=ALU.add)
                    v.drain()
                    if qt == 1 and shift == 0.0 and "d_dbg2" in self.dbg:
                        v.tensor_copy(self.ptb[:, 1024:1056], ang)
                        v.drain()
                    last = reduce_block(v, ang, self.pbuf[:, 0:32],
                                        self.pbuf[:, 32:64])
                    if qt == 1 and shift == 0.0 and "d_dbg2" in self.dbg:
                        v.tensor_copy(self.ptb[:, 1056:1088], self.pbuf[:, 0:32])
                        v.tensor_copy(self.ptb[:, 1088:1120], ang)
                return last
            tv = self.run("vector", deps, vv)

            def ss(s):
                last = None
                for qt in range(TQ):
                    a = dst[:, qt * 32:(qt + 1) * 32]
                    last = s.activation(a, a, AF.Sin)
                return last
            return self.run("scalar", [tv], ss)

        t_sintm = build_tm(self.sin_tm, 0.0, [ts_inv, tvc, ts_asig])
        t_costm = build_tm(self.cos_tm, np.pi / 2, [t_sintm])

        def t_fm(t):
            last = None
            for g4 in range(4):
                ps = self.ps_c if g4 % 2 == 0 else self.ps_d
                for i in range(4):
                    qt = g4 * 4 + i
                    t.transpose(ps[0:32, i * 128:(i + 1) * 128],
                                self.sin_tm[:, qt * 32:(qt + 1) * 32],
                                self.ident[:])
                    last = t.transpose(self.ps_s[0:32, i * 128:(i + 1) * 128],
                                       self.cos_tm[:, qt * 32:(qt + 1) * 32],
                                       self.ident[:])
            return last
        toks = []
        for g4 in range(4):
            ps = self.ps_c if g4 % 2 == 0 else self.ps_d

            def tf(t, g4=g4, ps=ps):
                last = None
                for i in range(4):
                    qt = g4 * 4 + i
                    t.transpose(ps[0:32, i * 128:(i + 1) * 128],
                                self.sin_tm[:, qt * 32:(qt + 1) * 32],
                                self.ident[:])
                    last = t.transpose(self.ps_s[0:32, (g4 % 2) * 512 + i * 128:
                                                 (g4 % 2) * 512 + (i + 1) * 128],
                                       self.cos_tm[:, qt * 32:(qt + 1) * 32],
                                       self.ident[:])
                return last
            dep = [t_costm] + ([toks[-2]] if len(toks) >= 2 else [])
            tt = self.run("tensor", dep, tf)

            def ve(v, g4=g4, ps=ps):
                v.tensor_copy(self.ssgnP[0:32, g4 * 512:(g4 + 1) * 512],
                              ps[0:32, 0:512])
                return v.tensor_copy(self.cosP[0:32, g4 * 512:(g4 + 1) * 512],
                                     self.ps_s[0:32, (g4 % 2) * 512:
                                               (g4 % 2) * 512 + 512])
            toks.append(self.run("vector", [tt], ve))
        t_rows = self.join(toks)

        rep_pairs = []
        for g0 in (32, 64, 96):
            rep_pairs.append((self.ssgnP[g0:g0 + 32, :], self.ssgnP[0:32, :]))
            rep_pairs.append((self.cosP[g0:g0 + 32, :], self.cosP[0:32, :]))
        t_rep = self.dma_many("init", rep_pairs, [t_rows])

        def s_sgn(s):
            s.mul(self.ssgnP[0:32, :], self.ssgnP[0:32, :], -1.0)
            return s.mul(self.ssgnP[64:96, :], self.ssgnP[64:96, :], -1.0)
        t_sgn = self.run("scalar", [t_rep], s_sgn)

        if "d_dbg2" in self.dbg:
            self.dma("dbg", self.dbg["d_dbg2"][:, 0:64], self.smalls[:], [t_sgn], q="gpsimd")
            self.dma("dbg", self.dbg["d_dbg2"][:, 64:160], self.ptb[:, 1024:1120],
                     [t_sgn], q="gpsimd")
        if "d_tab" in self.dbg:
            self.dma("dbg", self.dbg["d_tab"][:, 0:T], self.cosP[:], [t_sgn], q="gpsimd")
            self.dma("dbg", self.dbg["d_tab"][:, T:2 * T], self.ssgnP[:], [t_sgn], q="gpsimd")
            self.dma("dbg", self.dbg["d_tab"][:, 2 * T:2 * T + 512], self.cos_tm[:],
                     [t_sgn], q="gpsimd")
            self.dma("dbg", self.dbg["d_tab"][:, 2 * T + 512:2 * T + 1024],
                     self.sin_tm[:], [t_sgn], q="gpsimd")
        self.arena.reset(0)
        self.t_setup = [t_sgn, t_init, tg1]
        return self.t_setup

    def ln_tile(self, x_ap, out_ap, scratch_ap, eps_col, deps, gamma=None, rms=False,
                stat_base=0):
        st = self.stat
        c_sum = st[:, stat_base + 0:stat_base + 1]
        c_ss = st[:, stat_base + 1:stat_base + 2]
        c_negm = st[:, stat_base + 2:stat_base + 3]
        c_var = st[:, stat_base + 3:stat_base + 4]
        c_m2 = st[:, stat_base + 4:stat_base + 5]
        c_sq = st[:, stat_base + 5:stat_base + 6]
        c_rstd0 = st[:, stat_base + 6:stat_base + 7]
        c_rstd = st[:, stat_base + 7:stat_base + 8]

        ts0 = self.run("scalar", deps, lambda s: s.activation(
            scratch_ap, x_ap, AF.Square, accum_out=c_ss))
        if not rms:
            tv0 = self.run("vector", deps, lambda v: v.tensor_reduce(
                c_sum, x_ap, axis=AX.X, op=ALU.add))
            tneg = self.run("scalar", [tv0], lambda s: s.mul(c_negm, c_sum, -1.0 / D))

            def v1(v):
                v.tensor_scalar(c_var, c_ss, 1.0 / D, None, op0=ALU.mult)
                v.tensor_tensor(c_m2, c_negm, c_negm, ALU.mult)
                v.drain()
                return v.tensor_tensor(c_var, c_var, c_m2, ALU.subtract)
            tv1 = self.run("vector", [tneg, ts0], v1)
        else:
            tv1 = self.run("vector", [ts0], lambda v: v.tensor_scalar(
                c_var, c_ss, 1.0 / D, None, op0=ALU.mult))

        ts1 = self.run("scalar", [tv1], lambda s: s.activation(
            c_sq, c_var, AF.Sqrt, bias=eps_col))
        tv2a = self.run("vector", [ts1], lambda v: v.reciprocal(c_rstd0, c_sq))
        tv2 = self.run("scalar", [tv2a], lambda s: s.copy(c_rstd, c_rstd0))

        nfree = x_ap.shape[-1]
        rb = c_rstd.to_broadcast([128, nfree])

        def vap(v):
            if rms:
                last = v.tensor_tensor(out_ap, x_ap, rb, ALU.mult)
            else:
                v.tensor_scalar(out_ap, x_ap, c_negm, None, op0=ALU.add)
                last = v.tensor_tensor(out_ap, out_ap, rb, ALU.mult)
            if gamma is not None:
                last = v.tensor_tensor(out_ap, out_ap, gamma, ALU.mult)
            return last
        return self.run("vector", [tv2], vap)

    def tp_tile_to_fm(self, tile_ap, dst_fn, deps, nchunk=DC, evac_eng="vector"):
        tok = None
        toks = []
        for g0 in range(0, nchunk, 4):
            ps = self.ps_c if (g0 // 4) % 2 == 0 else self.ps_d
            n = min(4, nchunk - g0)

            def tf(t, g0=g0, ps=ps, n=n):
                last = None
                for i in range(n):
                    last = t.transpose(ps[:, i * 128:(i + 1) * 128],
                                       tile_ap[:, (g0 + i) * 128:(g0 + i + 1) * 128],
                                       self.ident[:])
                return last
            dep = list(deps) + ([toks[-2]] if len(toks) >= 2 else [])
            tt = self.run("tensor", dep, tf)

            def ve(e, g0=g0, ps=ps, n=n):
                last = None
                for i in range(n):
                    last = e.tensor_copy(dst_fn(g0 + i), ps[:, i * 128:(i + 1) * 128])
                return last
            tok = self.run(evac_eng, [tt], ve)
            toks.append(tok)
        return self.join(toks) if len(toks) > 1 else tok

    def join(self, toks):
        toks = [t for t in toks if t]
        if not toks:
            return None
        if len(toks) == 1:
            return toks[0]
        return self.run("gpsimd", toks, lambda g: g.memset(self.smalls[:, 63:64], 0.0))

    def load_wblock(self, wname, m, kc, slot, stream, deps=()):
        cols = kc * 128
        dst = self.wbuf[:, slot * 1024: slot * 1024 + cols]
        src = self.w[wname][:, m * cols:(m + 1) * cols]
        return dst, self.dma(stream, dst, src, deps)

    def dump_dram(self, name, src_ap, rows, deps):
        if name not in self.dbg:
            return None
        return self.dma("dbg", self.dbg[name][rows[0]:rows[1], :],
                        src_ap[rows[0]:rows[1], :], deps, q="gpsimd")

    def phaseA(self, b, t_prev=None):
        A = self.arena
        A.reset(0)
        xn = A.alloc(DC * 512)
        gs = A.alloc(DC * 512)
        dec = A.alloc(512)
        upd = A.alloc(512)
        gat = A.alloc(512)
        sts = A.alloc(512)
        tmp1 = A.alloc(512)
        tmp2 = A.alloc(512)
        yfm = A.alloc(DC * 512)
        stage_in = A.alloc(4 * D)
        stage_out = A.alloc(4 * D)

        eps_ln = self.smalls[:, 30:31]
        tini = self.run("vector", [], lambda v: v.memset(eps_ln, LN_EPS))

        row0 = b * T
        W4 = ["w_delta", "w_select", "w_in", "w_gate"]
        t_stage_free = t_prev
        t_carry = [None] * DC
        prev = {"T4": t_prev} if t_prev else {}

        for sp in range(4):
            r0 = row0 + sp * 512
            tx = self.dma_many("xin", [
                (stage_in[:, i * D:(i + 1) * D],
                 self.x_in[r0 + i * 128: r0 + (i + 1) * 128, :]) for i in range(4)],
                deps=[t_stage_free] if t_stage_free else [])

            if b == 0 and sp == 0 and "d_xn" in self.dbg:
                pass
            tok_fm = []
            for i in range(4):
                xt = stage_in[:, i * D:(i + 1) * D]
                ot = stage_out[:, i * D:(i + 1) * D]
                tln = self.ln_tile(xt, ot, ot, eps_ln, [tx, tini])
                if b == 0 and sp == 0 and i == 0 and "d_stat" in self.dbg:
                    self.dma("dbg", self.dbg["d_stat"][:, 0:24], self.stat[:],
                             [tln], q="gpsimd")
                    self.dma("dbg", self.dbg["d_stat"][:, 24:1048],
                             ot, [tln], q="gpsimd")
                ttr = self.tp_tile_to_fm(
                    ot, lambda c, i=i: xn[:, c * 512 + i * 128: c * 512 + (i + 1) * 128],
                    [tln] + ([tok_fm[-1]] if tok_fm else []))
                tok_fm.append(ttr)
            t_xn = self.join(tok_fm)

            if b == 0 and sp == 0 and "d_xn" in self.dbg:
                self.dma("dbg", self.dbg["d_xn"], xn, [t_xn], q="gpsimd")
            wtok = {}
            for wi, wn in enumerate(W4):
                wtok[(wn, 0)] = self.load_wblock(
                    wn, 0, DC, wi, f"wA{wi}",
                    deps=[prev["T4"]] if "T4" in prev else [])

            gs_done = []
            for e in range(DC):
                if e + 1 < DC:
                    base_slot = 4 if e % 2 == 0 else 0
                    pdep = [prev["T4"]] if "T4" in prev else []
                    sset = "wB" if e % 2 == 0 else "wA"
                    for wi, wn in enumerate(W4):
                        wtok[(wn, e + 1)] = self.load_wblock(
                            wn, e + 1, DC, base_slot + wi, f"{sset}{wi}", deps=pdep)

                def accum_mm(wap, ps, rhs):
                    def f(t):
                        last = None
                        for k in range(DC):
                            last = t.matmul(ps[:, 0:512], wap[:, k * 128:(k + 1) * 128],
                                            rhs[:, k * 512:(k + 1) * 512],
                                            start=(k == 0), stop=(k == DC - 1))
                        return last
                    return f

                wapA, wtA = wtok[(W4[0], e)]
                T1 = self.run("tensor",
                              [wtA, t_xn] + ([prev["S4"]] if "S4" in prev else []),
                              accum_mm(wapA, self.ps_a, xn))
                S1 = self.run("scalar",
                              [T1] + ([prev["V2"]] if "V2" in prev else []),
                              lambda s: (s.activation(tmp1, self.ps_a[:, 0:512], AF.Abs),
                                         s.activation(tmp1, tmp1, AF.Exp, scale=-1.0),
                                         s.activation(tmp1, tmp1, AF.Ln, bias=1.0))[-1])

                def v1f(v, e=e):
                    v.scalar_tensor_tensor(tmp2, self.ps_a[:, 0:512], 0.0, tmp1,
                                           op0=ALU.max, op1=ALU.add)
                    return v.tensor_scalar(tmp2, tmp2, 1e-4, self.a_sig[:, e:e + 1],
                                           op0=ALU.max, op1=ALU.mult)
                V1 = self.run("vector", [S1], v1f)
                S2 = self.run("scalar", [V1], lambda s: s.activation(
                    dec, tmp2, AF.Exp, scale=-1.0))

                wapB, wtB = wtok[(W4[1], e)]
                T2 = self.run("tensor",
                              [wtB] + ([prev["S5"]] if "S5" in prev else []),
                              accum_mm(wapB, self.ps_b, xn))
                S3 = self.run("scalar", [T2, V1], lambda s: s.activation(
                    tmp1, self.ps_b[:, 0:512], AF.Sigmoid))

                wapC, wtC = wtok[(W4[2], e)]
                T3 = self.run("tensor", [wtC, V1], accum_mm(wapC, self.ps_a, xn))
                S4 = self.run("scalar", [T3, S2], lambda s: s.activation(
                    tmp2, self.ps_a[:, 0:512], AF.Tanh))
                V2 = self.run("vector", [S4, S3], lambda v: v.tensor_tensor(
                    upd, tmp1, tmp2, ALU.mult))

                wapD, wtD = wtok[(W4[3], e)]
                T4 = self.run("tensor", [wtD, S3], accum_mm(wapD, self.ps_b, xn))
                S5 = self.run("scalar",
                              [T4] + ([prev["V4"]] if "V4" in prev else []),
                              lambda s: s.activation(gat, self.ps_b[:, 0:512],
                                                     AF.Sigmoid))

                carry = self.stat[:, 16 + e:17 + e]
                scan_deps = [V2, S2] + ([prev["S6"]] if "S6" in prev else [])
                if sp > 0 and t_carry[e]:
                    scan_deps.append(t_carry[e])
                if sp == 0:
                    V3 = self.run("vector", scan_deps, lambda v: v.tensor_tensor_scan(
                        sts, dec, upd, 0.0, op0=ALU.mult, op1=ALU.add))
                else:
                    V3 = self.run("vector", scan_deps,
                                  lambda v, carry=carry: v.tensor_tensor_scan(
                                      sts, dec, upd, carry, op0=ALU.mult, op1=ALU.add))
                S6 = self.run("scalar", [V3], lambda s, carry=carry: s.copy(
                    carry, sts[:, 511:512]))
                t_carry[e] = S6
                V4 = self.run("vector", [V3, S5], lambda v, e=e: v.tensor_tensor(
                    gs[:, e * 512:(e + 1) * 512], gat, sts, ALU.mult))
                gs_done.append(V4)
                prev = {"S4": S4, "S5": S5, "T4": T4, "V2": V2, "V4": V4, "S6": S6}

            if b == 0 and sp == 0 and "d_gs" in self.dbg:
                self.dma("dbg", self.dbg["d_gs"], gs, [self.join(gs_done)], q="gpsimd")
                if "d_dec" in self.dbg:
                    self.dma("dbg", self.dbg["d_dec"][:, 0:512], dec,
                             [prev["S6"]], q="gpsimd")
                    self.dma("dbg", self.dbg["d_dec"][:, 512:1024], upd,
                             [prev["S6"]], q="gpsimd")
                    self.dma("dbg", self.dbg["d_dec"][:, 1024:1536], gat,
                             [prev["S6"]], q="gpsimd")
                    self.dma("dbg", self.dbg["d_dec"][:, 1536:2048], sts,
                             [prev["S6"]], q="gpsimd")
            t_gs = self.join(gs_done)

            wo = {}
            wo[0] = self.load_wblock("w_out", 0, DC, 0, "wA0", deps=[prev["T4"]])
            ev = {}
            tmm_hist = {}
            for f in range(DC):
                if f + 1 < DC:
                    pdep = [prev["T4"]] if f == 0 else [tmm_hist[f - 1]]
                    wo[f + 1] = self.load_wblock(
                        "w_out", f + 1, DC, 1 if f % 2 == 0 else 0,
                        "wA1" if f % 2 == 0 else "wA0", deps=pdep)
                wap, wt = wo[f]
                psX = self.ps_a if f % 2 == 0 else self.ps_b
                dep = [wt, t_gs]
                if f >= 2:
                    dep.append(ev[f - 2])
                elif f < 2 and "S4" in prev:
                    dep.append(prev["S5" if f == 1 else "S4"])
                tmm = self.run("tensor", dep, accum_mm(wap, psX, gs))
                tmm_hist[f] = tmm
                ev[f] = self.run("scalar", [tmm], lambda s, f=f, psX=psX: s.copy(
                    yfm[:, f * 512:(f + 1) * 512], psX[:, 0:512]))

            sp_end = []
            prev_va = None
            for i in range(4):
                def tf(t, i=i):
                    last = None
                    for c in range(4):
                        last = t.transpose(
                            self.ps_c[:, c * 128:(c + 1) * 128],
                            yfm[:, c * 512 + i * 128: c * 512 + (i + 1) * 128],
                            self.ident[:])
                    for c in range(4, 8):
                        last = t.transpose(
                            self.ps_d[:, (c - 4) * 128:(c - 3) * 128],
                            yfm[:, c * 512 + i * 128: c * 512 + (i + 1) * 128],
                            self.ident[:])
                    return last
                ttr = self.run("tensor",
                               [ev[DC - 1], ev[DC - 2]] + ([prev_va] if prev_va else []),
                               tf)
                x1t = stage_out[:, i * D:(i + 1) * D]
                ht = stage_in[:, i * D:(i + 1) * D]

                def va(v, i=i, x1t=x1t):
                    v.tensor_tensor(x1t[:, 0:512], self.ps_c[:],
                                    stage_in[:, i * D:i * D + 512], ALU.add)
                    return v.tensor_tensor(x1t[:, 512:1024], self.ps_d[:],
                                           stage_in[:, i * D + 512:(i + 1) * D], ALU.add)
                tva = self.run("vector", [ttr], va)
                prev_va = tva
                tst = self.dma("sp1", self.s_x1[r0 + i * 128:r0 + (i + 1) * 128, :],
                               x1t, [tva], q="gpsimd")
                tln = self.ln_tile(x1t, ht, ht, eps_ln, [tva], stat_base=8)
                tsh = self.dma("sph", self.s_h[r0 + i * 128:r0 + (i + 1) * 128, :],
                               ht, [tln], q="gpsimd")
                sp_end += [tst, tsh]
                last_ttr = ttr
            prev = dict(prev)
            prev["S4"] = ev[DC - 2]
            prev["S5"] = ev[DC - 1]
            prev["T4"] = last_ttr
            t_stage_free = self.join(sp_end)

        self.dump_dram("d_x1", self.s_x1, (row0, row0 + T), [t_stage_free])
        self.dump_dram("d_h", self.s_h, (row0, row0 + T), [t_stage_free])
        return t_stage_free

    def phaseB(self, b, t_prev):
        A = self.arena
        A.reset(0)
        region0 = A.alloc(DC * T)
        astr = A.alloc(DC * 512)
        chunkbuf = A.alloc(2048)
        row0 = b * T
        eps_rms = self.smalls[:, 31:32]
        t0 = self.run("vector", [t_prev], lambda v: v.memset(eps_rms, RMS_EPS))

        stage_in = region0[:, 0:4 * D]
        stage_out = region0[:, 4 * D:8 * D]
        afm = region0[:, 8 * D:8 * D + 4096]
        t_b1 = t0
        a_sp = []
        for sp in range(4):
            r0 = row0 + sp * 512
            th = self.dma_many("hld", [
                (stage_in[:, i * D:(i + 1) * D],
                 self.s_h[r0 + i * 128:r0 + (i + 1) * 128, :]) for i in range(4)],
                deps=[t_b1], q="sync")
            toks = []
            for i in range(4):
                ht = stage_in[:, i * D:(i + 1) * D]
                at = stage_out[:, i * D:(i + 1) * D]
                tln = self.ln_tile(ht, at, at, eps_rms, [th], rms=True,
                                   gamma=self.gam["ln1_1"][:])
                ttr = self.tp_tile_to_fm(
                    at, lambda c, i=i: afm[:, c * 512 + i * 128:c * 512 + (i + 1) * 128],
                    [tln] + ([toks[-1]] if toks else []))
                toks.append(ttr)
            t_sp = self.dma_many("asp", [
                (self.s_a[b * 128:(b + 1) * 128, c * T + sp * 512:c * T + (sp + 1) * 512],
                 afm[:, c * 512:(c + 1) * 512]) for c in range(DC)],
                deps=[self.join(toks)], q="gpsimd")
            t_b1 = self.join([t_sp, th])
            a_sp.append(t_sp)
        t_a = self.join(a_sp)
        self.dump_dram("d_a", self.s_a, (b * 128, (b + 1) * 128), [t_a])

        K_fm = region0
        rope_tmp = chunkbuf[:, 0:512]
        slice_buf = chunkbuf[:, 512:1024]
        vstage = chunkbuf[:, 1024:1536]

        def wfull(wname, deps):
            prs = [(self.wbuf[:, m * 1024:(m + 1) * 1024],
                    self.w[wname][:, m * 1024:(m + 1) * 1024]) for m in range(DC)]
            return self.dma_many("wfull", prs, deps, q="sync")

        def rope_slice(dst_ap, ps_sw, cs_slice, deps):
            def vf(v):
                v.tensor_tensor(rope_tmp, ps_sw, self.ssgnP[:, cs_slice], ALU.mult)
                v.tensor_tensor(dst_ap, dst_ap, self.cosP[:, cs_slice], ALU.mult)
                return v.tensor_tensor(dst_ap, dst_ap, rope_tmp, ALU.add)
            return self.run("vector", deps, vf)

        t_wprev = t_a
        t_rope_toks = []
        t_vq_done = []
        prev_psum = {}
        for proj, wname in (("k", "wk1"), ("q", "wq1"), ("v", "wv1")):
            t_w = wfull(wname, [t_wprev])
            t_an_prev = None
            for n in range(4):
                cs = slice(n * 512, (n + 1) * 512)
                t_an = self.dma_many("ast", [
                    (astr[:, c * 512:(c + 1) * 512],
                     self.s_a[b * 128:(b + 1) * 128, c * T + n * 512:c * T + (n + 1) * 512])
                    for c in range(DC)],
                    deps=[t_an_prev or t_a, t_w], q="sync")
                for m in range(DC):
                    ps = self.ps_a if m % 2 == 0 else self.ps_b
                    dep = [t_an]
                    key = ("mm", id(ps))
                    if key in prev_psum:
                        dep.append(prev_psum[key])

                    def mmf(t, m=m, ps=ps):
                        last = None
                        for k in range(DC):
                            last = t.matmul(ps[:, 0:512],
                                            self.wbuf[:, m * 1024 + k * 128:
                                                       m * 1024 + (k + 1) * 128],
                                            astr[:, k * 512:(k + 1) * 512],
                                            start=(k == 0), stop=(k == DC - 1))
                        return last
                    tmm = self.run("tensor", dep, mmf)

                    if proj == "k":
                        dst = K_fm[:, m * T + n * 512:m * T + (n + 1) * 512]
                        tev = self.run("scalar", [tmm], lambda s, dst=dst, ps=ps:
                                       s.copy(dst, ps[:, 0:512]))
                        psw = self.ps_c if m % 2 == 0 else self.ps_d
                        depw = [tev]
                        keyw = ("sw", id(psw))
                        if keyw in prev_psum:
                            depw.append(prev_psum[keyw])
                        tsw = self.run("tensor", depw, lambda t, dst=dst, psw=psw:
                                       t.matmul(psw[:, 0:512], self.perm[:], dst,
                                                start=True, stop=True))
                        trp = rope_slice(dst, (self.ps_c if m % 2 == 0 else
                                               self.ps_d)[:, 0:512], cs, [tsw])
                        prev_psum[("mm", id(ps))] = tev
                        prev_psum[("sw", id(psw))] = trp
                        t_rope_toks.append(trp)
                    elif proj == "q":
                        tev = self.run("scalar", [tmm], lambda s, ps=ps:
                                       s.copy(slice_buf, ps[:, 0:512]))
                        psw = self.ps_c if m % 2 == 0 else self.ps_d
                        depw = [tev]
                        keyw = ("sw", id(psw))
                        if keyw in prev_psum:
                            depw.append(prev_psum[keyw])
                        tsw = self.run("tensor", depw, lambda t, psw=psw:
                                       t.matmul(psw[:, 0:512], self.perm[:], slice_buf,
                                                start=True, stop=True))
                        trp = rope_slice(slice_buf, (self.ps_c if m % 2 == 0 else
                                                     self.ps_d)[:, 0:512], cs, [tsw])
                        tq = self.dma("qsp",
                                      self.s_q[b * 128:(b + 1) * 128,
                                               m * T + n * 512:m * T + (n + 1) * 512],
                                      slice_buf, [trp], q="gpsimd")
                        prev_psum[("mm", id(ps))] = trp
                        prev_psum[("sw", id(psw))] = trp
                        t_vq_done.append(tq)
                        prev_psum[("mm", id(self.ps_a if (m + 1) % 2 == 0 else
                                        self.ps_b))] = tq
                    else:
                        psw = self.ps_c if m % 2 == 0 else self.ps_d
                        depw = [tmm]
                        keyw = ("sw", id(psw))
                        if keyw in prev_psum:
                            depw.append(prev_psum[keyw])
                        tev = self.run("scalar", [tmm], lambda s, ps=ps:
                                       s.copy(slice_buf, ps[:, 0:512]))

                        def tvt(t, psw=psw):
                            last = None
                            for i in range(4):
                                last = t.transpose(psw[:, i * 128:(i + 1) * 128],
                                                   slice_buf[:, i * 128:(i + 1) * 128],
                                                   self.ident[:])
                            return last
                        ttp = self.run("tensor", [tev] + depw[1:], tvt)
                        tvs = self.run("vector", [ttp], lambda v, psw=psw:
                                       v.tensor_copy(vstage, psw[:, 0:512]))
                        pr = []
                        for i in range(4):
                            rr = row0 + n * 512 + i * 128
                            pr.append((self.s_v[rr:rr + 128, m * 128:(m + 1) * 128],
                                       vstage[:, i * 128:(i + 1) * 128]))
                        tvsp = self.dma_many("vsp", pr, [tvs], q="gpsimd")
                        prev_psum[("mm", id(ps))] = tev
                        prev_psum[("sw", id(psw))] = tvsp
                        t_vq_done.append(tvsp)
                t_an_prev = tmm
            t_wprev = self.join([tmm, t_rope_toks[-1] if proj == "k" else t_vq_done[-1]])
        t_K = self.join(t_rope_toks)
        t_VQ = self.join(t_vq_done)
        if "d_k" in self.dbg:
            def dk(g):
                return g.dma_start(out=self.dbg["d_k"][b * 128:(b + 1) * 128, :],
                                   in_=K_fm)
            self.dma("dbg", self.dbg["d_k"][b * 128:(b + 1) * 128, :], K_fm,
                     [t_K], q="gpsimd")

        qtile = astr[:, 0:1024]
        vbuf = [astr[:, 1024 + i * 512: 1024 + (i + 1) * 512] for i in range(2)]
        o_tm = astr[:, 2048 + 512:2048 + 512 + 1024]
        st = self.stat
        negm = st[:, 0:1]
        negm8 = st[:, 1:2]
        ell = st[:, 2:3]
        rell = lambda h: st[:, 8 + (h % 8):9 + (h % 8)]

        t_ob = [t_K, t_VQ]
        t_prev_o = None
        t_vload = {}
        t_exp_prev = None
        t_pt_prev = None
        t_av_prev = {}
        for qt in range(TQ):
            klen = (qt + 1) * 128
            G = (qt + 4) // 4
            tq = self.dma_many("qld", [
                (qtile[:, c * 128:(c + 1) * 128],
                 self.s_q[b * 128:(b + 1) * 128,
                          c * T + qt * 128:c * T + (qt + 1) * 128])
                for c in range(DC)],
                deps=t_ob + ([t_prev_o] if t_prev_o else []), q="sync")
            ev_toks = []
            for h in range(NH):
                hp, hh = h // 2, h % 2
                def smm(t, hp=hp, hh=hh, klen=klen):
                    last = None
                    for c0 in range(0, klen, 512):
                        w = min(512, klen - c0)
                        last = t.matmul(
                            self.ps_s[:, c0:c0 + w],
                            qtile[hh * 64:(hh + 1) * 64, hp * 128:(hp + 1) * 128],
                            K_fm[hh * 64:(hh + 1) * 64, hp * T + c0:hp * T + c0 + w],
                            start=True, stop=True)
                    return last
                dep_s = [tq] + ([t_exp_prev] if t_exp_prev else [])
                tS = self.run("tensor", dep_s, smm)
                def vmx(v, qt=qt, klen=klen):
                    v.tensor_tensor(self.ps_s[:, qt * 128:klen],
                                    self.ps_s[:, qt * 128:klen],
                                    self.causal[:], ALU.add)
                    v.drain()
                    v.tensor_reduce(negm, self.ps_s[:, 0:klen], axis=AX.X,
                                    op=ALU.max, negate=True)
                    v.drain()
                    return v.tensor_scalar(negm8, negm, 0.125, None, op0=ALU.mult)
                tM = self.run("vector", [tS], vmx)
                tE = self.run("scalar", [tM], lambda s, klen=klen: s.activation(
                    self.pbuf[:, 0:klen], self.ps_s[:, 0:klen], AF.Exp,
                    scale=0.125, bias=negm8, accum_out=ell))
                t_exp_prev = tE
                tR = self.run("vector", [tE], lambda v, h=h: v.reciprocal(rell(h), ell))
                first_av = True
                for g in range(G):
                    kn = min(512, klen - g * 512)
                    nt = (kn + 127) // 128
                    psb = self.ps_c if g % 2 == 0 else self.ps_d

                    def tptf(t, g=g, nt=nt, psb=psb):
                        last = None
                        for i in range(nt):
                            kt = g * 4 + i
                            last = t.transpose(psb[:, i * 128:(i + 1) * 128],
                                               self.pbuf[:, kt * 128:(kt + 1) * 128],
                                               self.ident[:])
                        return last
                    depp = [tE] + ([t_pt_prev] if t_pt_prev else [])
                    tT = self.run("tensor", depp, tptf)
                    tEv = self.run("vector", [tT], lambda v, g=g, nt=nt, psb=psb:
                                   v.tensor_copy(self.ptb[:, g * 512:g * 512 + nt * 128],
                                                 psb[:, 0:nt * 128]))
                    t_pt_prev = tEv
                    pr = []
                    for i in range(nt):
                        rr = row0 + (g * 4 + i) * 128
                        pr.append((vbuf[g % 2][:, i * 128:(i + 1) * 128],
                                   self.s_v[rr:rr + 128, hp * 128:(hp + 1) * 128]))
                    depv = [t_VQ]
                    if (g % 2) in t_av_prev:
                        depv.append(t_av_prev[g % 2])
                    t_vload[g] = self.dma_many("vld%d" % (g % 2), pr, depv, q="sync")

                    def tavf(t, g=g, nt=nt, hh=hh, h=h, first=first_av,
                             last_g=(g == G - 1)):
                        lasti = None
                        for i in range(nt):
                            kt = g * 4 + i
                            lasti = t.matmul(
                                self.ps_b[:, (h % 8) * 64:(h % 8) * 64 + 64],
                                self.ptb[:, kt * 128:(kt + 1) * 128],
                                vbuf[g % 2][:, i * 128 + hh * 64:i * 128 + hh * 64 + 64],
                                start=(first and i == 0),
                                stop=(last_g and i == nt - 1))
                        return lasti
                    depav = [tEv, t_vload[g], tR]
                    if h % 8 == 0 and first_av and ev_toks:
                        depav.append(ev_toks[-1])
                    tAV = self.run("tensor", depav, tavf)
                    t_av_prev[g % 2] = tAV
                    first_av = False
                if h % 8 == 7:
                    base = (h // 8) * 512

                    def sev(s, h=h, base=base):
                        last = None
                        for hs in range(8):
                            hq = (h // 8) * 8 + hs
                            last = s.activation(
                                o_tm[:, hq * 64:(hq + 1) * 64],
                                self.ps_b[:, hs * 64:(hs + 1) * 64],
                                AF.Copy, scale=rell(hq))
                        return last
                    ev_toks.append(self.run("scalar", [tAV], sev))
            t_prev_o = self.dma("osp", self.s_o[row0 + qt * 128:row0 + (qt + 1) * 128, :],
                                o_tm, [ev_toks[-1]], q="gpsimd")
        t_attn = t_prev_o

        t_w = wfull("wo1", [t_attn])
        ofm = astr[:, 0:4096]
        x2fm = region0[:, 0:4096]
        stage_o = region0[:, 4096:4096 + 4096]
        stage_h = region0[:, 8192:8192 + 4096]
        t_sl_prev = None
        sl_toks = []
        for sl in range(4):
            r0 = row0 + sl * 512
            tld = self.dma_many("old", [
                (stage_o[:, i * D:(i + 1) * D],
                 self.s_o[r0 + i * 128:r0 + (i + 1) * 128, :]) for i in range(4)],
                deps=[t_attn] + ([t_sl_prev] if t_sl_prev else []), q="sync")
            tldh = self.dma_many("hld2", [
                (stage_h[:, i * D:(i + 1) * D],
                 self.s_h[r0 + i * 128:r0 + (i + 1) * 128, :]) for i in range(4)],
                deps=[t_sl_prev] if t_sl_prev else [], q="sync")
            ttoks = []
            for i in range(4):
                ttr = self.tp_tile_to_fm(
                    stage_o[:, i * D:(i + 1) * D],
                    lambda c, i=i: ofm[:, c * 512 + i * 128:c * 512 + (i + 1) * 128],
                    [tld] + ([ttoks[-1]] if ttoks else []))
                ttoks.append(ttr)
            t_ofm = self.join(ttoks)
            ev = {}
            for m in range(DC):
                ps = self.ps_a if m % 2 == 0 else self.ps_b
                dep = [t_w, t_ofm]
                if m >= 2:
                    dep.append(ev[m - 2])

                def mmf(t, m=m, ps=ps):
                    last = None
                    for k in range(DC):
                        last = t.matmul(ps[:, 0:512],
                                        self.wbuf[:, m * 1024 + k * 128:
                                                   m * 1024 + (k + 1) * 128],
                                        ofm[:, k * 512:(k + 1) * 512],
                                        start=(k == 0), stop=(k == DC - 1))
                    return last
                tmm = self.run("tensor", dep, mmf)
                ev[m] = self.run("scalar", [tmm], lambda s, m=m, ps=ps: s.copy(
                    x2fm[:, m * 512:(m + 1) * 512], ps[:, 0:512]))
            prev_va = None
            for i in range(4):
                def tf(t, i=i):
                    last = None
                    for c in range(4):
                        last = t.transpose(self.ps_c[:, c * 128:(c + 1) * 128],
                                           x2fm[:, c * 512 + i * 128:c * 512 + (i + 1) * 128],
                                           self.ident[:])
                    for c in range(4, 8):
                        last = t.transpose(self.ps_d[:, (c - 4) * 128:(c - 3) * 128],
                                           x2fm[:, c * 512 + i * 128:c * 512 + (i + 1) * 128],
                                           self.ident[:])
                    return last
                ttr2 = self.run("tensor", [ev[DC - 1], ev[DC - 2]] +
                                ([prev_va] if prev_va else []), tf)

                def va(v, i=i):
                    v.tensor_tensor(stage_o[:, i * D:i * D + 512], self.ps_c[:],
                                    stage_h[:, i * D:i * D + 512], ALU.add)
                    return v.tensor_tensor(stage_o[:, i * D + 512:(i + 1) * D],
                                           self.ps_d[:],
                                           stage_h[:, i * D + 512:(i + 1) * D], ALU.add)
                tva = self.run("vector", [ttr2, tldh], va)
                prev_va = tva
                sl_toks.append(self.dma("x2sp", self.s_x2[r0 + i * 128:r0 + (i + 1) * 128, :],
                                        stage_o[:, i * D:(i + 1) * D], [tva],
                                        q="gpsimd"))
            t_sl_prev = self.join(sl_toks[-4:])
        t_x2 = self.join(sl_toks)
        self.dump_dram("d_x2", self.s_x2, (row0, row0 + T), [t_x2])
        return t_x2

    def phaseC(self, b, t_prev):
        A = self.arena
        A.reset(0)
        stage_x2 = A.alloc(4 * D)
        buf_a = A.alloc(4 * D)
        buf_fm = A.alloc(4096)
        hid = A.alloc(16 * 512)
        gbuf = A.alloc(512)
        row0 = b * T
        eps_rms = self.smalls[:, 31:32]
        t0 = self.run("vector", [t_prev], lambda v: v.memset(eps_rms, RMS_EPS))

        t_sl = t0
        for sl in range(4):
            r0 = row0 + sl * 512
            tldx = self.dma_many("x2ld", [
                (stage_x2[:, i * D:(i + 1) * D],
                 self.s_x2[r0 + i * 128:r0 + (i + 1) * 128, :]) for i in range(4)],
                deps=[t_sl], q="sync")
            ttoks = []
            for i in range(4):
                xt = stage_x2[:, i * D:(i + 1) * D]
                bt = buf_a[:, i * D:(i + 1) * D]
                tln = self.ln_tile(xt, bt, bt, eps_rms, [tldx], rms=True,
                                   gamma=self.gam["ln2_1"][:])
                ttr = self.tp_tile_to_fm(
                    bt, lambda c, i=i: buf_fm[:, c * 512 + i * 128:c * 512 + (i + 1) * 128],
                    [tln] + ([ttoks[-1]] if ttoks else []))
                ttoks.append(ttr)
            t_b2 = self.join(ttoks)

            t_wd_ev = {}
            prev_ev = {}
            t_hid = []
            for half in range(2):
                hid_toks = []
                for fi in range(16):
                    f = half * 16 + fi
                    twg = self.dma("wgld", self.wbuf[:, 0:1024],
                                   self.w["wg1"][:, f * 1024:(f + 1) * 1024],
                                   [prev_ev.get("g", t_b2)], q="sync")
                    twu = self.dma("wuld", self.wbuf[:, 1024:2048],
                                   self.w["wu1"][:, f * 1024:(f + 1) * 1024],
                                   [prev_ev.get("u", t_b2)], q="sync")

                    def mmg(t):
                        last = None
                        for k in range(DC):
                            last = t.matmul(self.ps_a[:, 0:512],
                                            self.wbuf[:, k * 128:(k + 1) * 128],
                                            buf_fm[:, k * 512:(k + 1) * 512],
                                            start=(k == 0), stop=(k == DC - 1))
                        return last
                    tg = self.run("tensor", [twg, t_b2] +
                                  ([prev_ev["gv"]] if "gv" in prev_ev else []), mmg)
                    prev_ev["g"] = tg
                    tsil = self.run("scalar", [tg], lambda s: s.activation(
                        gbuf, self.ps_a[:, 0:512], AF.Silu))

                    def mmu(t):
                        last = None
                        for k in range(DC):
                            last = t.matmul(self.ps_b[:, 0:512],
                                            self.wbuf[:, 1024 + k * 128:1024 + (k + 1) * 128],
                                            buf_fm[:, k * 512:(k + 1) * 512],
                                            start=(k == 0), stop=(k == DC - 1))
                        return last
                    tu = self.run("tensor", [twu, tg] +
                                  ([prev_ev["uv"]] if "uv" in prev_ev else []), mmu)
                    prev_ev["u"] = tu
                    thid = self.run("vector", [tu, tsil], lambda v, fi=fi:
                                    v.tensor_tensor(hid[:, fi * 512:(fi + 1) * 512],
                                                    gbuf, self.ps_b[:, 0:512], ALU.mult))
                    prev_ev["gv"] = thid
                    prev_ev["uv"] = thid
                    hid_toks.append(thid)
                t_hid_half = self.join(hid_toks)
                twd_prev = None
                for m in range(DC):
                    if m >= 2:
                        wdep = [t_wd_ev[(m - 2, half)]]
                    elif half == 1:
                        wdep = [t_wd_ev[(m + 6, 0)]]
                    else:
                        wdep = [t_hid_half]
                    twd = self.dma("wdld%d" % (m % 2),
                                   self.wbuf[:, 2048 + (m % 2) * 2048:
                                             2048 + (m % 2) * 2048 + 2048],
                                   self.w["wd1"][:, m * 4096 + half * 2048:
                                                 m * 4096 + half * 2048 + 2048],
                                   wdep, q="sync")

                    psd = self.ps_c if m % 2 == 0 else self.ps_d

                    def mmd(t, m=m, psd=psd):
                        last = None
                        for kk in range(16):
                            last = t.matmul(
                                psd[:, 0:512],
                                self.wbuf[:, 2048 + (m % 2) * 2048 + kk * 128:
                                           2048 + (m % 2) * 2048 + (kk + 1) * 128],
                                hid[:, kk * 512:(kk + 1) * 512],
                                start=(kk == 0), stop=(kk == 15))
                        return last
                    dep = [twd, t_hid_half]
                    if (m, half) != (0, 0) and twd_prev:
                        dep.append(twd_prev)
                    if half == 1:
                        dep.append(t_wd_ev[m])
                    tmmd = self.run("tensor", dep +
                                    ([t_wd_ev.get((m - 2, half))] if (m - 2, half)
                                     in t_wd_ev else []), mmd)
                    twd_prev = tmmd
                    if half == 0:
                        tev = self.run("scalar", [tmmd], lambda s, m=m, psd=psd:
                                       s.copy(buf_a[:, m * 512:(m + 1) * 512],
                                              psd[:, 0:512]))
                        t_wd_ev[m] = tev
                    else:
                        tev = self.run("vector", [tmmd], lambda v, m=m, psd=psd:
                                       v.tensor_tensor(buf_a[:, m * 512:(m + 1) * 512],
                                                       buf_a[:, m * 512:(m + 1) * 512],
                                                       psd[:, 0:512], ALU.add))
                    t_wd_ev[(m, half)] = tev
                t_hid.append(tev)
            t_ffn = self.join([t_wd_ev[(m, 1)] for m in range(DC)])

            prev_va = None
            sp_toks = []
            for i in range(4):
                def tf(t, i=i):
                    last = None
                    for c in range(4):
                        last = t.transpose(self.ps_c[:, c * 128:(c + 1) * 128],
                                           buf_a[:, c * 512 + i * 128:c * 512 + (i + 1) * 128],
                                           self.ident[:])
                    for c in range(4, 8):
                        last = t.transpose(self.ps_d[:, (c - 4) * 128:(c - 3) * 128],
                                           buf_a[:, c * 512 + i * 128:c * 512 + (i + 1) * 128],
                                           self.ident[:])
                    return last
                ttr = self.run("tensor", [t_ffn] + ([prev_va] if prev_va else []), tf)

                def va(v, i=i):
                    v.tensor_tensor(hid[:, i * D:i * D + 512], self.ps_c[:],
                                    stage_x2[:, i * D:i * D + 512], ALU.add)
                    return v.tensor_tensor(hid[:, i * D + 512:(i + 1) * D], self.ps_d[:],
                                           stage_x2[:, i * D + 512:(i + 1) * D], ALU.add)
                tva = self.run("vector", [ttr], va)
                prev_va = tva
                sp_toks.append(self.dma("prsp",
                                        self.s_proc[r0 + i * 128:r0 + (i + 1) * 128, :],
                                        hid[:, i * D:(i + 1) * D], [tva], q="gpsimd"))
            t_sl = self.join(sp_toks)
        t_c1 = t_sl
        self.dump_dram("d_proc", self.s_proc, (row0, row0 + T), [t_c1])

        stage_h = stage_x2
        resb = buf_a
        phb = hid[:, 0:4096]
        mtb = hid[:, 4096:8192]
        neghalf = self.consts[:, 0:1]
        t_sl = t_c1
        for sl in range(4):
            r0 = row0 + sl * 512
            tldh = self.dma_many("hld3", [
                (stage_h[:, i * D:(i + 1) * D],
                 self.s_h[r0 + i * 128:r0 + (i + 1) * 128, :]) for i in range(4)],
                deps=[t_sl], q="sync")
            tldp = self.dma_many("prld", [
                (resb[:, i * D:(i + 1) * D],
                 self.s_proc[r0 + i * 128:r0 + (i + 1) * 128, :]) for i in range(4)],
                deps=[t_sl], q="sync")
            t_res = []
            for i in range(4):
                def vres(v, i=i):
                    v.tensor_tensor(resb[:, i * D:(i + 1) * D],
                                    resb[:, i * D:(i + 1) * D],
                                    stage_h[:, i * D:(i + 1) * D], ALU.subtract)
                    v.drain()
                    v.tensor_tensor(buf_fm[:, 0:1024], resb[:, i * D:(i + 1) * D],
                                    resb[:, i * D:(i + 1) * D], ALU.mult)
                    v.drain()
                    return v.tensor_reduce(self.stat[:, 16 + i:17 + i],
                                           buf_fm[:, 0:1024], axis=AX.X, op=ALU.add)
                t_res.append(self.run("vector", [tldh, tldp], vres))
            ttoks = []
            for i in range(4):
                ht = stage_h[:, i * D:(i + 1) * D]
                pt = phb[:, i * D:(i + 1) * D]
                tln = self.ln_tile(ht, pt, pt, eps_rms, [tldh, t_res[i]], rms=True,
                                   gamma=self.gam["prior_nw"][:], stat_base=8)
                ttr = self.tp_tile_to_fm(
                    pt, lambda c, i=i: buf_fm[:, c * 512 + i * 128:c * 512 + (i + 1) * 128],
                    [tln] + ([ttoks[-1]] if ttoks else []) + [t_res[3]])
                ttoks.append(ttr)
            t_ph = self.join(ttoks)
            prev_part_last = [t_ph]
            for part in range(2):
                ev = {}
                tw_prev = None
                for mi in range(DC):
                    m = part * 8 + mi
                    tw = self.dma("prwld%d" % (mi % 2),
                                  self.wbuf[:, (mi % 2) * 1024:(mi % 2) * 1024 + 1024],
                                  self.w["prior_w"][:, m * 1024:(m + 1) * 1024],
                                  [ev[mi - 2]] if mi >= 2 else prev_part_last,
                                  q="sync")
                    ps = self.ps_a if mi % 2 == 0 else self.ps_b

                    def mmp(t, mi=mi, ps=ps):
                        last = None
                        for k in range(DC):
                            last = t.matmul(ps[:, 0:512],
                                            self.wbuf[:, (mi % 2) * 1024 + k * 128:
                                                       (mi % 2) * 1024 + (k + 1) * 128],
                                            buf_fm[:, k * 512:(k + 1) * 512],
                                            start=(k == 0), stop=(k == DC - 1))
                        return last
                    dep = [tw, t_ph]
                    if tw_prev:
                        dep.append(tw_prev)
                    tmm = self.run("tensor", dep, mmp)
                    tw_prev = tmm
                    tev = self.run("scalar", [tmm], lambda s, ps=ps: s.copy(
                        gbuf, ps[:, 0:512]))
                    psd = self.ps_c if mi % 2 == 0 else self.ps_d

                    def ttf(t, psd=psd):
                        last = None
                        for i in range(4):
                            last = t.transpose(psd[:, i * 128:(i + 1) * 128],
                                               gbuf[:, i * 128:(i + 1) * 128],
                                               self.ident[:])
                        return last
                    dep2 = [tev]
                    if mi >= 2 and (mi - 2, part) in ev:
                        pass
                    ttp = self.run("tensor", dep2, ttf)
                    tev2 = self.run("vector", [ttp], lambda v, mi=mi, psd=psd:
                                    v.tensor_copy(
                                        _mtb_view(mtb, mi), psd[:, 0:512]))
                    ev[mi] = tev2
                t_part = self.join([ev[mi] for mi in range(DC)])
                prev_part_last = [t_part]
                if part == 0:
                    t_mu = []
                    for i in range(4):
                        def vmu(v, i=i):
                            v.tensor_tensor(resb[:, i * D:(i + 1) * D],
                                            resb[:, i * D:(i + 1) * D],
                                            mtb[:, i * D:(i + 1) * D], ALU.subtract)
                            v.drain()
                            return v.tensor_tensor(resb[:, i * D:(i + 1) * D],
                                                   resb[:, i * D:(i + 1) * D],
                                                   resb[:, i * D:(i + 1) * D], ALU.mult)
                        t_mu.append(self.run("vector", [t_part], vmu))
                    t_mu = self.join(t_mu)
                else:
                    t_dch = []
                    for i in range(4):
                        lv = mtb[:, i * D:(i + 1) * D]
                        t2 = resb[:, i * D:(i + 1) * D]
                        ei = phb[:, i * D:(i + 1) * D]
                        te = self.run("scalar", [t_part, t_mu], lambda s, lv=lv, ei=ei:
                                      s.activation(ei, lv, AF.Exp, scale=-1.0))

                        def vw(v, t2=t2, ei=ei, lv=lv):
                            v.scalar_tensor_tensor(t2, t2, 1.0, ei,
                                                   op0=ALU.add, op1=ALU.mult)
                            v.drain()
                            return v.tensor_tensor(t2, t2, lv, ALU.add)
                        tv = self.run("vector", [te], vw)
                        td = self.run("scalar", [tv], lambda s, t2=t2, i=i:
                                      s.activation(phb[:, i * D:(i + 1) * D], t2, AF.Copy,
                                                   scale=0.5,
                                                   accum_out=self.stat[:, 20 + i:21 + i]))
                        t_dch.append(td)
                    t_dch = self.join(t_dch)
            def vs(v):
                v.drain()
                for i in range(4):
                    qt = sl * 4 + i
                    v.tensor_tensor(self.s_all[:, b * TQ + qt:b * TQ + qt + 1],
                                    self.stat[:, 20 + i:21 + i],
                                    self.stat[:, 16 + i:17 + i], ALU.add)
                v.drain()
                last = None
                for i in range(4):
                    qt = sl * 4 + i
                    last = v.tensor_scalar(
                        self.s_all[:, b * TQ + qt:b * TQ + qt + 1],
                        self.s_all[:, b * TQ + qt:b * TQ + qt + 1],
                        1.0 / D, -0.5, op0=ALU.mult, op1=ALU.add)
                return last
            t_s = self.run("vector", [t_dch], vs)
            t_sl = self.join([t_s, t_dch])
        if "d_s" in self.dbg and b == NB - 1:
            self.dma("dbg", self.dbg["d_s"], self.s_all[:], [t_sl], q="gpsimd")
        return t_sl

    def phase_shift(self, t_prev):
        ones_col = self.smalls[:, 32:33]
        tv = self.run("vector", [t_prev], lambda v: v.memset(ones_col, 1.0))
        tmm = self.run("tensor", [tv, t_prev], lambda t: t.matmul(
            self.ps_a[0:1, 0:NB * TQ], ones_col, self.s_all[:],
            start=True, stop=True))
        def vr(v):
            v.tensor_copy(self.pbuf[0:1, 0:NB * TQ], self.ps_a[0:1, 0:NB * TQ])
            v.drain()
            return v.tensor_reduce(self.smalls[0:1, 33:34],
                                   self.pbuf[0:1, 0:NB * TQ],
                                   axis=AX.X, op=ALU.add)
        tvr = self.run("vector", [tmm], vr)
        t1 = self.dma("ccio", self.cc_in_t.ap()[0:1, 0:1],
                      self.smalls[0:1, 33:34], [tvr], q="gpsimd")

        def gcc(g):
            return g.collective_compute(
                "AllReduce", ALU.add,
                ins=[self.cc_in_t.ap().opt()], outs=[self.cc_out_t.ap().opt()],
                replica_groups=[list(range(8))])
        tcc = self.run("gpsimd", [t1], gcc)
        t2 = self.dma("ccio", self.smalls[0:1, 35:36], self.cc_out_t.ap()[0:1, 0:1],
                      [tcc], q="gpsimd")
        tv2 = self.run("vector", [t2], lambda v: v.tensor_scalar(
            self.smalls[0:1, 36:37], self.smalls[0:1, 35:36],
            -1.0 / (16.0 * T), None, op0=ALU.mult))
        t3 = self.dma("ccio", self.cc_in_t.ap()[0:1, 1:2], self.smalls[0:1, 36:37],
                      [tv2], q="gpsimd")
        t4 = self.dma("ccio", self.negshift[:],
                      self.cc_in_t.ap()[0:1, 1:2].to_broadcast([128, 1]), [t3],
                      q="gpsimd")
        return t4, tvr

    def phaseD(self, b, t_prev):
        A = self.arena
        A.reset(0)
        Pm = A.alloc(KQ * T)
        PT = A.alloc(TQ * 512)
        bc_s = A.alloc(T)
        bc_x = A.alloc(T)
        bc_r = A.alloc(T)
        ones128 = A.alloc(128)
        stg = A.alloc(1152)
        row0 = b * T
        st = self.stat
        sS = self.smalls[:, 40:41]
        rS = self.smalls[:, 41:42]
        selS = self.smalls[:, 42:43]

        t0 = self.run("vector", [t_prev], lambda v: v.memset(ones128, 1.0))

        tt = self.run("tensor", [t_prev], lambda t: t.transpose(
            self.ps_a[0:TQ, 0:128], self.s_all[:, b * TQ:(b + 1) * TQ],
            self.ident[:]))
        te = self.run("vector", [tt], lambda v: v.tensor_copy(
            self.pbuf[0:TQ, 0:128], self.ps_a[0:TQ, 0:128]))
        prs = [(self.s_bc[0:1, qt * 128:(qt + 1) * 128],
                self.pbuf[qt:qt + 1, 0:128]) for qt in range(TQ)]
        t1 = self.dma_many("dbc", prs, [te], q="gpsimd")
        t2 = self.dma("dbc", bc_s, self.s_bc[0:1, 0:T].to_broadcast([128, T]),
                      [t1], q="gpsimd")

        rank_toks = []
        for qt in range(TQ):
            scol = self.s_all[:, b * TQ + qt:b * TQ + qt + 1]
            tsc = self.run("scalar", [t2] + ([rank_toks[-1]] if rank_toks else []),
                           lambda s, scol=scol: s.copy(sS, scol))
            def vgt(v):
                v.tensor_scalar(bc_x, bc_s, sS, 0.0, op0=ALU.is_gt,
                                op1=ALU.add, accum_out=st[:, 0:1])
                v.drain()
                return v.tensor_scalar(bc_x, bc_s, sS, None, op0=ALU.is_equal)
            tv1 = self.run("vector", [tsc], vgt)
            tg1 = self.run("gpsimd", [tv1], lambda g, qt=qt: g.affine_select(
                bc_x, bc_x, pattern=[[-1, T]], compare_op=ALU.is_ge,
                fill=0.0, base=qt * 128 - 1, channel_multiplier=1))

            def veq(v, qt=qt):
                v.tensor_reduce(st[:, 1:2], bc_x, axis=AX.X, op=ALU.add)
                v.drain()
                v.tensor_tensor(self.rank_all[:, b * TQ + qt:b * TQ + qt + 1],
                                st[:, 0:1], st[:, 1:2], ALU.add)
                v.drain()
                return v.tensor_scalar(
                    self.sel_all[:, b * TQ + qt:b * TQ + qt + 1],
                    self.rank_all[:, b * TQ + qt:b * TQ + qt + 1],
                    float(KSEL), None, op0=ALU.is_lt)
            rank_toks.append(self.run("vector", [tg1], veq))
        t_rank = self.join(rank_toks)

        def tmr(t):
            last = None
            for iq in range(TQ):
                for jq in range(iq + 1):
                    lhs = self.tri_up[:] if jq == iq else ones128
                    last = t.matmul(self.ps_b[:, iq:iq + 1], lhs,
                                    self.sel_all[:, b * TQ + jq:b * TQ + jq + 1],
                                    start=(jq == 0), stop=(jq == iq))
            return last
        tr = self.run("tensor", [t_rank, t0], tmr)
        tre = self.run("vector", [tr], lambda v: v.tensor_copy(
            self.r_all[:, b * TQ:(b + 1) * TQ], self.ps_b[:, 0:TQ]))

        def tt2(t):
            t.transpose(self.ps_a[0:TQ, 0:128], self.sel_all[:, b * TQ:(b + 1) * TQ],
                        self.ident[:])
            return t.transpose(self.ps_a[0:TQ, 128:256],
                               self.r_all[:, b * TQ:(b + 1) * TQ], self.ident[:])
        tt2k = self.run("tensor", [tre], tt2)
        te2 = self.run("vector", [tt2k], lambda v: v.tensor_copy(
            self.pbuf[0:TQ, 0:256], self.ps_a[0:TQ, 0:256]))
        prs = []
        for qt in range(TQ):
            prs.append((self.s_bc[0:1, qt * 128:(qt + 1) * 128],
                        self.pbuf[qt:qt + 1, 0:128]))
            prs.append((self.s_bc2[0:1, qt * 128:(qt + 1) * 128],
                        self.pbuf[qt:qt + 1, 128:256]))
        t3 = self.dma_many("dbc", prs, [te2], q="gpsimd")
        bc_sel = bc_s
        t4a = self.dma("dbc", bc_sel, self.s_bc[0:1, 0:T].to_broadcast([128, T]),
                       [t3], q="gpsimd")
        t4b = self.dma("dbc", bc_r, self.s_bc2[0:1, 0:T].to_broadcast([128, T]),
                       [t3], q="gpsimd")

        pt_toks = []
        for iq in range(TQ):
            tsc = self.run("scalar", [tre, t_rank] +
                           ([pt_toks[-1]] if pt_toks else []),
                           lambda s, iq=iq: (
                               s.copy(rS, self.r_all[:, b * TQ + iq:b * TQ + iq + 1]),
                               s.copy(selS, self.sel_all[:, b * TQ + iq:b * TQ + iq + 1])
                           )[-1])
            pt_toks.append(self.run("vector", [tsc], lambda v, iq=iq: v.tensor_scalar(
                PT[:, iq * 512:(iq + 1) * 512], self.iota512[:], rS, selS,
                op0=ALU.is_equal, op1=ALU.mult)))
        t_PT = self.join(pt_toks)
        pm_toks = []
        for jq in range(KQ):
            jcol = self.smalls[:, 50 + jq:51 + jq]
            tg = self.run("gpsimd", [t4a, t4b], lambda g, jq=jq, jcol=jcol:
                          g.tensor_scalar(jcol, self.smalls[:, 8:9], float(jq * 128),
                                          None, op0=ALU.add))
            def vpm(v, jq=jq, jcol=jcol):
                v.tensor_scalar(Pm[:, jq * T:(jq + 1) * T], bc_r, jcol, None,
                                op0=ALU.is_equal)
                v.drain()
                return v.tensor_tensor(Pm[:, jq * T:(jq + 1) * T],
                                       Pm[:, jq * T:(jq + 1) * T], bc_sel, ALU.mult)
            pm_toks.append(self.run("vector", [tg, t4a, t4b], vpm))
        t_Pm = self.join(pm_toks)

        t_prev_g = None
        for jq in range(KQ):
            tldp = {}
            mm_toks = []
            for iq in range(TQ):
                pb = bc_x[:, 0:1024] if iq % 2 == 0 else bc_r[:, 0:1024]
                tldp[iq] = self.dma_many(
                    "gpr%d" % (iq % 2),
                    [(pb, self.s_proc[row0 + iq * 128:row0 + (iq + 1) * 128, :])],
                    [t_PT, t_Pm] + ([mm_toks[-2]] if len(mm_toks) >= 2 else []),
                    q="sync")

                def gmm(t, iq=iq, jq=jq, pb=pb, first=(iq == 0), last=(iq == TQ - 1)):
                    lhs = PT[:, iq * 512 + jq * 128:iq * 512 + (jq + 1) * 128]
                    t.matmul(self.ps_s[:, 0:512], lhs, pb[:, 0:512],
                             start=first, stop=last)
                    t.matmul(self.ps_s[:, 512:1024], lhs, pb[:, 512:1024],
                             start=first, stop=last)
                    t.matmul(self.ps_s[:, 1024:1056], lhs,
                             self.cos_tm[:, iq * 32:(iq + 1) * 32],
                             start=first, stop=last)
                    t.matmul(self.ps_s[:, 1536:1568], lhs,
                             self.sin_tm[:, iq * 32:(iq + 1) * 32],
                             start=first, stop=last)
                    return t.matmul(self.ps_a[:, 0:1], lhs,
                                    self.s_all[:, b * TQ + iq:b * TQ + iq + 1],
                                    start=first, stop=last)
                dep = [tldp[iq], t_PT]
                if t_prev_g and iq == 0:
                    dep.append(t_prev_g)
                mm_toks.append(self.run("tensor", dep, gmm))
            def vev(v):
                v.tensor_copy(stg[:, 0:1056], self.ps_s[:, 0:1056])
                v.tensor_copy(stg[:, 1056:1088], self.ps_s[:, 1536:1568])
                return v.tensor_copy(stg[:, 1088:1089], self.ps_a[:, 0:1])
            tev = self.run("vector", [mm_toks[-1]], vev)
            rr = b * KSEL + jq * 128
            tsp = self.dma_many("gsp", [
                (self.s_selg[rr:rr + 128, :], stg[:, 0:1024]),
                (self.s_ropeg[rr:rr + 128, 0:65], stg[:, 1024:1089])],
                [tev], q="gpsimd")
            t_prev_g = self.join([tev, tsp])
        if "d_sel" in self.dbg and b == NB - 1:
            self.dma("dbg", self.dbg["d_sel"], self.sel_all[:], [t_rank], q="gpsimd")
        if "d_r" in self.dbg and b == NB - 1:
            self.dma("dbg", self.dbg["d_r"], self.r_all[:], [tre], q="gpsimd")
        if "d_selg" in self.dbg:
            self.dma_dbg_rows("d_selg", self.s_selg, b * KSEL, KSEL, [t_prev_g])
        if "d_rope" in self.dbg:
            self.dma("dbg", self.dbg["d_rope"][b * KSEL:(b + 1) * KSEL, :],
                     self.s_ropeg[b * KSEL:(b + 1) * KSEL, 0:64],
                     [t_prev_g], q="gpsimd")
        return t_prev_g

    def dma_dbg_rows(self, name, src, r0, n, deps):
        return self.dma("dbg", self.dbg[name][r0:r0 + n, :], src[r0:r0 + n, :],
                        deps, q="gpsimd")

    def phaseE(self, b, t_prev, t_shift):
        A = self.arena
        A.reset(KQ * T)
        K2s = A.alloc(4096)
        Q2s = A.alloc(4096)
        V2s = A.alloc(4096)
        O2s = A.alloc(4096)
        Pm = self.arena.t[:, 0:KQ * T]
        cos2P = O2s[:, 0:512]
        sgn2P = O2s[:, 512:1024]
        row0 = b * T
        rsel = b * KSEL
        eps_rms = self.smalls[:, 31:32]
        gbuf = self.pbuf[:, 1536:2048]

        trg = self.dma_many("rgld", [
            (K2s[:, i * 64:(i + 1) * 64],
             self.s_ropeg[rsel + i * 128:rsel + (i + 1) * 128, 0:64])
            for i in range(KQ)], [t_prev], q="sync")

        def t_rt(t):
            last = None
            for i in range(KQ):
                t.transpose(self.ps_c[0:32, i * 128:(i + 1) * 128],
                            K2s[:, i * 64:i * 64 + 32], self.ident[:])
                last = t.transpose(self.ps_d[0:32, i * 128:(i + 1) * 128],
                                   K2s[:, i * 64 + 32:i * 64 + 64], self.ident[:])
            return last
        ttr = self.run("tensor", [trg], t_rt)

        def vrt(v):
            v.tensor_copy(cos2P[0:32, :], self.ps_c[0:32, 0:512])
            return v.tensor_copy(sgn2P[0:32, :], self.ps_d[0:32, 0:512])
        tvrt = self.run("vector", [ttr], vrt)
        trep = self.dma_many("rrep", [
            (cos2P[g:g + 32, :], cos2P[0:32, :]) for g in (32, 64, 96)] + [
            (sgn2P[g:g + 32, :], sgn2P[0:32, :]) for g in (32, 64, 96)],
            [tvrt], q="gpsimd")
        tsg = self.run("scalar", [trep], lambda s: (
            s.mul(sgn2P[0:32, :], sgn2P[0:32, :], -1.0),
            s.mul(sgn2P[64:96, :], sgn2P[64:96, :], -1.0))[-1])

        tsel = self.dma_many("seld", [
            (K2s[:, i * D:(i + 1) * D],
             self.s_selg[rsel + i * 128:rsel + (i + 1) * 128, :])
            for i in range(KQ)], [tvrt], q="sync")
        ttoks = []
        for i in range(KQ):
            xt = K2s[:, i * D:(i + 1) * D]
            at = Q2s[:, i * D:(i + 1) * D]
            tln = self.ln_tile(xt, at, at, eps_rms, [tsel], rms=True,
                               gamma=self.gam["ln1_2"][:])
            ttr2 = self.tp_tile_to_fm(
                at, lambda c, i=i: V2s[:, c * 512 + i * 128:c * 512 + (i + 1) * 128],
                [tln] + ([ttoks[-1]] if ttoks else []))
            ttoks.append(ttr2)
        t_a2 = self.join(ttoks)

        def proj8(wname, dst, deps, evac_eng="scalar"):
            tw = self.dma_many("wfull", [
                (self.wbuf[:, m * 1024:(m + 1) * 1024],
                 self.w[wname][:, m * 1024:(m + 1) * 1024]) for m in range(DC)],
                deps, q="sync")
            evs = []
            for m in range(DC):
                ps = self.ps_a if m % 2 == 0 else self.ps_b

                def mmf(t, m=m, ps=ps):
                    last = None
                    for k in range(DC):
                        last = t.matmul(ps[:, 0:512],
                                        self.wbuf[:, m * 1024 + k * 128:
                                                   m * 1024 + (k + 1) * 128],
                                        V2s[:, k * 512:(k + 1) * 512],
                                        start=(k == 0), stop=(k == DC - 1))
                    return last
                dep = [tw, t_a2] + ([evs[m - 2]] if m >= 2 else list(deps))
                tmm = self.run("tensor", dep, mmf)
                evs.append(self.run(evac_eng, [tmm], lambda s, m=m, ps=ps: s.copy(
                    dst[:, m * 512:(m + 1) * 512], ps[:, 0:512])))
            return self.join(evs), tmm

        def rope8(dst, deps):
            toks = list(deps)
            for m in range(DC):
                psw = self.ps_c if m % 2 == 0 else self.ps_d
                sl = dst[:, m * 512:(m + 1) * 512]
                tsw = self.run("tensor", [toks[-1]], lambda t, sl=sl, psw=psw:
                               t.matmul(psw[:, 0:512], self.perm[:], sl,
                                        start=True, stop=True))

                def vf(v, sl=sl, psw=psw):
                    v.tensor_tensor(gbuf, psw[:, 0:512], sgn2P, ALU.mult)
                    v.tensor_tensor(sl, sl, cos2P, ALU.mult)
                    v.drain()
                    return v.tensor_tensor(sl, sl, gbuf, ALU.add)
                toks.append(self.run("vector", [tsw], vf))
            return self.join(toks)

        t_k2, _ = proj8("wk2", K2s, [t_a2, tsel, tsg])
        t_k2 = rope8(K2s, [t_k2])
        t_q2, _ = proj8("wq2", Q2s, [t_k2])
        t_q2 = rope8(Q2s, [t_q2])
        vsp_toks = []
        tw = self.dma_many("wfull", [
            (self.wbuf[:, m * 1024:(m + 1) * 1024],
             self.w["wv2"][:, m * 1024:(m + 1) * 1024]) for m in range(DC)],
            [t_q2], q="sync")
        prevv = None
        for m in range(DC):
            ps = self.ps_a if m % 2 == 0 else self.ps_b

            def mmf(t, m=m, ps=ps):
                last = None
                for k in range(DC):
                    last = t.matmul(ps[:, 0:512],
                                    self.wbuf[:, m * 1024 + k * 128:
                                               m * 1024 + (k + 1) * 128],
                                    V2s[:, k * 512:(k + 1) * 512],
                                    start=(k == 0), stop=(k == DC - 1))
                return last
            dep = [tw, t_q2] + ([vsp_toks[-2][0]] if m >= 2 else [])
            tmm = self.run("tensor", dep, mmf)
            tev = self.run("scalar", [tmm], lambda s, ps=ps: s.copy(gbuf, ps[:, 0:512]))
            psw = self.ps_c if m % 2 == 0 else self.ps_d

            def tvt(t, psw=psw):
                last = None
                for i in range(4):
                    last = t.transpose(psw[:, i * 128:(i + 1) * 128],
                                       gbuf[:, i * 128:(i + 1) * 128], self.ident[:])
                return last
            dep2 = [tev] + ([vsp_toks[-2][1]] if m >= 2 else [])
            ttp = self.run("tensor", dep2, tvt)
            tvs = self.run("vector", [ttp], lambda v, psw=psw:
                           v.tensor_copy(self.ptb[:, 0:512], psw[:, 0:512]))
            pr = []
            for i in range(4):
                rr = rsel + i * 128
                pr.append((self.s_e[rr:rr + 128, m * 128:(m + 1) * 128],
                           self.ptb[:, i * 128:(i + 1) * 128]))
            tvsp = self.dma_many("vsp2", pr, [tvs], q="gpsimd")
            vsp_toks.append((tev, tvsp))
        t_v2sp = self.join([tt for _, tt in vsp_toks])
        t_v2 = self.dma_many("v2ld", [
            (V2s[:, i * D:(i + 1) * D],
             self.s_e[rsel + i * 128:rsel + (i + 1) * 128, :]) for i in range(KQ)],
            [t_v2sp, t_q2], q="sync")

        st = self.stat
        negm = st[:, 0:1]
        negm8 = st[:, 1:2]
        ell = st[:, 2:3]
        rell = lambda h: st[:, 8 + (h % 8):9 + (h % 8)]
        t_exp_prev = None
        t_pt_prev = None
        ev_prev = None
        o_ev = []
        for qt in range(KQ):
            klen = (qt + 1) * 128
            ev_toks = []
            for h in range(NH):
                hp, hh = h // 2, h % 2

                def smm(t, hp=hp, hh=hh, klen=klen, qt=qt):
                    return t.matmul(
                        self.ps_s[:, 0:klen],
                        Q2s[hh * 64:(hh + 1) * 64,
                            hp * 512 + qt * 128:hp * 512 + (qt + 1) * 128],
                        K2s[hh * 64:(hh + 1) * 64, hp * 512:hp * 512 + klen],
                        start=True, stop=True)
                dep_s = [t_k2, t_q2] + ([t_exp_prev] if t_exp_prev else [])
                tS = self.run("tensor", dep_s, smm)

                def vmx(v, qt=qt, klen=klen):
                    v.tensor_tensor(self.ps_s[:, qt * 128:klen],
                                    self.ps_s[:, qt * 128:klen],
                                    self.causal[:], ALU.add)
                    v.drain()
                    v.tensor_reduce(negm, self.ps_s[:, 0:klen], axis=AX.X,
                                    op=ALU.max, negate=True)
                    v.drain()
                    return v.tensor_scalar(negm8, negm, 0.125, None, op0=ALU.mult)
                tM = self.run("vector", [tS], vmx)
                tE = self.run("scalar", [tM], lambda s, klen=klen: s.activation(
                    self.pbuf[:, 0:klen], self.ps_s[:, 0:klen], AF.Exp,
                    scale=0.125, bias=negm8, accum_out=ell))
                t_exp_prev = tE
                tR = self.run("vector", [tE], lambda v, h=h: v.reciprocal(rell(h), ell))
                nt = qt + 1

                def tptf(t, nt=nt):
                    last = None
                    for i in range(nt):
                        last = t.transpose(self.ps_c[:, i * 128:(i + 1) * 128],
                                           self.pbuf[:, i * 128:(i + 1) * 128],
                                           self.ident[:])
                    return last
                tT = self.run("tensor", [tE] + ([t_pt_prev] if t_pt_prev else []), tptf)
                tEv = self.run("vector", [tT], lambda v, nt=nt: v.tensor_copy(
                    self.ptb[:, 0:nt * 128], self.ps_c[:, 0:nt * 128]))
                t_pt_prev = tEv

                def tavf(t, nt=nt, hh=hh, h=h, hp=hp):
                    lasti = None
                    for i in range(nt):
                        lasti = t.matmul(
                            self.ps_b[:, (h % 8) * 64:(h % 8) * 64 + 64],
                            self.ptb[:, i * 128:(i + 1) * 128],
                            V2s[:, i * 1024 + hp * 128 + hh * 64:
                                i * 1024 + hp * 128 + hh * 64 + 64],
                            start=(i == 0), stop=(i == nt - 1))
                    return lasti
                depav = [tEv, t_v2, tR]
                if h % 8 == 0 and ev_toks:
                    depav.append(ev_toks[-1])
                if h % 8 == 0 and not ev_toks and o_ev:
                    depav.append(o_ev[-1])
                tAV = self.run("tensor", depav, tavf)
                if h % 8 == 7:
                    def sev(s, h=h, qt=qt):
                        last = None
                        for hs in range(8):
                            hq = (h // 8) * 8 + hs
                            last = s.activation(
                                O2s[:, qt * 1024 + hq * 64:qt * 1024 + (hq + 1) * 64],
                                self.ps_b[:, hs * 64:(hs + 1) * 64],
                                AF.Copy, scale=rell(hq))
                        return last
                    dep_ev = [tAV]
                    if qt == 0 and h == 7:
                        dep_ev.append(t_v2)
                    ev_toks.append(self.run("scalar", dep_ev, sev))
            o_ev.append(ev_toks[-1])
        t_o2 = self.join(o_ev)

        ofm = K2s
        ttoks = []
        for i in range(KQ):
            ttr3 = self.tp_tile_to_fm(
                O2s[:, i * D:(i + 1) * D],
                lambda c, i=i: ofm[:, c * 512 + i * 128:c * 512 + (i + 1) * 128],
                [t_o2] + ([ttoks[-1]] if ttoks else []))
            ttoks.append(ttr3)
        t_ofm = self.join(ttoks)
        tw = self.dma_many("wfull", [
            (self.wbuf[:, m * 1024:(m + 1) * 1024],
             self.w["wo2"][:, m * 1024:(m + 1) * 1024]) for m in range(DC)],
            [t_o2], q="sync")
        evs = []
        for m in range(DC):
            ps = self.ps_a if m % 2 == 0 else self.ps_b

            def mmf(t, m=m, ps=ps):
                last = None
                for k in range(DC):
                    last = t.matmul(ps[:, 0:512],
                                    self.wbuf[:, m * 1024 + k * 128:
                                               m * 1024 + (k + 1) * 128],
                                    ofm[:, k * 512:(k + 1) * 512],
                                    start=(k == 0), stop=(k == DC - 1))
                return last
            dep = [tw, t_ofm] + ([evs[m - 2]] if m >= 2 else [])
            tmm = self.run("tensor", dep, mmf)
            evs.append(self.run("scalar", [tmm], lambda s, m=m, ps=ps: s.copy(
                Q2s[:, m * 512:(m + 1) * 512], ps[:, 0:512])))
        t_x2fm = self.join(evs)
        t_selld = self.dma_many("seld2", [
            (V2s[:, i * D:(i + 1) * D],
             self.s_selg[rsel + i * 128:rsel + (i + 1) * 128, :]) for i in range(KQ)],
            [t_x2fm], q="sync")
        x2sp = []
        prev_va = None
        for i in range(KQ):
            def tf(t, i=i):
                last = None
                for c in range(4):
                    last = t.transpose(self.ps_c[:, c * 128:(c + 1) * 128],
                                       Q2s[:, c * 512 + i * 128:c * 512 + (i + 1) * 128],
                                       self.ident[:])
                for c in range(4, 8):
                    last = t.transpose(self.ps_d[:, (c - 4) * 128:(c - 3) * 128],
                                       Q2s[:, c * 512 + i * 128:c * 512 + (i + 1) * 128],
                                       self.ident[:])
                return last
            ttr4 = self.run("tensor", [t_x2fm] + ([prev_va] if prev_va else []), tf)

            def va(v, i=i):
                v.tensor_tensor(O2s[:, i * D:i * D + 512], self.ps_c[:],
                                V2s[:, i * D:i * D + 512], ALU.add)
                return v.tensor_tensor(O2s[:, i * D + 512:(i + 1) * D], self.ps_d[:],
                                       V2s[:, i * D + 512:(i + 1) * D], ALU.add)
            tva = self.run("vector", [ttr4, t_selld], va)
            prev_va = tva
            x2sp.append(self.dma("x2e", self.s_e[rsel + i * 128:rsel + (i + 1) * 128, :],
                                 O2s[:, i * D:(i + 1) * D], [tva], q="gpsimd"))
        t_x2p = self.join(x2sp + [prev_va])

        ttoks = []
        for i in range(KQ):
            xt = O2s[:, i * D:(i + 1) * D]
            bt = V2s[:, i * D:(i + 1) * D]
            tln = self.ln_tile(xt, bt, bt, eps_rms, [t_x2p], rms=True,
                               gamma=self.gam["ln2_2"][:], stat_base=16)
            ttr5 = self.tp_tile_to_fm(
                bt, lambda c, i=i: K2s[:, c * 512 + i * 128:c * 512 + (i + 1) * 128],
                [tln] + ([ttoks[-1]] if ttoks else []))
            ttoks.append(ttr5)
        t_b2p = self.join(ttoks)

        hid = self.arena.t[:, KQ * T + 2 * 4096:KQ * T + 4 * 4096]
        t_wd_ev = {}
        prev_ev = {}
        for half in range(2):
            hid_toks = []
            for fi in range(16):
                f = half * 16 + fi
                twg = self.dma("wgld", self.wbuf[:, 0:1024],
                               self.w["wg2"][:, f * 1024:(f + 1) * 1024],
                               [prev_ev.get("g", t_b2p)], q="sync")
                twu = self.dma("wuld", self.wbuf[:, 1024:2048],
                               self.w["wu2"][:, f * 1024:(f + 1) * 1024],
                               [prev_ev.get("u", t_b2p)], q="sync")

                def mmg(t):
                    last = None
                    for k in range(DC):
                        last = t.matmul(self.ps_a[:, 0:512],
                                        self.wbuf[:, k * 128:(k + 1) * 128],
                                        K2s[:, k * 512:(k + 1) * 512],
                                        start=(k == 0), stop=(k == DC - 1))
                    return last
                tg = self.run("tensor", [twg, t_b2p] +
                              ([prev_ev["gv"]] if "gv" in prev_ev else []), mmg)
                prev_ev["g"] = tg
                tsil = self.run("scalar", [tg], lambda s: s.activation(
                    gbuf, self.ps_a[:, 0:512], AF.Silu))

                def mmu(t):
                    last = None
                    for k in range(DC):
                        last = t.matmul(self.ps_b[:, 0:512],
                                        self.wbuf[:, 1024 + k * 128:1024 + (k + 1) * 128],
                                        K2s[:, k * 512:(k + 1) * 512],
                                        start=(k == 0), stop=(k == DC - 1))
                    return last
                tu = self.run("tensor", [twu, tg] +
                              ([prev_ev["uv"]] if "uv" in prev_ev else []), mmu)
                prev_ev["u"] = tu
                thid = self.run("vector", [tu, tsil], lambda v, fi=fi:
                                v.tensor_tensor(hid[:, fi * 512:(fi + 1) * 512],
                                                gbuf, self.ps_b[:, 0:512], ALU.mult))
                prev_ev["gv"] = thid
                prev_ev["uv"] = thid
                hid_toks.append(thid)
            t_hid_half = self.join(hid_toks)
            twd_prev = None
            for m in range(DC):
                if m >= 2:
                    wdep = [t_wd_ev[(m - 2, half)]]
                elif half == 1:
                    wdep = [t_wd_ev[(m + 6, 0)]]
                else:
                    wdep = [t_hid_half]
                twd = self.dma("wdld%d" % (m % 2),
                               self.wbuf[:, 2048 + (m % 2) * 2048:
                                         2048 + (m % 2) * 2048 + 2048],
                               self.w["wd2"][:, m * 4096 + half * 2048:
                                             m * 4096 + half * 2048 + 2048],
                               wdep, q="sync")
                psd = self.ps_c if m % 2 == 0 else self.ps_d

                def mmd(t, m=m, psd=psd):
                    last = None
                    for kk in range(16):
                        last = t.matmul(
                            psd[:, 0:512],
                            self.wbuf[:, 2048 + (m % 2) * 2048 + kk * 128:
                                       2048 + (m % 2) * 2048 + (kk + 1) * 128],
                            hid[:, kk * 512:(kk + 1) * 512],
                            start=(kk == 0), stop=(kk == 15))
                    return last
                dep = [twd, t_hid_half]
                if twd_prev:
                    dep.append(twd_prev)
                if half == 1:
                    dep.append(t_wd_ev[(m, 0)])
                if (m - 2, half) in t_wd_ev:
                    dep.append(t_wd_ev[(m - 2, half)])
                tmmd = self.run("tensor", dep, mmd)
                twd_prev = tmmd
                if half == 0:
                    tev = self.run("scalar", [tmmd], lambda s, m=m, psd=psd:
                                   s.copy(Q2s[:, m * 512:(m + 1) * 512], psd[:, 0:512]))
                else:
                    tev = self.run("vector", [tmmd], lambda v, m=m, psd=psd:
                                   v.tensor_tensor(Q2s[:, m * 512:(m + 1) * 512],
                                                   Q2s[:, m * 512:(m + 1) * 512],
                                                   psd[:, 0:512], ALU.add))
                t_wd_ev[(m, half)] = tev
        t_ffn = self.join([t_wd_ev[(m, 1)] for m in range(DC)])

        tscl = self.dma_many("scld", [
            (self.smalls[:, 44 + i:45 + i],
             self.s_ropeg[rsel + i * 128:rsel + (i + 1) * 128, 64:65])
            for i in range(KQ)], [t_prev], q="sync", slow=True)
        tsc = self.run("scalar", [tscl, t_shift], lambda s: [
            s.activation(self.sc_sel[:, b * KQ + i:b * KQ + i + 1],
                         self.smalls[:, 44 + i:45 + i], AF.Sigmoid,
                         bias=self.negshift[:]) for i in range(KQ)][-1])
        if "d_sc" in self.dbg and b == NB - 1:
            self.dma("dbg", self.dbg["d_sc"], self.sc_sel[:], [tsc], q="gpsimd")
        t_x2ld = self.dma_many("x2e2", [
            (V2s[:, i * D:(i + 1) * D],
             self.s_e[rsel + i * 128:rsel + (i + 1) * 128, :]) for i in range(KQ)],
            [t_ffn], q="sync")
        t_selld2 = self.dma_many("seld3", [
            (O2s[:, i * D:(i + 1) * D],
             self.s_selg[rsel + i * 128:rsel + (i + 1) * 128, :]) for i in range(KQ)],
            [t_ffn], q="sync")
        u_toks = []
        prev_va = None
        for i in range(KQ):
            def tf(t, i=i):
                last = None
                for c in range(4):
                    last = t.transpose(self.ps_c[:, c * 128:(c + 1) * 128],
                                       Q2s[:, c * 512 + i * 128:c * 512 + (i + 1) * 128],
                                       self.ident[:])
                for c in range(4, 8):
                    last = t.transpose(self.ps_d[:, (c - 4) * 128:(c - 3) * 128],
                                       Q2s[:, c * 512 + i * 128:c * 512 + (i + 1) * 128],
                                       self.ident[:])
                return last
            ttr6 = self.run("tensor", [t_ffn] + ([prev_va] if prev_va else []), tf)

            def vu(v, i=i):
                v.tensor_tensor(K2s[:, i * D:i * D + 512], self.ps_c[:],
                                V2s[:, i * D:i * D + 512], ALU.add)
                v.tensor_tensor(K2s[:, i * D + 512:(i + 1) * D], self.ps_d[:],
                                V2s[:, i * D + 512:(i + 1) * D], ALU.add)
                v.drain()
                v.tensor_tensor(K2s[:, i * D:(i + 1) * D],
                                K2s[:, i * D:(i + 1) * D],
                                O2s[:, i * D:(i + 1) * D], ALU.subtract)
                v.drain()
                return v.tensor_scalar(K2s[:, i * D:(i + 1) * D],
                                       K2s[:, i * D:(i + 1) * D],
                                       self.sc_sel[:, b * KQ + i:b * KQ + i + 1],
                                       None, op0=ALU.mult)
            tvu = self.run("vector", [ttr6, t_x2ld, t_selld2, tsc], vu)
            prev_va = tvu
            u_toks.append(tvu)
        t_u = self.join(u_toks)

        t_prev_f = None
        out_toks = []
        for iq in range(TQ):
            tld1 = self.dma_many("f1ld", [
                (V2s[:, 0:1024], self.s_x1[row0 + iq * 128:row0 + (iq + 1) * 128, :]),
                (V2s[:, 1024:2048],
                 self.s_proc[row0 + iq * 128:row0 + (iq + 1) * 128, :])],
                [t_u] + ([t_prev_f] if t_prev_f else []), q="sync")

            def tsc2(t, iq=iq):
                last = None
                for jq in range(KQ):
                    lhs = Pm[:, jq * T + iq * 128:jq * T + (iq + 1) * 128]
                    for nh in range(2):
                        last = t.matmul(
                            self.ps_s[:, nh * 512:(nh + 1) * 512], lhs,
                            K2s[:, jq * D + nh * 512:jq * D + (nh + 1) * 512],
                            start=(jq == 0), stop=(jq == KQ - 1))
                return last
            dep = [t_u]
            if t_prev_f:
                dep.append(t_prev_f)
            tmm = self.run("tensor", dep, tsc2)

            def vf(v, iq=iq):
                v.tensor_tensor(V2s[:, 2048:3072], self.ps_s[:, 0:1024],
                                V2s[:, 0:1024], ALU.add)
                v.drain()
                return v.tensor_tensor(V2s[:, 2048:3072], V2s[:, 2048:3072],
                                       V2s[:, 1024:2048], ALU.add)
            tvf = self.run("vector", [tmm, tld1], vf)
            tout = self.dma("outw", self.out[row0 + iq * 128:row0 + (iq + 1) * 128, :],
                            V2s[:, 2048:3072], [tvf], q="gpsimd")
            t_prev_f = self.join([tvf, tout])
            out_toks.append(tout)
        return self.join(out_toks + [t_prev_f])

    def finish(self, deps):
        def g(gp):
            for name, (sem, cnt, q) in self.streams.items():
                if cnt:
                    gp.wait_ge(sem, cnt)
            return gp.memset(self.smalls[:, 62:63], 0.0)
        self.run("gpsimd", deps, g)

    def finish_dev(self, deps):
        t1 = self.dma("fin", self.out[:, :], self.s_x1[:, :], deps, q="gpsimd")

        def g(gp):
            for name, (sem, cnt, q) in self.streams.items():
                if cnt:
                    gp.wait_ge(sem, cnt)
            return gp.memset(self.smalls[:, 62:63], 0.0)
        self.run("gpsimd", [t1], g)


def build_program(dbg_names=(), phases=("A",)):
    nc = bass.Bass("TRN2", target_bir_lowering=False)
    es = ExitStack()
    with es:
        block = es.enter_context(nc.Block())
        mk = MK(nc, es, block, dbg_names)
        t = mk.setup()
        toks = list(t)
        tA = mk.join(list(t))
        if "A" in phases:
            for b in range(NB):
                tA = mk.phaseA(b, tA)
            toks.append(tA)
        if "B" in phases:
            for b in range(NB):
                tA = mk.phaseB(b, tA)
            toks.append(tA)
        if "C" in phases:
            for b in range(NB):
                tA = mk.phaseC(b, tA)
            toks.append(tA)
        if "D" in phases:
            t_shift, t_psfree = mk.phase_shift(tA)
            tA = t_psfree
            for b in range(NB):
                tA = mk.phaseD(b, tA)
                if "E" in phases:
                    tA = mk.phaseE(b, tA, t_shift)
            toks.append(tA)
            toks.append(t_shift)
        if "E" in phases:
            mk.finish(toks)
        else:
            mk.finish_dev(toks)
    return nc, mk


def host_inputs(xfull, W, core):
    ident, causal, tri, perm = host_consts()
    m = {
        "x_in": np.ascontiguousarray(
            xfull[2 * core:2 * core + 2].reshape(NB * T, D)),
        "ident_in": ident, "causal_in": causal, "tri_in": tri, "perm_in": perm,
    }
    for nm, arr, transpose in [
        ("w_delta", W["L0_attn_W_delta"], True),
        ("w_select", W["L0_attn_W_select"], True),
        ("w_in", W["L0_attn_W_in"], True),
        ("w_gate", W["L0_attn_W_gate"], True),
        ("w_out", W["L0_attn_W_out"], True),
        ("wq1", W["L0_ffn_wq1"], False), ("wk1", W["L0_ffn_wk1"], False),
        ("wv1", W["L0_ffn_wv1"], False), ("wo1", W["L0_ffn_wo1"], False),
        ("wg1", W["L0_ffn_wg1"], False), ("wu1", W["L0_ffn_wu1"], False),
        ("wd1", W["L0_ffn_wd1"], False),
        ("wq2", W["L0_ffn_wq2"], False), ("wk2", W["L0_ffn_wk2"], False),
        ("wv2", W["L0_ffn_wv2"], False), ("wo2", W["L0_ffn_wo2"], False),
        ("wg2", W["L0_ffn_wg2"], False), ("wu2", W["L0_ffn_wu2"], False),
        ("wd2", W["L0_ffn_wd2"], False),
        ("prior_w", W["L0_ffn_prior_w"], False),
    ]:
        a = np.asarray(arr, np.float32)
        m[nm] = host_prep_weight(np.ascontiguousarray(a.T) if transpose else a)
    for nm, key in [("log_a", "L0_attn_log_a"), ("ln1_1", "L0_ffn_ln1_1"),
                    ("ln2_1", "L0_ffn_ln2_1"), ("ln1_2", "L0_ffn_ln1_2"),
                    ("ln2_2", "L0_ffn_ln2_2"), ("prior_nw", "L0_ffn_prior_norm_w")]:
        m[nm] = np.asarray(W[key], np.float32).reshape(1, D)
    return m



_CACHED = {}


def _get_program():
    if "nc" not in _CACHED:
        _CACHED["nc"], _CACHED["mk"] = build_program(
            dbg_names=(), phases=("A", "B", "C", "D", "E"))
    return _CACHED["nc"]


def kernel(**inputs):
    from concourse.bass_utils import run_bass_kernel_spmd
    x = np.asarray(inputs["L0_attn_x"], np.float32)
    nc = _get_program()
    ims = [host_inputs(x, inputs, core=c) for c in range(8)]
    res = run_bass_kernel_spmd(nc, ims, core_ids=list(range(8)))
    out = np.concatenate([res.results[c]["out"] for c in range(8)], 0)
    return out.reshape(16, T, D)


# version 02
# speedup vs baseline: 1.0869x; 1.0869x over previous
import numpy as np
from contextlib import ExitStack

import concourse.bass as bass
import concourse.mybir as mybir

F32 = mybir.dt.float32
I32 = mybir.dt.int32
AF = mybir.ActivationFunctionType
ALU = mybir.AluOpType
AX = mybir.AxisListType

D = 1024
T = 2048
NB = 2
DC = D // 128
TQ = T // 128
NH = 16
DH = 64
FF = 4096
FC = FF // 128
KSEL = 512
KQ = KSEL // 128
LN_EPS = 1e-5
RMS_EPS = 1e-6
NEG_BIG = -1e30
CW = (6.28125, 0.0019350051879882812, 3.019916050561733e-07)
LOG1E4_32 = float(np.log(10000.0) / 32.0)

ENGS = ("tensor", "vector", "scalar", "gpsimd", "sync")


def _mtb_view(mtb, mi):
    import dataclasses
    base = mtb[:, 0:1]
    pitch = base.ap[0][0]
    return dataclasses.replace(
        base, offset=base.offset + mi * 128,
        ap=[[pitch, 128], [1024, 4], [1, 128]])


def host_prep_weight(w):
    K, M = w.shape
    kc, mc = K // 128, M // 128
    return np.ascontiguousarray(
        w.reshape(kc, 128, mc, 128).transpose(1, 2, 0, 3).reshape(128, mc * kc * 128))


def host_consts():
    ident = np.eye(128, dtype=np.float32)
    causal = np.where(np.tril(np.ones((128, 128), bool)), 0.0, NEG_BIG).astype(np.float32)
    tri_up = np.triu(np.ones((128, 128), np.float32), 1)
    perm = np.zeros((128, 128), np.float32)
    for m in range(128):
        perm[m ^ 32, m] = 1.0
    return ident, causal, tri_up, perm


class Arena:
    def __init__(self, tensor, n):
        self.t, self.n, self.off = tensor, n, 0

    def alloc(self, cols):
        assert self.off + cols <= self.n, f"arena overflow {self.off}+{cols}>{self.n}"
        ap = self.t[:, self.off:self.off + cols]
        self.off += cols
        return ap

    def mark(self):
        return self.off

    def reset(self, mark=0):
        self.off = mark


class MK:
    SERIAL = False

    def __init__(self, nc, es, block, dbg_names=()):
        self.nc, self.es, self.block = nc, es, block
        self.dbg_names = set(dbg_names)
        self.dbg = {}
        self._last_tok = None

        dt = nc.dram_tensor
        self.x_in = dt("x_in", [NB * T, D], F32, kind="ExternalInput").ap()
        self.out = dt("out", [NB * T, D], F32, kind="ExternalOutput").ap()

        self.w = {}
        for name, K, M in [
            ("w_delta", D, D), ("w_select", D, D), ("w_in", D, D),
            ("w_gate", D, D), ("w_out", D, D),
            ("wq1", D, D), ("wk1", D, D), ("wv1", D, D), ("wo1", D, D),
            ("wg1", D, FF), ("wu1", D, FF), ("wd1", FF, D),
            ("wq2", D, D), ("wk2", D, D), ("wv2", D, D), ("wo2", D, D),
            ("wg2", D, FF), ("wu2", D, FF), ("wd2", FF, D),
            ("prior_w", D, 2 * D),
        ]:
            self.w[name] = dt(name, [128, (M // 128) * (K // 128) * 128], F32,
                              kind="ExternalInput").ap()
        for name in ["log_a", "ln1_1", "ln2_1", "ln1_2", "ln2_2", "prior_nw"]:
            self.w[name] = dt(name, [1, D], F32, kind="ExternalInput").ap()
        self.ident_in = dt("ident_in", [128, 128], F32, kind="ExternalInput").ap()
        self.causal_in = dt("causal_in", [128, 128], F32, kind="ExternalInput").ap()
        self.tri_in = dt("tri_in", [128, 128], F32, kind="ExternalInput").ap()
        self.perm_in = dt("perm_in", [128, 128], F32, kind="ExternalInput").ap()

        self.s_x1 = dt("s_x1", [NB * T, D], F32, kind="Internal").ap()
        self.s_h = dt("s_h", [NB * T, D], F32, kind="Internal").ap()
        self.s_x2 = dt("s_x2", [NB * T, D], F32, kind="Internal").ap()
        self.s_proc = dt("s_proc", [NB * T, D], F32, kind="Internal").ap()
        self.s_a = dt("s_a", [NB * 128, DC * T], F32, kind="Internal").ap()
        self.s_q = dt("s_q", [NB * 128, DC * T], F32, kind="Internal").ap()
        self.s_v = dt("s_v", [NB * T, D], F32, kind="Internal").ap()
        self.s_o = dt("s_o", [NB * T, D], F32, kind="Internal").ap()
        self.s_selg = dt("s_selg", [NB * KSEL, D], F32, kind="Internal").ap()
        self.s_ropeg = dt("s_ropeg", [NB * KSEL, 128], F32, kind="Internal").ap()
        self.s_e = dt("s_e", [NB * KSEL, D], F32, kind="Internal").ap()
        self.s_bc = dt("s_bc", [1, T], F32, kind="Internal").ap()
        self.s_bc2 = dt("s_bc2", [1, T], F32, kind="Internal").ap()
        self.cc_in_t = dt("cc_in", [1, 512], F32, kind="Internal")
        self.cc_out_t = dt("cc_out", [1, 512], F32, kind="Internal")

        sb = lambda n, s, dtt=F32: es.enter_context(nc.sbuf_tensor(n, s, dtt))
        self.ident = sb("ident", [128, 128])
        self.causal = sb("causal", [128, 128])
        self.tri_up = sb("tri_up", [128, 128])
        self.perm = sb("perm", [128, 128])
        self.cosP = sb("cosP", [128, T])
        self.ssgnP = sb("ssgnP", [128, T])
        self.cos_tm = sb("cos_tm", [128, TQ * 32])
        self.sin_tm = sb("sin_tm", [128, TQ * 32])
        self.gam = {n: sb("gam_" + n, [128, D]) for n in
                    ["ln1_1", "ln2_1", "ln1_2", "ln2_2", "prior_nw"]}
        self.a_sig = sb("a_sig", [128, DC])
        self.iota512 = sb("iota512", [128, 512])
        self.consts = sb("consts", [128, 8])
        self.s_all = sb("s_all", [128, NB * TQ])
        self.sel_all = sb("sel_all", [128, NB * TQ])
        self.rank_all = sb("rank_all", [128, NB * TQ])
        self.r_all = sb("r_all", [128, NB * TQ])
        self.dsum = sb("dsum", [128, 8])
        self.negshift = sb("negshift", [128, 1])
        self.sc_sel = sb("sc_sel", [128, NB * KQ])
        self.s_sel = sb("s_sel", [128, NB * KQ])
        self.rowbuf = sb("rowbuf", [1, T])
        self.smalls = sb("smalls", [128, 64])
        self.stat = sb("stat", [128, 24])
        self.wbuf = sb("wbuf", [128, 8 * 1024])
        self.pbuf = sb("pbuf", [128, 2048])
        self.iscr = sb("iscr", [128, 64], I32)
        self.ptb = sb("ptb", [128, 2048])
        ARENA_COLS = 24576
        self.arena_t = sb("arena", [128, ARENA_COLS])
        self.arena = Arena(self.arena_t, ARENA_COLS)

        ps = lambda n, c: es.enter_context(nc.psum_tensor(n, [128, c], F32))
        self.ps_s = ps("ps_s", 2048)
        self.ps_a = ps("ps_a", 512)
        self.ps_b = ps("ps_b", 512)
        self.ps_c = ps("ps_c", 512)
        self.ps_d = ps("ps_d", 512)

        self.sem = {}
        for e in ENGS:
            self.sem[e] = [es.enter_context(nc.semaphore("p_" + e)), 0]
        self.streams = {}

        for nm, shape in [
            ("d_tab", [128, 2 * T + 1024]), ("d_dbg2", [128, 160]),
            ("d_stat", [128, 1048]),
            ("d_xn", [128, DC * 512]), ("d_gs", [128, DC * 512]),
            ("d_dec", [128, 2048]),
            ("d_x1", [NB * T, D]), ("d_h", [NB * T, D]),
            ("d_a", [NB * 128, DC * T]), ("d_k", [NB * 128, DC * T]),
            ("d_x2", [NB * T, D]), ("d_proc", [NB * T, D]),
            ("d_s", [128, NB * TQ]), ("d_sel", [128, NB * TQ]),
            ("d_r", [128, NB * TQ]),
            ("d_selg", [NB * KSEL, D]), ("d_rope", [NB * KSEL, 2 * 32]),
            ("d_y2", [NB * KSEL, D]), ("d_u", [NB * KSEL, D]),
            ("d_sc", [128, NB * KQ]),
        ]:
            if nm in self.dbg_names:
                self.dbg[nm] = dt(nm, shape, F32, kind="ExternalOutput").ap()

    def run(self, eng, deps, fn):
        if self.SERIAL:
            deps = list(deps) + ([self._last_tok] if self._last_tok else [])
        sem, _ = self.sem[eng]
        self.sem[eng][1] += 1
        target = self.sem[eng][1]

        def f(e):
            for (s, c) in deps:
                if c > 0:
                    e.wait_ge(s, c)
            last = fn(e)
            assert last is not None, f"section on {eng} returned no instruction"
            last.then_inc(sem, 1)
        getattr(self.block, eng)(f)
        if self.SERIAL:
            self._last_tok = (sem, target)
        return (sem, target)

    def _stream(self, name, q):
        if name not in self.streams:
            self.streams[name] = [
                self.es.enter_context(self.nc.semaphore("st_" + name)), 0, q]
        st = self.streams[name]
        assert st[2] == q, f"stream {name} queue mismatch"
        return st

    def dma(self, name, out, in_, deps=(), q="sync", slow=False):
        return self.dma_many(name, [(out, in_)], deps, q, slow)

    def dma_many(self, name, pairs, deps=(), q="sync", slow=False):
        if self.SERIAL:
            deps = list(deps) + ([self._last_tok] if self._last_tok else [])
        st = self._stream(name, q)
        sem = st[0]
        st[1] += 16 * len(pairs)
        target = st[1]

        def f(e):
            for (s, c) in deps:
                if c > 0:
                    e.wait_ge(s, c)
            for out, in_ in pairs:
                kw = {"allow_slow_non_contiguous": True} if slow else {}
                e.dma_start(out=out, in_=in_, **kw).then_inc(sem, 16)
        getattr(self.block, q)(f)
        if self.SERIAL:
            self._last_tok = (sem, target)
        return (sem, target)

    def dump(self, name, pairs, deps):
        if name not in self.dbg:
            return None
        d = self.dbg[name]
        return self.dma_many("gpsimd", [(fn(d), src) for fn, src in pairs], deps)

    @staticmethod
    def fmc(ap, c, cols=T, width=1):
        return ap[:, c * cols:(c + width) * cols]

    def setup(self):
        self.arena.reset(0)
        tb = self.arena.alloc(T)
        pairs = [
            (self.ident[:], self.ident_in), (self.causal[:], self.causal_in),
            (self.tri_up[:], self.tri_in), (self.perm[:], self.perm_in)]
        pairs += [(self.gam[n][:], self.w[n].to_broadcast([128, D]))
                  for n in ["ln1_1", "ln2_1", "ln1_2", "ln2_2", "prior_nw"]]
        pairs += [(self.smalls[:, c:c + 1],
                   self.w["log_a"][0, c * 128:(c + 1) * 128]
                   .rearrange("(p one) -> p one", one=1)) for c in range(DC)]
        t_init = self.dma_many("init", pairs)

        def g1(g):
            g.iota(self.iota512[:], pattern=[[1, 512]], base=0,
                   channel_multiplier=0, allow_small_or_imprecise_dtypes=True)
            g.iota(tb, pattern=[[1, T]], base=0,
                   channel_multiplier=0, allow_small_or_imprecise_dtypes=True)
            g.iota(self.ptb[:, 0:32], pattern=[[1, 32]], base=0,
                   channel_multiplier=0, allow_small_or_imprecise_dtypes=True)
            return g.iota(self.smalls[:, 8:9], pattern=[[0, 1]], base=0,
                          channel_multiplier=1, allow_small_or_imprecise_dtypes=True)
        tg1 = self.run("gpsimd", [t_init], g1)

        def g_tcols(g):
            last = None
            for qt in range(TQ):
                last = g.tensor_scalar(self.smalls[:, 12 + qt:13 + qt],
                                       self.smalls[:, 8:9], float(qt * 128), None,
                                       op0=ALU.add)
            return last
        tg1 = self.run("gpsimd", [tg1], g_tcols)

        ts_asig = self.run("scalar", [t_init], lambda s: s.activation(
            self.a_sig[:], self.smalls[:, 0:DC], AF.Sigmoid))

        c = float(np.log(10000.0) / 32.0)
        c_hi = float(np.float32(np.float64(c)))
        u = np.float32(c_hi).view(np.uint32) if False else None
        ch = np.frombuffer(np.float32(c).tobytes(), dtype=np.uint32)[0]
        ch12 = np.frombuffer((ch & np.uint32(0xFFFFF000)).tobytes(),
                             dtype=np.float32)[0]
        c_lo = float(np.float32(np.float64(c) - np.float64(ch12)))
        c_hi = float(ch12)

        def v_inv(v):
            j = self.ptb[:, 0:32]
            e1 = self.pbuf[:, 0:32]
            e2 = self.pbuf[:, 32:64]
            v.tensor_scalar(e1, j, -c_hi, None, op0=ALU.mult)
            v.tensor_scalar(e2, j, -c_lo, None, op0=ALU.mult)
            v.drain()
            return v.tensor_tensor(e1, e1, e2, ALU.add)
        tvi = self.run("vector", [tg1], v_inv)
        ts_inv = self.run("scalar", [tvi], lambda s: s.activation(
            self.ptb[:, 32:64], self.pbuf[:, 0:32], AF.Exp))

        def memc(v):
            v.memset(self.consts[:, 0:1], -0.5)
            v.memset(self.consts[:, 1:2], 1.0 / D)
            return v.memset(self.consts[:, 2:3], 0.0)
        tvc = self.run("vector", [], memc)

        CLAMP = 3.1415925
        inv_bc = self.ptb[:, 32:64]

        def reduce_block(v, ang, kq, ki):
            w = ang.shape[-1]
            v.tensor_scalar(kq, ang, 1.0 / (2 * np.pi), None, op0=ALU.mult)
            v.drain()
            v.tensor_copy(self.iscr[:, 0:w], kq)
            v.drain()
            v.tensor_copy(kq, self.iscr[:, 0:w])
            v.drain()
            for cc in CW:
                v.scalar_tensor_tensor(ang, kq, -cc, ang, op0=ALU.mult, op1=ALU.add)
                v.drain()
            return v.tensor_scalar(ang, ang, CLAMP, -CLAMP, op0=ALU.min, op1=ALU.max)

        def build_tm(dst, shift, deps):
            def vv(v):
                last = None
                for qt in range(TQ):
                    tcol = self.smalls[:, 12 + qt:13 + qt]
                    ang = dst[:, qt * 32:(qt + 1) * 32]
                    if shift == 0.0:
                        v.tensor_scalar(ang, inv_bc, tcol, None, op0=ALU.mult)
                    else:
                        v.tensor_scalar(ang, inv_bc, tcol, shift,
                                        op0=ALU.mult, op1=ALU.add)
                    v.drain()
                    if qt == 1 and shift == 0.0 and "d_dbg2" in self.dbg:
                        v.tensor_copy(self.ptb[:, 1024:1056], ang)
                        v.drain()
                    last = reduce_block(v, ang, self.pbuf[:, 0:32],
                                        self.pbuf[:, 32:64])
                    if qt == 1 and shift == 0.0 and "d_dbg2" in self.dbg:
                        v.tensor_copy(self.ptb[:, 1056:1088], self.pbuf[:, 0:32])
                        v.tensor_copy(self.ptb[:, 1088:1120], ang)
                return last
            tv = self.run("vector", deps, vv)

            def ss(s):
                last = None
                for qt in range(TQ):
                    a = dst[:, qt * 32:(qt + 1) * 32]
                    last = s.activation(a, a, AF.Sin)
                return last
            return self.run("scalar", [tv], ss)

        t_sintm = build_tm(self.sin_tm, 0.0, [ts_inv, tvc, ts_asig])
        t_costm = build_tm(self.cos_tm, np.pi / 2, [t_sintm])

        def t_fm(t):
            last = None
            for g4 in range(4):
                ps = self.ps_c if g4 % 2 == 0 else self.ps_d
                for i in range(4):
                    qt = g4 * 4 + i
                    t.transpose(ps[0:32, i * 128:(i + 1) * 128],
                                self.sin_tm[:, qt * 32:(qt + 1) * 32],
                                self.ident[:])
                    last = t.transpose(self.ps_s[0:32, i * 128:(i + 1) * 128],
                                       self.cos_tm[:, qt * 32:(qt + 1) * 32],
                                       self.ident[:])
            return last
        toks = []
        for g4 in range(4):
            ps = self.ps_c if g4 % 2 == 0 else self.ps_d

            def tf(t, g4=g4, ps=ps):
                last = None
                for i in range(4):
                    qt = g4 * 4 + i
                    t.transpose(ps[0:32, i * 128:(i + 1) * 128],
                                self.sin_tm[:, qt * 32:(qt + 1) * 32],
                                self.ident[:])
                    last = t.transpose(self.ps_s[0:32, (g4 % 2) * 512 + i * 128:
                                                 (g4 % 2) * 512 + (i + 1) * 128],
                                       self.cos_tm[:, qt * 32:(qt + 1) * 32],
                                       self.ident[:])
                return last
            dep = [t_costm] + ([toks[-2]] if len(toks) >= 2 else [])
            tt = self.run("tensor", dep, tf)

            def ve(v, g4=g4, ps=ps):
                v.tensor_copy(self.ssgnP[0:32, g4 * 512:(g4 + 1) * 512],
                              ps[0:32, 0:512])
                return v.tensor_copy(self.cosP[0:32, g4 * 512:(g4 + 1) * 512],
                                     self.ps_s[0:32, (g4 % 2) * 512:
                                               (g4 % 2) * 512 + 512])
            toks.append(self.run("vector", [tt], ve))
        t_rows = self.join(toks)

        rep_pairs = []
        for g0 in (32, 64, 96):
            rep_pairs.append((self.ssgnP[g0:g0 + 32, :], self.ssgnP[0:32, :]))
            rep_pairs.append((self.cosP[g0:g0 + 32, :], self.cosP[0:32, :]))
        t_rep = self.dma_many("init", rep_pairs, [t_rows])

        def s_sgn(s):
            s.mul(self.ssgnP[0:32, :], self.ssgnP[0:32, :], -1.0)
            return s.mul(self.ssgnP[64:96, :], self.ssgnP[64:96, :], -1.0)
        t_sgn = self.run("scalar", [t_rep], s_sgn)

        if "d_dbg2" in self.dbg:
            self.dma("dbg", self.dbg["d_dbg2"][:, 0:64], self.smalls[:], [t_sgn], q="gpsimd")
            self.dma("dbg", self.dbg["d_dbg2"][:, 64:160], self.ptb[:, 1024:1120],
                     [t_sgn], q="gpsimd")
        if "d_tab" in self.dbg:
            self.dma("dbg", self.dbg["d_tab"][:, 0:T], self.cosP[:], [t_sgn], q="gpsimd")
            self.dma("dbg", self.dbg["d_tab"][:, T:2 * T], self.ssgnP[:], [t_sgn], q="gpsimd")
            self.dma("dbg", self.dbg["d_tab"][:, 2 * T:2 * T + 512], self.cos_tm[:],
                     [t_sgn], q="gpsimd")
            self.dma("dbg", self.dbg["d_tab"][:, 2 * T + 512:2 * T + 1024],
                     self.sin_tm[:], [t_sgn], q="gpsimd")
        self.arena.reset(0)
        self.t_setup = [t_sgn, t_init, tg1]
        return self.t_setup

    def ln_tile(self, x_ap, out_ap, scratch_ap, eps_col, deps, gamma=None, rms=False,
                stat_base=0):
        st = self.stat
        c_sum = st[:, stat_base + 0:stat_base + 1]
        c_ss = st[:, stat_base + 1:stat_base + 2]
        c_negm = st[:, stat_base + 2:stat_base + 3]
        c_var = st[:, stat_base + 3:stat_base + 4]
        c_m2 = st[:, stat_base + 4:stat_base + 5]
        c_sq = st[:, stat_base + 5:stat_base + 6]
        c_rstd0 = st[:, stat_base + 6:stat_base + 7]
        c_rstd = st[:, stat_base + 7:stat_base + 8]

        ts0 = self.run("scalar", deps, lambda s: s.activation(
            scratch_ap, x_ap, AF.Square, accum_out=c_ss))
        if not rms:
            tv0 = self.run("vector", deps, lambda v: v.tensor_reduce(
                c_sum, x_ap, axis=AX.X, op=ALU.add))
            tneg = self.run("scalar", [tv0], lambda s: s.mul(c_negm, c_sum, -1.0 / D))

            def v1(v):
                v.tensor_scalar(c_var, c_ss, 1.0 / D, None, op0=ALU.mult)
                v.tensor_tensor(c_m2, c_negm, c_negm, ALU.mult)
                v.drain()
                return v.tensor_tensor(c_var, c_var, c_m2, ALU.subtract)
            tv1 = self.run("vector", [tneg, ts0], v1)
        else:
            tv1 = self.run("vector", [ts0], lambda v: v.tensor_scalar(
                c_var, c_ss, 1.0 / D, None, op0=ALU.mult))

        ts1 = self.run("scalar", [tv1], lambda s: s.activation(
            c_sq, c_var, AF.Sqrt, bias=eps_col))
        tv2a = self.run("vector", [ts1], lambda v: v.reciprocal(c_rstd0, c_sq))
        tv2 = self.run("scalar", [tv2a], lambda s: s.copy(c_rstd, c_rstd0))

        nfree = x_ap.shape[-1]
        rb = c_rstd.to_broadcast([128, nfree])

        def vap(v):
            if rms:
                last = v.tensor_tensor(out_ap, x_ap, rb, ALU.mult)
            else:
                v.tensor_scalar(out_ap, x_ap, c_negm, None, op0=ALU.add)
                last = v.tensor_tensor(out_ap, out_ap, rb, ALU.mult)
            if gamma is not None:
                last = v.tensor_tensor(out_ap, out_ap, gamma, ALU.mult)
            return last
        return self.run("vector", [tv2], vap)

    def tp_tile_to_fm(self, tile_ap, dst_fn, deps, nchunk=DC, evac_eng="vector"):
        tok = None
        toks = []
        for g0 in range(0, nchunk, 4):
            ps = self.ps_c if (g0 // 4) % 2 == 0 else self.ps_d
            n = min(4, nchunk - g0)

            def tf(t, g0=g0, ps=ps, n=n):
                last = None
                for i in range(n):
                    last = t.transpose(ps[:, i * 128:(i + 1) * 128],
                                       tile_ap[:, (g0 + i) * 128:(g0 + i + 1) * 128],
                                       self.ident[:])
                return last
            dep = list(deps) + ([toks[-2]] if len(toks) >= 2 else [])
            tt = self.run("tensor", dep, tf)

            def ve(e, g0=g0, ps=ps, n=n):
                last = None
                for i in range(n):
                    last = e.tensor_copy(dst_fn(g0 + i), ps[:, i * 128:(i + 1) * 128])
                return last
            tok = self.run(evac_eng, [tt], ve)
            toks.append(tok)
        return self.join(toks) if len(toks) > 1 else tok

    def join(self, toks):
        toks = [t for t in toks if t]
        if not toks:
            return None
        if len(toks) == 1:
            return toks[0]
        return self.run("gpsimd", toks, lambda g: g.memset(self.smalls[:, 63:64], 0.0))

    def load_wblock(self, wname, m, kc, slot, stream, deps=()):
        cols = kc * 128
        dst = self.wbuf[:, slot * 1024: slot * 1024 + cols]
        src = self.w[wname][:, m * cols:(m + 1) * cols]
        return dst, self.dma(stream, dst, src, deps)

    def dump_dram(self, name, src_ap, rows, deps):
        if name not in self.dbg:
            return None
        return self.dma("dbg", self.dbg[name][rows[0]:rows[1], :],
                        src_ap[rows[0]:rows[1], :], deps, q="gpsimd")

    def phaseA(self, b, t_prev=None):
        A = self.arena
        A.reset(0)
        xn = A.alloc(DC * 512)
        gs = A.alloc(DC * 512)
        dec = A.alloc(512)
        upd = A.alloc(512)
        gat = A.alloc(512)
        sts = A.alloc(512)
        tmp1 = A.alloc(512)
        tmp2 = A.alloc(512)
        yfm = A.alloc(DC * 512)
        stage_in = A.alloc(4 * D)
        stage_out = A.alloc(4 * D)

        eps_ln = self.smalls[:, 30:31]
        tini = self.run("vector", [], lambda v: v.memset(eps_ln, LN_EPS))

        row0 = b * T
        W4 = ["w_delta", "w_select", "w_in", "w_gate"]
        t_stage_free = t_prev
        t_carry = [None] * DC
        prev = {"T4": t_prev} if t_prev else {}

        for sp in range(4):
            r0 = row0 + sp * 512
            tx = self.dma_many("xin", [
                (stage_in[:, i * D:(i + 1) * D],
                 self.x_in[r0 + i * 128: r0 + (i + 1) * 128, :]) for i in range(4)],
                deps=[t_stage_free] if t_stage_free else [])

            if b == 0 and sp == 0 and "d_xn" in self.dbg:
                pass
            tok_fm = []
            for i in range(4):
                xt = stage_in[:, i * D:(i + 1) * D]
                ot = stage_out[:, i * D:(i + 1) * D]
                tln = self.ln_tile(xt, ot, ot, eps_ln, [tx, tini])
                if b == 0 and sp == 0 and i == 0 and "d_stat" in self.dbg:
                    self.dma("dbg", self.dbg["d_stat"][:, 0:24], self.stat[:],
                             [tln], q="gpsimd")
                    self.dma("dbg", self.dbg["d_stat"][:, 24:1048],
                             ot, [tln], q="gpsimd")
                ttr = self.tp_tile_to_fm(
                    ot, lambda c, i=i: xn[:, c * 512 + i * 128: c * 512 + (i + 1) * 128],
                    [tln] + ([tok_fm[-1]] if tok_fm else []))
                tok_fm.append(ttr)
            t_xn = self.join(tok_fm)

            if b == 0 and sp == 0 and "d_xn" in self.dbg:
                self.dma("dbg", self.dbg["d_xn"], xn, [t_xn], q="gpsimd")
            wtok = {}
            for wi, wn in enumerate(W4):
                wtok[(wn, 0)] = self.load_wblock(
                    wn, 0, DC, wi, f"wA{wi}",
                    deps=[prev["T4"]] if "T4" in prev else [])

            gs_done = []
            for e in range(DC):
                if e + 1 < DC:
                    base_slot = 4 if e % 2 == 0 else 0
                    pdep = [prev["T4"]] if "T4" in prev else []
                    sset = "wB" if e % 2 == 0 else "wA"
                    for wi, wn in enumerate(W4):
                        wtok[(wn, e + 1)] = self.load_wblock(
                            wn, e + 1, DC, base_slot + wi, f"{sset}{wi}", deps=pdep)

                def accum_mm(wap, ps, rhs):
                    def f(t):
                        last = None
                        for k in range(DC):
                            last = t.matmul(ps[:, 0:512], wap[:, k * 128:(k + 1) * 128],
                                            rhs[:, k * 512:(k + 1) * 512],
                                            start=(k == 0), stop=(k == DC - 1))
                        return last
                    return f

                wapA, wtA = wtok[(W4[0], e)]
                T1 = self.run("tensor",
                              [wtA, t_xn] + ([prev["S4"]] if "S4" in prev else []),
                              accum_mm(wapA, self.ps_a, xn))
                S1 = self.run("scalar",
                              [T1] + ([prev["V2"]] if "V2" in prev else []),
                              lambda s: (s.activation(tmp1, self.ps_a[:, 0:512], AF.Abs),
                                         s.activation(tmp1, tmp1, AF.Exp, scale=-1.0),
                                         s.activation(tmp1, tmp1, AF.Ln, bias=1.0))[-1])

                def v1f(v, e=e):
                    v.scalar_tensor_tensor(tmp2, self.ps_a[:, 0:512], 0.0, tmp1,
                                           op0=ALU.max, op1=ALU.add)
                    return v.tensor_scalar(tmp2, tmp2, 1e-4, self.a_sig[:, e:e + 1],
                                           op0=ALU.max, op1=ALU.mult)
                V1 = self.run("vector", [S1], v1f)
                S2 = self.run("scalar", [V1], lambda s: s.activation(
                    dec, tmp2, AF.Exp, scale=-1.0))

                wapB, wtB = wtok[(W4[1], e)]
                T2 = self.run("tensor",
                              [wtB] + ([prev["S5"]] if "S5" in prev else []),
                              accum_mm(wapB, self.ps_b, xn))
                S3 = self.run("scalar", [T2, V1], lambda s: s.activation(
                    tmp1, self.ps_b[:, 0:512], AF.Sigmoid))

                wapC, wtC = wtok[(W4[2], e)]
                T3 = self.run("tensor", [wtC, V1], accum_mm(wapC, self.ps_a, xn))
                S4 = self.run("scalar", [T3, S2], lambda s: s.activation(
                    tmp2, self.ps_a[:, 0:512], AF.Tanh))
                V2 = self.run("vector", [S4, S3], lambda v: v.tensor_tensor(
                    upd, tmp1, tmp2, ALU.mult))

                wapD, wtD = wtok[(W4[3], e)]
                T4 = self.run("tensor", [wtD, S3], accum_mm(wapD, self.ps_b, xn))
                S5 = self.run("scalar",
                              [T4] + ([prev["V4"]] if "V4" in prev else []),
                              lambda s: s.activation(gat, self.ps_b[:, 0:512],
                                                     AF.Sigmoid))

                carry = self.stat[:, 16 + e:17 + e]
                scan_deps = [V2, S2] + ([prev["S6"]] if "S6" in prev else [])
                if sp > 0 and t_carry[e]:
                    scan_deps.append(t_carry[e])
                if sp == 0:
                    V3 = self.run("vector", scan_deps, lambda v: v.tensor_tensor_scan(
                        sts, dec, upd, 0.0, op0=ALU.mult, op1=ALU.add))
                else:
                    V3 = self.run("vector", scan_deps,
                                  lambda v, carry=carry: v.tensor_tensor_scan(
                                      sts, dec, upd, carry, op0=ALU.mult, op1=ALU.add))
                S6 = self.run("scalar", [V3], lambda s, carry=carry: s.copy(
                    carry, sts[:, 511:512]))
                t_carry[e] = S6
                V4 = self.run("vector", [V3, S5], lambda v, e=e: v.tensor_tensor(
                    gs[:, e * 512:(e + 1) * 512], gat, sts, ALU.mult))
                gs_done.append(V4)
                prev = {"S4": S4, "S5": S5, "T4": T4, "V2": V2, "V4": V4, "S6": S6}

            if b == 0 and sp == 0 and "d_gs" in self.dbg:
                self.dma("dbg", self.dbg["d_gs"], gs, [self.join(gs_done)], q="gpsimd")
                if "d_dec" in self.dbg:
                    self.dma("dbg", self.dbg["d_dec"][:, 0:512], dec,
                             [prev["S6"]], q="gpsimd")
                    self.dma("dbg", self.dbg["d_dec"][:, 512:1024], upd,
                             [prev["S6"]], q="gpsimd")
                    self.dma("dbg", self.dbg["d_dec"][:, 1024:1536], gat,
                             [prev["S6"]], q="gpsimd")
                    self.dma("dbg", self.dbg["d_dec"][:, 1536:2048], sts,
                             [prev["S6"]], q="gpsimd")
            t_gs = self.join(gs_done)

            wo = {}
            wo[0] = self.load_wblock("w_out", 0, DC, 0, "wA0", deps=[prev["T4"]])
            ev = {}
            tmm_hist = {}
            for f in range(DC):
                if f + 1 < DC:
                    pdep = [prev["T4"]] if f == 0 else [tmm_hist[f - 1]]
                    wo[f + 1] = self.load_wblock(
                        "w_out", f + 1, DC, 1 if f % 2 == 0 else 0,
                        "wA1" if f % 2 == 0 else "wA0", deps=pdep)
                wap, wt = wo[f]
                psX = self.ps_a if f % 2 == 0 else self.ps_b
                dep = [wt, t_gs]
                if f >= 2:
                    dep.append(ev[f - 2])
                elif f < 2 and "S4" in prev:
                    dep.append(prev["S5" if f == 1 else "S4"])
                tmm = self.run("tensor", dep, accum_mm(wap, psX, gs))
                tmm_hist[f] = tmm
                ev[f] = self.run("scalar", [tmm], lambda s, f=f, psX=psX: s.copy(
                    yfm[:, f * 512:(f + 1) * 512], psX[:, 0:512]))

            sp_end = []
            prev_va = None
            for i in range(4):
                def tf(t, i=i):
                    last = None
                    for c in range(4):
                        last = t.transpose(
                            self.ps_c[:, c * 128:(c + 1) * 128],
                            yfm[:, c * 512 + i * 128: c * 512 + (i + 1) * 128],
                            self.ident[:])
                    for c in range(4, 8):
                        last = t.transpose(
                            self.ps_d[:, (c - 4) * 128:(c - 3) * 128],
                            yfm[:, c * 512 + i * 128: c * 512 + (i + 1) * 128],
                            self.ident[:])
                    return last
                ttr = self.run("tensor",
                               [ev[DC - 1], ev[DC - 2]] + ([prev_va] if prev_va else []),
                               tf)
                x1t = stage_out[:, i * D:(i + 1) * D]
                ht = stage_in[:, i * D:(i + 1) * D]

                def va(v, i=i, x1t=x1t):
                    v.tensor_tensor(x1t[:, 0:512], self.ps_c[:],
                                    stage_in[:, i * D:i * D + 512], ALU.add)
                    return v.tensor_tensor(x1t[:, 512:1024], self.ps_d[:],
                                           stage_in[:, i * D + 512:(i + 1) * D], ALU.add)
                tva = self.run("vector", [ttr], va)
                prev_va = tva
                tst = self.dma("sp1", self.s_x1[r0 + i * 128:r0 + (i + 1) * 128, :],
                               x1t, [tva], q="gpsimd")
                tln = self.ln_tile(x1t, ht, ht, eps_ln, [tva], stat_base=8)
                tsh = self.dma("sph", self.s_h[r0 + i * 128:r0 + (i + 1) * 128, :],
                               ht, [tln], q="gpsimd")
                sp_end += [tst, tsh]
                last_ttr = ttr
            prev = dict(prev)
            prev["S4"] = ev[DC - 2]
            prev["S5"] = ev[DC - 1]
            prev["T4"] = last_ttr
            t_stage_free = self.join(sp_end)

        self.dump_dram("d_x1", self.s_x1, (row0, row0 + T), [t_stage_free])
        self.dump_dram("d_h", self.s_h, (row0, row0 + T), [t_stage_free])
        return t_stage_free

    def phaseB(self, b, t_prev):
        A = self.arena
        A.reset(0)
        region0 = A.alloc(DC * T)
        astr = A.alloc(DC * 512)
        chunkbuf = A.alloc(2048)
        row0 = b * T
        eps_rms = self.smalls[:, 31:32]
        t0 = self.run("vector", [t_prev], lambda v: v.memset(eps_rms, RMS_EPS))

        stage_in = region0[:, 0:4 * D]
        stage_out = region0[:, 4 * D:8 * D]
        afm = region0[:, 8 * D:8 * D + 4096]
        t_b1 = t0
        a_sp = []
        for sp in range(4):
            r0 = row0 + sp * 512
            th = self.dma_many("hld", [
                (stage_in[:, i * D:(i + 1) * D],
                 self.s_h[r0 + i * 128:r0 + (i + 1) * 128, :]) for i in range(4)],
                deps=[t_b1], q="sync")
            toks = []
            for i in range(4):
                ht = stage_in[:, i * D:(i + 1) * D]
                at = stage_out[:, i * D:(i + 1) * D]
                tln = self.ln_tile(ht, at, at, eps_rms, [th], rms=True,
                                   gamma=self.gam["ln1_1"][:])
                ttr = self.tp_tile_to_fm(
                    at, lambda c, i=i: afm[:, c * 512 + i * 128:c * 512 + (i + 1) * 128],
                    [tln] + ([toks[-1]] if toks else []))
                toks.append(ttr)
            t_sp = self.dma_many("asp", [
                (self.s_a[b * 128:(b + 1) * 128, c * T + sp * 512:c * T + (sp + 1) * 512],
                 afm[:, c * 512:(c + 1) * 512]) for c in range(DC)],
                deps=[self.join(toks)], q="gpsimd")
            t_b1 = self.join([t_sp, th])
            a_sp.append(t_sp)
        t_a = self.join(a_sp)
        self.dump_dram("d_a", self.s_a, (b * 128, (b + 1) * 128), [t_a])

        K_fm = region0
        rope_tmp = chunkbuf[:, 0:512]
        slice_buf = chunkbuf[:, 512:1024]
        vstage = chunkbuf[:, 1024:1536]

        def wfull(wname, deps):
            prs = [(self.wbuf[:, m * 1024:(m + 1) * 1024],
                    self.w[wname][:, m * 1024:(m + 1) * 1024]) for m in range(DC)]
            return self.dma_many("wfull", prs, deps, q="sync")

        def rope_slice(dst_ap, ps_sw, cs_slice, deps):
            def vf(v):
                v.tensor_tensor(rope_tmp, ps_sw, self.ssgnP[:, cs_slice], ALU.mult)
                v.tensor_tensor(dst_ap, dst_ap, self.cosP[:, cs_slice], ALU.mult)
                return v.tensor_tensor(dst_ap, dst_ap, rope_tmp, ALU.add)
            return self.run("vector", deps, vf)

        t_wprev = t_a
        t_rope_toks = []
        t_vq_done = []
        prev_psum = {}
        for proj, wname in (("k", "wk1"), ("q", "wq1"), ("v", "wv1")):
            t_w = wfull(wname, [t_wprev])
            t_an_prev = None
            for n in range(4):
                cs = slice(n * 512, (n + 1) * 512)
                t_an = self.dma_many("ast", [
                    (astr[:, c * 512:(c + 1) * 512],
                     self.s_a[b * 128:(b + 1) * 128, c * T + n * 512:c * T + (n + 1) * 512])
                    for c in range(DC)],
                    deps=[t_an_prev or t_a, t_w], q="sync")
                for m in range(DC):
                    ps = self.ps_a if m % 2 == 0 else self.ps_b
                    dep = [t_an]
                    key = ("mm", id(ps))
                    if key in prev_psum:
                        dep.append(prev_psum[key])

                    def mmf(t, m=m, ps=ps):
                        last = None
                        for k in range(DC):
                            last = t.matmul(ps[:, 0:512],
                                            self.wbuf[:, m * 1024 + k * 128:
                                                       m * 1024 + (k + 1) * 128],
                                            astr[:, k * 512:(k + 1) * 512],
                                            start=(k == 0), stop=(k == DC - 1))
                        return last
                    tmm = self.run("tensor", dep, mmf)

                    if proj == "k":
                        dst = K_fm[:, m * T + n * 512:m * T + (n + 1) * 512]
                        tev = self.run("scalar", [tmm], lambda s, dst=dst, ps=ps:
                                       s.copy(dst, ps[:, 0:512]))
                        psw = self.ps_c if m % 2 == 0 else self.ps_d
                        depw = [tev]
                        keyw = ("sw", id(psw))
                        if keyw in prev_psum:
                            depw.append(prev_psum[keyw])
                        tsw = self.run("tensor", depw, lambda t, dst=dst, psw=psw:
                                       t.matmul(psw[:, 0:512], self.perm[:], dst,
                                                start=True, stop=True))
                        trp = rope_slice(dst, (self.ps_c if m % 2 == 0 else
                                               self.ps_d)[:, 0:512], cs, [tsw])
                        prev_psum[("mm", id(ps))] = tev
                        prev_psum[("sw", id(psw))] = trp
                        t_rope_toks.append(trp)
                    elif proj == "q":
                        tev = self.run("scalar", [tmm], lambda s, ps=ps:
                                       s.copy(slice_buf, ps[:, 0:512]))
                        psw = self.ps_c if m % 2 == 0 else self.ps_d
                        depw = [tev]
                        keyw = ("sw", id(psw))
                        if keyw in prev_psum:
                            depw.append(prev_psum[keyw])
                        tsw = self.run("tensor", depw, lambda t, psw=psw:
                                       t.matmul(psw[:, 0:512], self.perm[:], slice_buf,
                                                start=True, stop=True))
                        trp = rope_slice(slice_buf, (self.ps_c if m % 2 == 0 else
                                                     self.ps_d)[:, 0:512], cs, [tsw])
                        tq = self.dma("qsp",
                                      self.s_q[b * 128:(b + 1) * 128,
                                               m * T + n * 512:m * T + (n + 1) * 512],
                                      slice_buf, [trp], q="gpsimd")
                        prev_psum[("mm", id(ps))] = trp
                        prev_psum[("sw", id(psw))] = trp
                        t_vq_done.append(tq)
                        prev_psum[("mm", id(self.ps_a if (m + 1) % 2 == 0 else
                                        self.ps_b))] = tq
                    else:
                        psw = self.ps_c if m % 2 == 0 else self.ps_d
                        depw = [tmm]
                        keyw = ("sw", id(psw))
                        if keyw in prev_psum:
                            depw.append(prev_psum[keyw])
                        tev = self.run("scalar", [tmm], lambda s, ps=ps:
                                       s.copy(slice_buf, ps[:, 0:512]))

                        def tvt(t, psw=psw):
                            last = None
                            for i in range(4):
                                last = t.transpose(psw[:, i * 128:(i + 1) * 128],
                                                   slice_buf[:, i * 128:(i + 1) * 128],
                                                   self.ident[:])
                            return last
                        ttp = self.run("tensor", [tev] + depw[1:], tvt)
                        tvs = self.run("vector", [ttp], lambda v, psw=psw:
                                       v.tensor_copy(vstage, psw[:, 0:512]))
                        pr = []
                        for i in range(4):
                            rr = row0 + n * 512 + i * 128
                            pr.append((self.s_v[rr:rr + 128, m * 128:(m + 1) * 128],
                                       vstage[:, i * 128:(i + 1) * 128]))
                        tvsp = self.dma_many("vsp", pr, [tvs], q="gpsimd")
                        prev_psum[("mm", id(ps))] = tev
                        prev_psum[("sw", id(psw))] = tvsp
                        t_vq_done.append(tvsp)
                t_an_prev = tmm
            t_wprev = self.join([tmm, t_rope_toks[-1] if proj == "k" else t_vq_done[-1]])
        t_K = self.join(t_rope_toks)
        t_VQ = self.join(t_vq_done)
        if "d_k" in self.dbg:
            def dk(g):
                return g.dma_start(out=self.dbg["d_k"][b * 128:(b + 1) * 128, :],
                                   in_=K_fm)
            self.dma("dbg", self.dbg["d_k"][b * 128:(b + 1) * 128, :], K_fm,
                     [t_K], q="gpsimd")

        qtile = astr[:, 0:1024]
        vbuf = [astr[:, 1024 + i * 512: 1024 + (i + 1) * 512] for i in range(2)]
        o_tm = astr[:, 2048 + 512:2048 + 512 + 1024]
        st = self.stat
        negm = st[:, 0:1]
        negm8 = st[:, 1:2]
        ell = st[:, 2:3]
        rell = lambda h: st[:, 8 + (h % 8):9 + (h % 8)]

        t_ob = [t_K, t_VQ]
        t_prev_o = None
        t_vload = {}
        t_exp_prev = None
        t_pt_prev = None
        t_av_prev = {}
        for qt in range(TQ):
            klen = (qt + 1) * 128
            G = (qt + 4) // 4
            tq = self.dma_many("qld", [
                (qtile[:, c * 128:(c + 1) * 128],
                 self.s_q[b * 128:(b + 1) * 128,
                          c * T + qt * 128:c * T + (qt + 1) * 128])
                for c in range(DC)],
                deps=t_ob + ([t_prev_o] if t_prev_o else []), q="sync")
            ev_toks = []
            for h in range(NH):
                hp, hh = h // 2, h % 2
                def smm(t, hp=hp, hh=hh, klen=klen):
                    last = None
                    for c0 in range(0, klen, 512):
                        w = min(512, klen - c0)
                        last = t.matmul(
                            self.ps_s[:, c0:c0 + w],
                            qtile[hh * 64:(hh + 1) * 64, hp * 128:(hp + 1) * 128],
                            K_fm[hh * 64:(hh + 1) * 64, hp * T + c0:hp * T + c0 + w],
                            start=True, stop=True)
                    return last
                dep_s = [tq] + ([t_exp_prev] if t_exp_prev else [])
                tS = self.run("tensor", dep_s, smm)
                def vmx(v, qt=qt, klen=klen):
                    v.tensor_tensor(self.ps_s[:, qt * 128:klen],
                                    self.ps_s[:, qt * 128:klen],
                                    self.causal[:], ALU.add)
                    v.drain()
                    v.tensor_reduce(negm, self.ps_s[:, 0:klen], axis=AX.X,
                                    op=ALU.max, negate=True)
                    v.drain()
                    return v.tensor_scalar(negm8, negm, 0.125, None, op0=ALU.mult)
                tM = self.run("vector", [tS], vmx)
                tE = self.run("scalar", [tM], lambda s, klen=klen: s.activation(
                    self.pbuf[:, 0:klen], self.ps_s[:, 0:klen], AF.Exp,
                    scale=0.125, bias=negm8, accum_out=ell))
                t_exp_prev = tE
                tR = self.run("vector", [tE], lambda v, h=h: v.reciprocal(rell(h), ell))
                first_av = True
                for g in range(G):
                    kn = min(512, klen - g * 512)
                    nt = (kn + 127) // 128
                    psb = self.ps_c if g % 2 == 0 else self.ps_d

                    def tptf(t, g=g, nt=nt, psb=psb):
                        last = None
                        for i in range(nt):
                            kt = g * 4 + i
                            last = t.transpose(psb[:, i * 128:(i + 1) * 128],
                                               self.pbuf[:, kt * 128:(kt + 1) * 128],
                                               self.ident[:])
                        return last
                    depp = [tE] + ([t_pt_prev] if t_pt_prev else [])
                    tT = self.run("tensor", depp, tptf)
                    tEv = self.run("vector", [tT], lambda v, g=g, nt=nt, psb=psb:
                                   v.tensor_copy(self.ptb[:, g * 512:g * 512 + nt * 128],
                                                 psb[:, 0:nt * 128]))
                    t_pt_prev = tEv
                    pr = []
                    for i in range(nt):
                        rr = row0 + (g * 4 + i) * 128
                        pr.append((vbuf[g % 2][:, i * 128:(i + 1) * 128],
                                   self.s_v[rr:rr + 128, hp * 128:(hp + 1) * 128]))
                    depv = [t_VQ]
                    if (g % 2) in t_av_prev:
                        depv.append(t_av_prev[g % 2])
                    t_vload[g] = self.dma_many("vld%d" % (g % 2), pr, depv, q="sync")

                    def tavf(t, g=g, nt=nt, hh=hh, h=h, first=first_av,
                             last_g=(g == G - 1)):
                        lasti = None
                        for i in range(nt):
                            kt = g * 4 + i
                            lasti = t.matmul(
                                self.ps_b[:, (h % 8) * 64:(h % 8) * 64 + 64],
                                self.ptb[:, kt * 128:(kt + 1) * 128],
                                vbuf[g % 2][:, i * 128 + hh * 64:i * 128 + hh * 64 + 64],
                                start=(first and i == 0),
                                stop=(last_g and i == nt - 1))
                        return lasti
                    depav = [tEv, t_vload[g], tR]
                    if h % 8 == 0 and first_av and ev_toks:
                        depav.append(ev_toks[-1])
                    tAV = self.run("tensor", depav, tavf)
                    t_av_prev[g % 2] = tAV
                    first_av = False
                if h % 8 == 7:
                    base = (h // 8) * 512

                    def sev(s, h=h, base=base):
                        last = None
                        for hs in range(8):
                            hq = (h // 8) * 8 + hs
                            last = s.activation(
                                o_tm[:, hq * 64:(hq + 1) * 64],
                                self.ps_b[:, hs * 64:(hs + 1) * 64],
                                AF.Copy, scale=rell(hq))
                        return last
                    ev_toks.append(self.run("scalar", [tAV], sev))
            t_prev_o = self.dma("osp", self.s_o[row0 + qt * 128:row0 + (qt + 1) * 128, :],
                                o_tm, [ev_toks[-1]], q="gpsimd")
        t_attn = t_prev_o

        t_w = wfull("wo1", [t_attn])
        ofm = astr[:, 0:4096]
        x2fm = region0[:, 0:4096]
        stage_o = region0[:, 4096:4096 + 4096]
        stage_h = region0[:, 8192:8192 + 4096]
        t_sl_prev = None
        sl_toks = []
        for sl in range(4):
            r0 = row0 + sl * 512
            tld = self.dma_many("old", [
                (stage_o[:, i * D:(i + 1) * D],
                 self.s_o[r0 + i * 128:r0 + (i + 1) * 128, :]) for i in range(4)],
                deps=[t_attn] + ([t_sl_prev] if t_sl_prev else []), q="sync")
            tldh = self.dma_many("hld2", [
                (stage_h[:, i * D:(i + 1) * D],
                 self.s_h[r0 + i * 128:r0 + (i + 1) * 128, :]) for i in range(4)],
                deps=[t_sl_prev] if t_sl_prev else [], q="sync")
            ttoks = []
            for i in range(4):
                ttr = self.tp_tile_to_fm(
                    stage_o[:, i * D:(i + 1) * D],
                    lambda c, i=i: ofm[:, c * 512 + i * 128:c * 512 + (i + 1) * 128],
                    [tld] + ([ttoks[-1]] if ttoks else []))
                ttoks.append(ttr)
            t_ofm = self.join(ttoks)
            ev = {}
            for m in range(DC):
                ps = self.ps_a if m % 2 == 0 else self.ps_b
                dep = [t_w, t_ofm]
                if m >= 2:
                    dep.append(ev[m - 2])

                def mmf(t, m=m, ps=ps):
                    last = None
                    for k in range(DC):
                        last = t.matmul(ps[:, 0:512],
                                        self.wbuf[:, m * 1024 + k * 128:
                                                   m * 1024 + (k + 1) * 128],
                                        ofm[:, k * 512:(k + 1) * 512],
                                        start=(k == 0), stop=(k == DC - 1))
                    return last
                tmm = self.run("tensor", dep, mmf)
                ev[m] = self.run("scalar", [tmm], lambda s, m=m, ps=ps: s.copy(
                    x2fm[:, m * 512:(m + 1) * 512], ps[:, 0:512]))
            prev_va = None
            for i in range(4):
                def tf(t, i=i):
                    last = None
                    for c in range(4):
                        last = t.transpose(self.ps_c[:, c * 128:(c + 1) * 128],
                                           x2fm[:, c * 512 + i * 128:c * 512 + (i + 1) * 128],
                                           self.ident[:])
                    for c in range(4, 8):
                        last = t.transpose(self.ps_d[:, (c - 4) * 128:(c - 3) * 128],
                                           x2fm[:, c * 512 + i * 128:c * 512 + (i + 1) * 128],
                                           self.ident[:])
                    return last
                ttr2 = self.run("tensor", [ev[DC - 1], ev[DC - 2]] +
                                ([prev_va] if prev_va else []), tf)

                def va(v, i=i):
                    v.tensor_tensor(stage_o[:, i * D:i * D + 512], self.ps_c[:],
                                    stage_h[:, i * D:i * D + 512], ALU.add)
                    return v.tensor_tensor(stage_o[:, i * D + 512:(i + 1) * D],
                                           self.ps_d[:],
                                           stage_h[:, i * D + 512:(i + 1) * D], ALU.add)
                tva = self.run("vector", [ttr2, tldh], va)
                prev_va = tva
                sl_toks.append(self.dma("x2sp", self.s_x2[r0 + i * 128:r0 + (i + 1) * 128, :],
                                        stage_o[:, i * D:(i + 1) * D], [tva],
                                        q="gpsimd"))
            t_sl_prev = self.join(sl_toks[-4:])
        t_x2 = self.join(sl_toks)
        self.dump_dram("d_x2", self.s_x2, (row0, row0 + T), [t_x2])
        return t_x2

    def phaseC(self, b, t_prev):
        A = self.arena
        A.reset(0)
        stage_x2 = A.alloc(4 * D)
        buf_a = A.alloc(4 * D)
        buf_fm = A.alloc(4096)
        hid = A.alloc(16 * 512)
        gbuf = A.alloc(512)
        row0 = b * T
        eps_rms = self.smalls[:, 31:32]
        t0 = self.run("vector", [t_prev], lambda v: v.memset(eps_rms, RMS_EPS))

        t_sl = t0
        for sl in range(4):
            r0 = row0 + sl * 512
            tldx = self.dma_many("x2ld", [
                (stage_x2[:, i * D:(i + 1) * D],
                 self.s_x2[r0 + i * 128:r0 + (i + 1) * 128, :]) for i in range(4)],
                deps=[t_sl], q="sync")
            ttoks = []
            for i in range(4):
                xt = stage_x2[:, i * D:(i + 1) * D]
                bt = buf_a[:, i * D:(i + 1) * D]
                tln = self.ln_tile(xt, bt, bt, eps_rms, [tldx], rms=True,
                                   gamma=self.gam["ln2_1"][:])
                ttr = self.tp_tile_to_fm(
                    bt, lambda c, i=i: buf_fm[:, c * 512 + i * 128:c * 512 + (i + 1) * 128],
                    [tln] + ([ttoks[-1]] if ttoks else []))
                ttoks.append(ttr)
            t_b2 = self.join(ttoks)

            t_wd_ev = {}
            prev_ev = {}
            t_hid = []
            for half in range(2):
                hid_toks = []
                for fi in range(16):
                    f = half * 16 + fi
                    twg = self.dma("wgld", self.wbuf[:, 0:1024],
                                   self.w["wg1"][:, f * 1024:(f + 1) * 1024],
                                   [prev_ev.get("g", t_b2)], q="sync")
                    twu = self.dma("wuld", self.wbuf[:, 1024:2048],
                                   self.w["wu1"][:, f * 1024:(f + 1) * 1024],
                                   [prev_ev.get("u", t_b2)], q="sync")

                    def mmg(t):
                        last = None
                        for k in range(DC):
                            last = t.matmul(self.ps_a[:, 0:512],
                                            self.wbuf[:, k * 128:(k + 1) * 128],
                                            buf_fm[:, k * 512:(k + 1) * 512],
                                            start=(k == 0), stop=(k == DC - 1))
                        return last
                    tg = self.run("tensor", [twg, t_b2] +
                                  ([prev_ev["gv"]] if "gv" in prev_ev else []), mmg)
                    prev_ev["g"] = tg
                    tsil = self.run("scalar", [tg], lambda s: s.activation(
                        gbuf, self.ps_a[:, 0:512], AF.Silu))

                    def mmu(t):
                        last = None
                        for k in range(DC):
                            last = t.matmul(self.ps_b[:, 0:512],
                                            self.wbuf[:, 1024 + k * 128:1024 + (k + 1) * 128],
                                            buf_fm[:, k * 512:(k + 1) * 512],
                                            start=(k == 0), stop=(k == DC - 1))
                        return last
                    tu = self.run("tensor", [twu, tg] +
                                  ([prev_ev["uv"]] if "uv" in prev_ev else []), mmu)
                    prev_ev["u"] = tu
                    thid = self.run("vector", [tu, tsil], lambda v, fi=fi:
                                    v.tensor_tensor(hid[:, fi * 512:(fi + 1) * 512],
                                                    gbuf, self.ps_b[:, 0:512], ALU.mult))
                    prev_ev["gv"] = thid
                    prev_ev["uv"] = thid
                    hid_toks.append(thid)
                t_hid_half = self.join(hid_toks)
                twd_prev = None
                for m in range(DC):
                    if m >= 2:
                        wdep = [t_wd_ev[(m - 2, half)]]
                    elif half == 1:
                        wdep = [t_wd_ev[(m + 6, 0)]]
                    else:
                        wdep = [t_hid_half]
                    twd = self.dma("wdld%d" % (m % 2),
                                   self.wbuf[:, 2048 + (m % 2) * 2048:
                                             2048 + (m % 2) * 2048 + 2048],
                                   self.w["wd1"][:, m * 4096 + half * 2048:
                                                 m * 4096 + half * 2048 + 2048],
                                   wdep, q="sync")

                    psd = self.ps_c if m % 2 == 0 else self.ps_d

                    def mmd(t, m=m, psd=psd):
                        last = None
                        for kk in range(16):
                            last = t.matmul(
                                psd[:, 0:512],
                                self.wbuf[:, 2048 + (m % 2) * 2048 + kk * 128:
                                           2048 + (m % 2) * 2048 + (kk + 1) * 128],
                                hid[:, kk * 512:(kk + 1) * 512],
                                start=(kk == 0), stop=(kk == 15))
                        return last
                    dep = [twd, t_hid_half]
                    if (m, half) != (0, 0) and twd_prev:
                        dep.append(twd_prev)
                    if half == 1:
                        dep.append(t_wd_ev[m])
                    tmmd = self.run("tensor", dep +
                                    ([t_wd_ev.get((m - 2, half))] if (m - 2, half)
                                     in t_wd_ev else []), mmd)
                    twd_prev = tmmd
                    if half == 0:
                        tev = self.run("scalar", [tmmd], lambda s, m=m, psd=psd:
                                       s.copy(buf_a[:, m * 512:(m + 1) * 512],
                                              psd[:, 0:512]))
                        t_wd_ev[m] = tev
                    else:
                        tev = self.run("vector", [tmmd], lambda v, m=m, psd=psd:
                                       v.tensor_tensor(buf_a[:, m * 512:(m + 1) * 512],
                                                       buf_a[:, m * 512:(m + 1) * 512],
                                                       psd[:, 0:512], ALU.add))
                    t_wd_ev[(m, half)] = tev
                t_hid.append(tev)
            t_ffn = self.join([t_wd_ev[(m, 1)] for m in range(DC)])

            prev_va = None
            sp_toks = []
            for i in range(4):
                def tf(t, i=i):
                    last = None
                    for c in range(4):
                        last = t.transpose(self.ps_c[:, c * 128:(c + 1) * 128],
                                           buf_a[:, c * 512 + i * 128:c * 512 + (i + 1) * 128],
                                           self.ident[:])
                    for c in range(4, 8):
                        last = t.transpose(self.ps_d[:, (c - 4) * 128:(c - 3) * 128],
                                           buf_a[:, c * 512 + i * 128:c * 512 + (i + 1) * 128],
                                           self.ident[:])
                    return last
                ttr = self.run("tensor", [t_ffn] + ([prev_va] if prev_va else []), tf)

                def va(v, i=i):
                    v.tensor_tensor(hid[:, i * D:i * D + 512], self.ps_c[:],
                                    stage_x2[:, i * D:i * D + 512], ALU.add)
                    return v.tensor_tensor(hid[:, i * D + 512:(i + 1) * D], self.ps_d[:],
                                           stage_x2[:, i * D + 512:(i + 1) * D], ALU.add)
                tva = self.run("vector", [ttr], va)
                prev_va = tva
                sp_toks.append(self.dma("prsp",
                                        self.s_proc[r0 + i * 128:r0 + (i + 1) * 128, :],
                                        hid[:, i * D:(i + 1) * D], [tva], q="gpsimd"))
            t_sl = self.join(sp_toks)
        t_c1 = t_sl
        self.dump_dram("d_proc", self.s_proc, (row0, row0 + T), [t_c1])

        stage_h = stage_x2
        resb = buf_a
        phb = hid[:, 0:4096]
        mtb = hid[:, 4096:8192]
        neghalf = self.consts[:, 0:1]
        t_sl = t_c1
        for sl in range(4):
            r0 = row0 + sl * 512
            tldh = self.dma_many("hld3", [
                (stage_h[:, i * D:(i + 1) * D],
                 self.s_h[r0 + i * 128:r0 + (i + 1) * 128, :]) for i in range(4)],
                deps=[t_sl], q="sync")
            tldp = self.dma_many("prld", [
                (resb[:, i * D:(i + 1) * D],
                 self.s_proc[r0 + i * 128:r0 + (i + 1) * 128, :]) for i in range(4)],
                deps=[t_sl], q="sync")
            t_res = []
            for i in range(4):
                def vres(v, i=i):
                    v.tensor_tensor(resb[:, i * D:(i + 1) * D],
                                    resb[:, i * D:(i + 1) * D],
                                    stage_h[:, i * D:(i + 1) * D], ALU.subtract)
                    v.drain()
                    v.tensor_tensor(buf_fm[:, 0:1024], resb[:, i * D:(i + 1) * D],
                                    resb[:, i * D:(i + 1) * D], ALU.mult)
                    v.drain()
                    return v.tensor_reduce(self.stat[:, 16 + i:17 + i],
                                           buf_fm[:, 0:1024], axis=AX.X, op=ALU.add)
                t_res.append(self.run("vector", [tldh, tldp], vres))
            ttoks = []
            for i in range(4):
                ht = stage_h[:, i * D:(i + 1) * D]
                pt = phb[:, i * D:(i + 1) * D]
                tln = self.ln_tile(ht, pt, pt, eps_rms, [tldh, t_res[i]], rms=True,
                                   gamma=self.gam["prior_nw"][:], stat_base=8)
                ttr = self.tp_tile_to_fm(
                    pt, lambda c, i=i: buf_fm[:, c * 512 + i * 128:c * 512 + (i + 1) * 128],
                    [tln] + ([ttoks[-1]] if ttoks else []) + [t_res[3]])
                ttoks.append(ttr)
            t_ph = self.join(ttoks)
            prev_part_last = [t_ph]
            for part in range(2):
                ev = {}
                tw_prev = None
                for mi in range(DC):
                    m = part * 8 + mi
                    tw = self.dma("prwld%d" % (mi % 2),
                                  self.wbuf[:, (mi % 2) * 1024:(mi % 2) * 1024 + 1024],
                                  self.w["prior_w"][:, m * 1024:(m + 1) * 1024],
                                  [ev[mi - 2]] if mi >= 2 else prev_part_last,
                                  q="sync")
                    ps = self.ps_a if mi % 2 == 0 else self.ps_b

                    def mmp(t, mi=mi, ps=ps):
                        last = None
                        for k in range(DC):
                            last = t.matmul(ps[:, 0:512],
                                            self.wbuf[:, (mi % 2) * 1024 + k * 128:
                                                       (mi % 2) * 1024 + (k + 1) * 128],
                                            buf_fm[:, k * 512:(k + 1) * 512],
                                            start=(k == 0), stop=(k == DC - 1))
                        return last
                    dep = [tw, t_ph]
                    if tw_prev:
                        dep.append(tw_prev)
                    tmm = self.run("tensor", dep, mmp)
                    tw_prev = tmm
                    tev = self.run("scalar", [tmm], lambda s, ps=ps: s.copy(
                        gbuf, ps[:, 0:512]))
                    psd = self.ps_c if mi % 2 == 0 else self.ps_d

                    def ttf(t, psd=psd):
                        last = None
                        for i in range(4):
                            last = t.transpose(psd[:, i * 128:(i + 1) * 128],
                                               gbuf[:, i * 128:(i + 1) * 128],
                                               self.ident[:])
                        return last
                    dep2 = [tev]
                    if mi >= 2 and (mi - 2, part) in ev:
                        pass
                    ttp = self.run("tensor", dep2, ttf)
                    tev2 = self.run("vector", [ttp], lambda v, mi=mi, psd=psd:
                                    v.tensor_copy(
                                        _mtb_view(mtb, mi), psd[:, 0:512]))
                    ev[mi] = tev2
                t_part = self.join([ev[mi] for mi in range(DC)])
                prev_part_last = [t_part]
                if part == 0:
                    t_mu = []
                    for i in range(4):
                        def vmu(v, i=i):
                            v.tensor_tensor(resb[:, i * D:(i + 1) * D],
                                            resb[:, i * D:(i + 1) * D],
                                            mtb[:, i * D:(i + 1) * D], ALU.subtract)
                            v.drain()
                            return v.tensor_tensor(resb[:, i * D:(i + 1) * D],
                                                   resb[:, i * D:(i + 1) * D],
                                                   resb[:, i * D:(i + 1) * D], ALU.mult)
                        t_mu.append(self.run("vector", [t_part], vmu))
                    t_mu = self.join(t_mu)
                else:
                    t_dch = []
                    for i in range(4):
                        lv = mtb[:, i * D:(i + 1) * D]
                        t2 = resb[:, i * D:(i + 1) * D]
                        ei = phb[:, i * D:(i + 1) * D]
                        te = self.run("scalar", [t_part, t_mu], lambda s, lv=lv, ei=ei:
                                      s.activation(ei, lv, AF.Exp, scale=-1.0))

                        def vw(v, t2=t2, ei=ei, lv=lv):
                            v.scalar_tensor_tensor(t2, t2, 1.0, ei,
                                                   op0=ALU.add, op1=ALU.mult)
                            v.drain()
                            return v.tensor_tensor(t2, t2, lv, ALU.add)
                        tv = self.run("vector", [te], vw)
                        td = self.run("scalar", [tv], lambda s, t2=t2, i=i:
                                      s.activation(phb[:, i * D:(i + 1) * D], t2, AF.Copy,
                                                   scale=0.5,
                                                   accum_out=self.stat[:, 20 + i:21 + i]))
                        t_dch.append(td)
                    t_dch = self.join(t_dch)
            def vs(v):
                v.drain()
                for i in range(4):
                    qt = sl * 4 + i
                    v.tensor_tensor(self.s_all[:, b * TQ + qt:b * TQ + qt + 1],
                                    self.stat[:, 20 + i:21 + i],
                                    self.stat[:, 16 + i:17 + i], ALU.add)
                v.drain()
                last = None
                for i in range(4):
                    qt = sl * 4 + i
                    last = v.tensor_scalar(
                        self.s_all[:, b * TQ + qt:b * TQ + qt + 1],
                        self.s_all[:, b * TQ + qt:b * TQ + qt + 1],
                        1.0 / D, -0.5, op0=ALU.mult, op1=ALU.add)
                return last
            t_s = self.run("vector", [t_dch], vs)
            t_sl = self.join([t_s, t_dch])
        if "d_s" in self.dbg and b == NB - 1:
            self.dma("dbg", self.dbg["d_s"], self.s_all[:], [t_sl], q="gpsimd")
        return t_sl

    def phase_shift(self, t_prev):
        ones_col = self.smalls[:, 32:33]
        tv = self.run("vector", [t_prev], lambda v: v.memset(ones_col, 1.0))
        tmm = self.run("tensor", [tv, t_prev], lambda t: t.matmul(
            self.ps_a[0:1, 0:NB * TQ], ones_col, self.s_all[:],
            start=True, stop=True))
        def vr(v):
            v.tensor_copy(self.pbuf[0:1, 0:NB * TQ], self.ps_a[0:1, 0:NB * TQ])
            v.drain()
            return v.tensor_reduce(self.smalls[0:1, 33:34],
                                   self.pbuf[0:1, 0:NB * TQ],
                                   axis=AX.X, op=ALU.add)
        tvr = self.run("vector", [tmm], vr)
        t1 = self.dma("ccio", self.cc_in_t.ap()[0:1, 0:1],
                      self.smalls[0:1, 33:34], [tvr], q="gpsimd")

        def gcc(g):
            return g.collective_compute(
                "AllReduce", ALU.add,
                ins=[self.cc_in_t.ap().opt()], outs=[self.cc_out_t.ap().opt()],
                replica_groups=[list(range(8))])
        tcc = self.run("gpsimd", [t1], gcc)
        t2 = self.dma("ccio", self.smalls[0:1, 35:36], self.cc_out_t.ap()[0:1, 0:1],
                      [tcc], q="gpsimd")
        tv2 = self.run("vector", [t2], lambda v: v.tensor_scalar(
            self.smalls[0:1, 36:37], self.smalls[0:1, 35:36],
            -1.0 / (16.0 * T), None, op0=ALU.mult))
        t3 = self.dma("ccio", self.cc_in_t.ap()[0:1, 1:2], self.smalls[0:1, 36:37],
                      [tv2], q="gpsimd")
        t4 = self.dma("ccio", self.negshift[:],
                      self.cc_in_t.ap()[0:1, 1:2].to_broadcast([128, 1]), [t3],
                      q="gpsimd")
        return t4, tvr

    def phaseD(self, b, t_prev):
        A = self.arena
        A.reset(0)
        Pm = A.alloc(KQ * T)
        PT = A.alloc(TQ * 512)
        bc_s = A.alloc(T)
        bc_x = A.alloc(T)
        bc_r = A.alloc(T)
        ones128 = A.alloc(128)
        stg = A.alloc(1152)
        row0 = b * T
        st = self.stat
        sS = self.smalls[:, 40:41]
        rS = self.smalls[:, 41:42]
        selS = self.smalls[:, 42:43]

        t0 = self.run("vector", [t_prev], lambda v: v.memset(ones128, 1.0))

        tt = self.run("tensor", [t_prev], lambda t: t.transpose(
            self.ps_a[0:TQ, 0:128], self.s_all[:, b * TQ:(b + 1) * TQ],
            self.ident[:]))
        te = self.run("vector", [tt], lambda v: v.tensor_copy(
            self.pbuf[0:TQ, 0:128], self.ps_a[0:TQ, 0:128]))
        prs = [(self.s_bc[0:1, qt * 128:(qt + 1) * 128],
                self.pbuf[qt:qt + 1, 0:128]) for qt in range(TQ)]
        t1 = self.dma_many("dbc", prs, [te], q="gpsimd")
        t2 = self.dma("dbc", bc_s, self.s_bc[0:1, 0:T].to_broadcast([128, T]),
                      [t1], q="gpsimd")

        rank_toks = []
        for qt in range(TQ):
            scol = self.s_all[:, b * TQ + qt:b * TQ + qt + 1]
            tsc = self.run("scalar", [t2] + ([rank_toks[-1]] if rank_toks else []),
                           lambda s, scol=scol: s.copy(sS, scol))
            def vgt(v):
                v.tensor_scalar(bc_x, bc_s, sS, 0.0, op0=ALU.is_gt,
                                op1=ALU.add, accum_out=st[:, 0:1])
                v.drain()
                return v.tensor_scalar(bc_x, bc_s, sS, None, op0=ALU.is_equal)
            tv1 = self.run("vector", [tsc], vgt)
            tg1 = self.run("gpsimd", [tv1], lambda g, qt=qt: g.affine_select(
                bc_x, bc_x, pattern=[[-1, T]], compare_op=ALU.is_ge,
                fill=0.0, base=qt * 128 - 1, channel_multiplier=1))

            def veq(v, qt=qt):
                v.tensor_reduce(st[:, 1:2], bc_x, axis=AX.X, op=ALU.add)
                v.drain()
                v.tensor_tensor(self.rank_all[:, b * TQ + qt:b * TQ + qt + 1],
                                st[:, 0:1], st[:, 1:2], ALU.add)
                v.drain()
                return v.tensor_scalar(
                    self.sel_all[:, b * TQ + qt:b * TQ + qt + 1],
                    self.rank_all[:, b * TQ + qt:b * TQ + qt + 1],
                    float(KSEL), None, op0=ALU.is_lt)
            rank_toks.append(self.run("vector", [tg1], veq))
        t_rank = self.join(rank_toks)

        def tmr(t):
            last = None
            for iq in range(TQ):
                for jq in range(iq + 1):
                    lhs = self.tri_up[:] if jq == iq else ones128
                    last = t.matmul(self.ps_b[:, iq:iq + 1], lhs,
                                    self.sel_all[:, b * TQ + jq:b * TQ + jq + 1],
                                    start=(jq == 0), stop=(jq == iq))
            return last
        tr = self.run("tensor", [t_rank, t0], tmr)
        tre = self.run("vector", [tr], lambda v: v.tensor_copy(
            self.r_all[:, b * TQ:(b + 1) * TQ], self.ps_b[:, 0:TQ]))

        def tt2(t):
            t.transpose(self.ps_a[0:TQ, 0:128], self.sel_all[:, b * TQ:(b + 1) * TQ],
                        self.ident[:])
            return t.transpose(self.ps_a[0:TQ, 128:256],
                               self.r_all[:, b * TQ:(b + 1) * TQ], self.ident[:])
        tt2k = self.run("tensor", [tre], tt2)
        te2 = self.run("vector", [tt2k], lambda v: v.tensor_copy(
            self.pbuf[0:TQ, 0:256], self.ps_a[0:TQ, 0:256]))
        prs = []
        for qt in range(TQ):
            prs.append((self.s_bc[0:1, qt * 128:(qt + 1) * 128],
                        self.pbuf[qt:qt + 1, 0:128]))
            prs.append((self.s_bc2[0:1, qt * 128:(qt + 1) * 128],
                        self.pbuf[qt:qt + 1, 128:256]))
        t3 = self.dma_many("dbc", prs, [te2], q="gpsimd")
        bc_sel = bc_s
        t4a = self.dma("dbc", bc_sel, self.s_bc[0:1, 0:T].to_broadcast([128, T]),
                       [t3], q="gpsimd")
        t4b = self.dma("dbc", bc_r, self.s_bc2[0:1, 0:T].to_broadcast([128, T]),
                       [t3], q="gpsimd")

        pt_toks = []
        for iq in range(TQ):
            tsc = self.run("scalar", [tre, t_rank] +
                           ([pt_toks[-1]] if pt_toks else []),
                           lambda s, iq=iq: (
                               s.copy(rS, self.r_all[:, b * TQ + iq:b * TQ + iq + 1]),
                               s.copy(selS, self.sel_all[:, b * TQ + iq:b * TQ + iq + 1])
                           )[-1])
            pt_toks.append(self.run("vector", [tsc], lambda v, iq=iq: v.tensor_scalar(
                PT[:, iq * 512:(iq + 1) * 512], self.iota512[:], rS, selS,
                op0=ALU.is_equal, op1=ALU.mult)))
        t_PT = self.join(pt_toks)
        pm_toks = []
        for jq in range(KQ):
            jcol = self.smalls[:, 50 + jq:51 + jq]
            tg = self.run("gpsimd", [t4a, t4b], lambda g, jq=jq, jcol=jcol:
                          g.tensor_scalar(jcol, self.smalls[:, 8:9], float(jq * 128),
                                          None, op0=ALU.add))
            def vpm(v, jq=jq, jcol=jcol):
                v.tensor_scalar(Pm[:, jq * T:(jq + 1) * T], bc_r, jcol, None,
                                op0=ALU.is_equal)
                v.drain()
                return v.tensor_tensor(Pm[:, jq * T:(jq + 1) * T],
                                       Pm[:, jq * T:(jq + 1) * T], bc_sel, ALU.mult)
            pm_toks.append(self.run("vector", [tg, t4a, t4b], vpm))
        t_Pm = self.join(pm_toks)

        t_prev_g = None
        for jq in range(KQ):
            tldp = {}
            mm_toks = []
            for iq in range(TQ):
                pb = bc_x[:, 0:1024] if iq % 2 == 0 else bc_r[:, 0:1024]
                tldp[iq] = self.dma_many(
                    "gpr%d" % (iq % 2),
                    [(pb, self.s_proc[row0 + iq * 128:row0 + (iq + 1) * 128, :])],
                    [t_PT, t_Pm] + ([mm_toks[-2]] if len(mm_toks) >= 2 else []),
                    q="sync")

                def gmm(t, iq=iq, jq=jq, pb=pb, first=(iq == 0), last=(iq == TQ - 1)):
                    lhs = PT[:, iq * 512 + jq * 128:iq * 512 + (jq + 1) * 128]
                    t.matmul(self.ps_s[:, 0:512], lhs, pb[:, 0:512],
                             start=first, stop=last)
                    t.matmul(self.ps_s[:, 512:1024], lhs, pb[:, 512:1024],
                             start=first, stop=last)
                    t.matmul(self.ps_s[:, 1024:1056], lhs,
                             self.cos_tm[:, iq * 32:(iq + 1) * 32],
                             start=first, stop=last)
                    t.matmul(self.ps_s[:, 1536:1568], lhs,
                             self.sin_tm[:, iq * 32:(iq + 1) * 32],
                             start=first, stop=last)
                    return t.matmul(self.ps_a[:, 0:1], lhs,
                                    self.s_all[:, b * TQ + iq:b * TQ + iq + 1],
                                    start=first, stop=last)
                dep = [tldp[iq], t_PT]
                if t_prev_g and iq == 0:
                    dep.append(t_prev_g)
                mm_toks.append(self.run("tensor", dep, gmm))
            def vev(v):
                v.tensor_copy(stg[:, 0:1056], self.ps_s[:, 0:1056])
                v.tensor_copy(stg[:, 1056:1088], self.ps_s[:, 1536:1568])
                return v.tensor_copy(stg[:, 1088:1089], self.ps_a[:, 0:1])
            tev = self.run("vector", [mm_toks[-1]], vev)
            rr = b * KSEL + jq * 128
            tsp = self.dma_many("gsp", [
                (self.s_selg[rr:rr + 128, :], stg[:, 0:1024]),
                (self.s_ropeg[rr:rr + 128, 0:65], stg[:, 1024:1089])],
                [tev], q="gpsimd")
            t_prev_g = self.join([tev, tsp])
        if "d_sel" in self.dbg and b == NB - 1:
            self.dma("dbg", self.dbg["d_sel"], self.sel_all[:], [t_rank], q="gpsimd")
        if "d_r" in self.dbg and b == NB - 1:
            self.dma("dbg", self.dbg["d_r"], self.r_all[:], [tre], q="gpsimd")
        if "d_selg" in self.dbg:
            self.dma_dbg_rows("d_selg", self.s_selg, b * KSEL, KSEL, [t_prev_g])
        if "d_rope" in self.dbg:
            self.dma("dbg", self.dbg["d_rope"][b * KSEL:(b + 1) * KSEL, :],
                     self.s_ropeg[b * KSEL:(b + 1) * KSEL, 0:64],
                     [t_prev_g], q="gpsimd")
        return t_prev_g

    def dma_dbg_rows(self, name, src, r0, n, deps):
        return self.dma("dbg", self.dbg[name][r0:r0 + n, :], src[r0:r0 + n, :],
                        deps, q="gpsimd")

    def phaseE(self, b, t_prev, t_shift):
        A = self.arena
        A.reset(KQ * T)
        K2s = A.alloc(4096)
        Q2s = A.alloc(4096)
        V2s = A.alloc(4096)
        O2s = A.alloc(4096)
        Pm = self.arena.t[:, 0:KQ * T]
        cos2P = O2s[:, 0:512]
        sgn2P = O2s[:, 512:1024]
        row0 = b * T
        rsel = b * KSEL
        eps_rms = self.smalls[:, 31:32]
        gbuf = self.pbuf[:, 1536:2048]

        trg = self.dma_many("rgld", [
            (K2s[:, i * 64:(i + 1) * 64],
             self.s_ropeg[rsel + i * 128:rsel + (i + 1) * 128, 0:64])
            for i in range(KQ)], [t_prev], q="sync")

        def t_rt(t):
            last = None
            for i in range(KQ):
                t.transpose(self.ps_c[0:32, i * 128:(i + 1) * 128],
                            K2s[:, i * 64:i * 64 + 32], self.ident[:])
                last = t.transpose(self.ps_d[0:32, i * 128:(i + 1) * 128],
                                   K2s[:, i * 64 + 32:i * 64 + 64], self.ident[:])
            return last
        ttr = self.run("tensor", [trg], t_rt)

        def vrt(v):
            v.tensor_copy(cos2P[0:32, :], self.ps_c[0:32, 0:512])
            return v.tensor_copy(sgn2P[0:32, :], self.ps_d[0:32, 0:512])
        tvrt = self.run("vector", [ttr], vrt)
        trep = self.dma_many("rrep", [
            (cos2P[g:g + 32, :], cos2P[0:32, :]) for g in (32, 64, 96)] + [
            (sgn2P[g:g + 32, :], sgn2P[0:32, :]) for g in (32, 64, 96)],
            [tvrt], q="gpsimd")
        tsg = self.run("scalar", [trep], lambda s: (
            s.mul(sgn2P[0:32, :], sgn2P[0:32, :], -1.0),
            s.mul(sgn2P[64:96, :], sgn2P[64:96, :], -1.0))[-1])

        tsel = self.dma_many("seld", [
            (K2s[:, i * D:(i + 1) * D],
             self.s_selg[rsel + i * 128:rsel + (i + 1) * 128, :])
            for i in range(KQ)], [tvrt], q="sync")
        ttoks = []
        for i in range(KQ):
            xt = K2s[:, i * D:(i + 1) * D]
            at = Q2s[:, i * D:(i + 1) * D]
            tln = self.ln_tile(xt, at, at, eps_rms, [tsel], rms=True,
                               gamma=self.gam["ln1_2"][:])
            ttr2 = self.tp_tile_to_fm(
                at, lambda c, i=i: V2s[:, c * 512 + i * 128:c * 512 + (i + 1) * 128],
                [tln] + ([ttoks[-1]] if ttoks else []))
            ttoks.append(ttr2)
        t_a2 = self.join(ttoks)

        def proj8(wname, dst, deps, evac_eng="scalar"):
            tw = self.dma_many("wfull", [
                (self.wbuf[:, m * 1024:(m + 1) * 1024],
                 self.w[wname][:, m * 1024:(m + 1) * 1024]) for m in range(DC)],
                deps, q="sync")
            evs = []
            for m in range(DC):
                ps = self.ps_a if m % 2 == 0 else self.ps_b

                def mmf(t, m=m, ps=ps):
                    last = None
                    for k in range(DC):
                        last = t.matmul(ps[:, 0:512],
                                        self.wbuf[:, m * 1024 + k * 128:
                                                   m * 1024 + (k + 1) * 128],
                                        V2s[:, k * 512:(k + 1) * 512],
                                        start=(k == 0), stop=(k == DC - 1))
                    return last
                dep = [tw, t_a2] + ([evs[m - 2]] if m >= 2 else list(deps))
                tmm = self.run("tensor", dep, mmf)
                evs.append(self.run(evac_eng, [tmm], lambda s, m=m, ps=ps: s.copy(
                    dst[:, m * 512:(m + 1) * 512], ps[:, 0:512])))
            return self.join(evs), tmm

        def rope8(dst, deps):
            toks = list(deps)
            for m in range(DC):
                psw = self.ps_c if m % 2 == 0 else self.ps_d
                sl = dst[:, m * 512:(m + 1) * 512]
                tsw = self.run("tensor", [toks[-1]], lambda t, sl=sl, psw=psw:
                               t.matmul(psw[:, 0:512], self.perm[:], sl,
                                        start=True, stop=True))

                def vf(v, sl=sl, psw=psw):
                    v.tensor_tensor(gbuf, psw[:, 0:512], sgn2P, ALU.mult)
                    v.tensor_tensor(sl, sl, cos2P, ALU.mult)
                    v.drain()
                    return v.tensor_tensor(sl, sl, gbuf, ALU.add)
                toks.append(self.run("vector", [tsw], vf))
            return self.join(toks)

        t_k2, _ = proj8("wk2", K2s, [t_a2, tsel, tsg])
        t_k2 = rope8(K2s, [t_k2])
        t_q2, _ = proj8("wq2", Q2s, [t_k2])
        t_q2 = rope8(Q2s, [t_q2])
        vsp_toks = []
        tw = self.dma_many("wfull", [
            (self.wbuf[:, m * 1024:(m + 1) * 1024],
             self.w["wv2"][:, m * 1024:(m + 1) * 1024]) for m in range(DC)],
            [t_q2], q="sync")
        prevv = None
        for m in range(DC):
            ps = self.ps_a if m % 2 == 0 else self.ps_b

            def mmf(t, m=m, ps=ps):
                last = None
                for k in range(DC):
                    last = t.matmul(ps[:, 0:512],
                                    self.wbuf[:, m * 1024 + k * 128:
                                               m * 1024 + (k + 1) * 128],
                                    V2s[:, k * 512:(k + 1) * 512],
                                    start=(k == 0), stop=(k == DC - 1))
                return last
            dep = [tw, t_q2] + ([vsp_toks[-2][0]] if m >= 2 else [])
            tmm = self.run("tensor", dep, mmf)
            tev = self.run("scalar", [tmm], lambda s, ps=ps: s.copy(gbuf, ps[:, 0:512]))
            psw = self.ps_c if m % 2 == 0 else self.ps_d

            def tvt(t, psw=psw):
                last = None
                for i in range(4):
                    last = t.transpose(psw[:, i * 128:(i + 1) * 128],
                                       gbuf[:, i * 128:(i + 1) * 128], self.ident[:])
                return last
            dep2 = [tev] + ([vsp_toks[-2][1]] if m >= 2 else [])
            ttp = self.run("tensor", dep2, tvt)
            tvs = self.run("vector", [ttp], lambda v, psw=psw:
                           v.tensor_copy(self.ptb[:, 0:512], psw[:, 0:512]))
            pr = []
            for i in range(4):
                rr = rsel + i * 128
                pr.append((self.s_e[rr:rr + 128, m * 128:(m + 1) * 128],
                           self.ptb[:, i * 128:(i + 1) * 128]))
            tvsp = self.dma_many("vsp2", pr, [tvs], q="gpsimd")
            vsp_toks.append((tev, tvsp))
        t_v2sp = self.join([tt for _, tt in vsp_toks])
        t_v2 = self.dma_many("v2ld", [
            (V2s[:, i * D:(i + 1) * D],
             self.s_e[rsel + i * 128:rsel + (i + 1) * 128, :]) for i in range(KQ)],
            [t_v2sp, t_q2], q="sync")

        st = self.stat
        negm = st[:, 0:1]
        negm8 = st[:, 1:2]
        ell = st[:, 2:3]
        rell = lambda h: st[:, 8 + (h % 8):9 + (h % 8)]
        t_exp_prev = None
        t_pt_prev = None
        ev_prev = None
        o_ev = []
        for qt in range(KQ):
            klen = (qt + 1) * 128
            ev_toks = []
            for h in range(NH):
                hp, hh = h // 2, h % 2

                def smm(t, hp=hp, hh=hh, klen=klen, qt=qt):
                    return t.matmul(
                        self.ps_s[:, 0:klen],
                        Q2s[hh * 64:(hh + 1) * 64,
                            hp * 512 + qt * 128:hp * 512 + (qt + 1) * 128],
                        K2s[hh * 64:(hh + 1) * 64, hp * 512:hp * 512 + klen],
                        start=True, stop=True)
                dep_s = [t_k2, t_q2] + ([t_exp_prev] if t_exp_prev else [])
                tS = self.run("tensor", dep_s, smm)

                def vmx(v, qt=qt, klen=klen):
                    v.tensor_tensor(self.ps_s[:, qt * 128:klen],
                                    self.ps_s[:, qt * 128:klen],
                                    self.causal[:], ALU.add)
                    v.drain()
                    v.tensor_reduce(negm, self.ps_s[:, 0:klen], axis=AX.X,
                                    op=ALU.max, negate=True)
                    v.drain()
                    return v.tensor_scalar(negm8, negm, 0.125, None, op0=ALU.mult)
                tM = self.run("vector", [tS], vmx)
                tE = self.run("scalar", [tM], lambda s, klen=klen: s.activation(
                    self.pbuf[:, 0:klen], self.ps_s[:, 0:klen], AF.Exp,
                    scale=0.125, bias=negm8, accum_out=ell))
                t_exp_prev = tE
                tR = self.run("vector", [tE], lambda v, h=h: v.reciprocal(rell(h), ell))
                nt = qt + 1

                def tptf(t, nt=nt):
                    last = None
                    for i in range(nt):
                        last = t.transpose(self.ps_c[:, i * 128:(i + 1) * 128],
                                           self.pbuf[:, i * 128:(i + 1) * 128],
                                           self.ident[:])
                    return last
                tT = self.run("tensor", [tE] + ([t_pt_prev] if t_pt_prev else []), tptf)
                tEv = self.run("vector", [tT], lambda v, nt=nt: v.tensor_copy(
                    self.ptb[:, 0:nt * 128], self.ps_c[:, 0:nt * 128]))
                t_pt_prev = tEv

                def tavf(t, nt=nt, hh=hh, h=h, hp=hp):
                    lasti = None
                    for i in range(nt):
                        lasti = t.matmul(
                            self.ps_b[:, (h % 8) * 64:(h % 8) * 64 + 64],
                            self.ptb[:, i * 128:(i + 1) * 128],
                            V2s[:, i * 1024 + hp * 128 + hh * 64:
                                i * 1024 + hp * 128 + hh * 64 + 64],
                            start=(i == 0), stop=(i == nt - 1))
                    return lasti
                depav = [tEv, t_v2, tR]
                if h % 8 == 0 and ev_toks:
                    depav.append(ev_toks[-1])
                if h % 8 == 0 and not ev_toks and o_ev:
                    depav.append(o_ev[-1])
                tAV = self.run("tensor", depav, tavf)
                if h % 8 == 7:
                    def sev(s, h=h, qt=qt):
                        last = None
                        for hs in range(8):
                            hq = (h // 8) * 8 + hs
                            last = s.activation(
                                O2s[:, qt * 1024 + hq * 64:qt * 1024 + (hq + 1) * 64],
                                self.ps_b[:, hs * 64:(hs + 1) * 64],
                                AF.Copy, scale=rell(hq))
                        return last
                    dep_ev = [tAV]
                    if qt == 0 and h == 7:
                        dep_ev.append(t_v2)
                    ev_toks.append(self.run("scalar", dep_ev, sev))
            o_ev.append(ev_toks[-1])
        t_o2 = self.join(o_ev)

        ofm = K2s
        ttoks = []
        for i in range(KQ):
            ttr3 = self.tp_tile_to_fm(
                O2s[:, i * D:(i + 1) * D],
                lambda c, i=i: ofm[:, c * 512 + i * 128:c * 512 + (i + 1) * 128],
                [t_o2] + ([ttoks[-1]] if ttoks else []))
            ttoks.append(ttr3)
        t_ofm = self.join(ttoks)
        tw = self.dma_many("wfull", [
            (self.wbuf[:, m * 1024:(m + 1) * 1024],
             self.w["wo2"][:, m * 1024:(m + 1) * 1024]) for m in range(DC)],
            [t_o2], q="sync")
        evs = []
        for m in range(DC):
            ps = self.ps_a if m % 2 == 0 else self.ps_b

            def mmf(t, m=m, ps=ps):
                last = None
                for k in range(DC):
                    last = t.matmul(ps[:, 0:512],
                                    self.wbuf[:, m * 1024 + k * 128:
                                               m * 1024 + (k + 1) * 128],
                                    ofm[:, k * 512:(k + 1) * 512],
                                    start=(k == 0), stop=(k == DC - 1))
                return last
            dep = [tw, t_ofm] + ([evs[m - 2]] if m >= 2 else [])
            tmm = self.run("tensor", dep, mmf)
            evs.append(self.run("scalar", [tmm], lambda s, m=m, ps=ps: s.copy(
                Q2s[:, m * 512:(m + 1) * 512], ps[:, 0:512])))
        t_x2fm = self.join(evs)
        t_selld = self.dma_many("seld2", [
            (V2s[:, i * D:(i + 1) * D],
             self.s_selg[rsel + i * 128:rsel + (i + 1) * 128, :]) for i in range(KQ)],
            [t_x2fm], q="sync")
        x2sp = []
        prev_va = None
        for i in range(KQ):
            def tf(t, i=i):
                last = None
                for c in range(4):
                    last = t.transpose(self.ps_c[:, c * 128:(c + 1) * 128],
                                       Q2s[:, c * 512 + i * 128:c * 512 + (i + 1) * 128],
                                       self.ident[:])
                for c in range(4, 8):
                    last = t.transpose(self.ps_d[:, (c - 4) * 128:(c - 3) * 128],
                                       Q2s[:, c * 512 + i * 128:c * 512 + (i + 1) * 128],
                                       self.ident[:])
                return last
            ttr4 = self.run("tensor", [t_x2fm] + ([prev_va] if prev_va else []), tf)

            def va(v, i=i):
                v.tensor_tensor(O2s[:, i * D:i * D + 512], self.ps_c[:],
                                V2s[:, i * D:i * D + 512], ALU.add)
                return v.tensor_tensor(O2s[:, i * D + 512:(i + 1) * D], self.ps_d[:],
                                       V2s[:, i * D + 512:(i + 1) * D], ALU.add)
            tva = self.run("vector", [ttr4, t_selld], va)
            prev_va = tva
            x2sp.append(self.dma("x2e", self.s_e[rsel + i * 128:rsel + (i + 1) * 128, :],
                                 O2s[:, i * D:(i + 1) * D], [tva], q="gpsimd"))
        t_x2p = self.join(x2sp + [prev_va])

        ttoks = []
        for i in range(KQ):
            xt = O2s[:, i * D:(i + 1) * D]
            bt = V2s[:, i * D:(i + 1) * D]
            tln = self.ln_tile(xt, bt, bt, eps_rms, [t_x2p], rms=True,
                               gamma=self.gam["ln2_2"][:], stat_base=16)
            ttr5 = self.tp_tile_to_fm(
                bt, lambda c, i=i: K2s[:, c * 512 + i * 128:c * 512 + (i + 1) * 128],
                [tln] + ([ttoks[-1]] if ttoks else []))
            ttoks.append(ttr5)
        t_b2p = self.join(ttoks)

        hid = self.arena.t[:, KQ * T + 2 * 4096:KQ * T + 4 * 4096]
        t_wd_ev = {}
        prev_ev = {}
        for half in range(2):
            hid_toks = []
            for fi in range(16):
                f = half * 16 + fi
                twg = self.dma("wgld", self.wbuf[:, 0:1024],
                               self.w["wg2"][:, f * 1024:(f + 1) * 1024],
                               [prev_ev.get("g", t_b2p)], q="sync")
                twu = self.dma("wuld", self.wbuf[:, 1024:2048],
                               self.w["wu2"][:, f * 1024:(f + 1) * 1024],
                               [prev_ev.get("u", t_b2p)], q="sync")

                def mmg(t):
                    last = None
                    for k in range(DC):
                        last = t.matmul(self.ps_a[:, 0:512],
                                        self.wbuf[:, k * 128:(k + 1) * 128],
                                        K2s[:, k * 512:(k + 1) * 512],
                                        start=(k == 0), stop=(k == DC - 1))
                    return last
                tg = self.run("tensor", [twg, t_b2p] +
                              ([prev_ev["gv"]] if "gv" in prev_ev else []), mmg)
                prev_ev["g"] = tg
                tsil = self.run("scalar", [tg], lambda s: s.activation(
                    gbuf, self.ps_a[:, 0:512], AF.Silu))

                def mmu(t):
                    last = None
                    for k in range(DC):
                        last = t.matmul(self.ps_b[:, 0:512],
                                        self.wbuf[:, 1024 + k * 128:1024 + (k + 1) * 128],
                                        K2s[:, k * 512:(k + 1) * 512],
                                        start=(k == 0), stop=(k == DC - 1))
                    return last
                tu = self.run("tensor", [twu, tg] +
                              ([prev_ev["uv"]] if "uv" in prev_ev else []), mmu)
                prev_ev["u"] = tu
                thid = self.run("vector", [tu, tsil], lambda v, fi=fi:
                                v.tensor_tensor(hid[:, fi * 512:(fi + 1) * 512],
                                                gbuf, self.ps_b[:, 0:512], ALU.mult))
                prev_ev["gv"] = thid
                prev_ev["uv"] = thid
                hid_toks.append(thid)
            t_hid_half = self.join(hid_toks)
            twd_prev = None
            for m in range(DC):
                if m >= 2:
                    wdep = [t_wd_ev[(m - 2, half)]]
                elif half == 1:
                    wdep = [t_wd_ev[(m + 6, 0)]]
                else:
                    wdep = [t_hid_half]
                twd = self.dma("wdld%d" % (m % 2),
                               self.wbuf[:, 2048 + (m % 2) * 2048:
                                         2048 + (m % 2) * 2048 + 2048],
                               self.w["wd2"][:, m * 4096 + half * 2048:
                                             m * 4096 + half * 2048 + 2048],
                               wdep, q="sync")
                psd = self.ps_c if m % 2 == 0 else self.ps_d

                def mmd(t, m=m, psd=psd):
                    last = None
                    for kk in range(16):
                        last = t.matmul(
                            psd[:, 0:512],
                            self.wbuf[:, 2048 + (m % 2) * 2048 + kk * 128:
                                       2048 + (m % 2) * 2048 + (kk + 1) * 128],
                            hid[:, kk * 512:(kk + 1) * 512],
                            start=(kk == 0), stop=(kk == 15))
                    return last
                dep = [twd, t_hid_half]
                if twd_prev:
                    dep.append(twd_prev)
                if half == 1:
                    dep.append(t_wd_ev[(m, 0)])
                if (m - 2, half) in t_wd_ev:
                    dep.append(t_wd_ev[(m - 2, half)])
                tmmd = self.run("tensor", dep, mmd)
                twd_prev = tmmd
                if half == 0:
                    tev = self.run("scalar", [tmmd], lambda s, m=m, psd=psd:
                                   s.copy(Q2s[:, m * 512:(m + 1) * 512], psd[:, 0:512]))
                else:
                    tev = self.run("vector", [tmmd], lambda v, m=m, psd=psd:
                                   v.tensor_tensor(Q2s[:, m * 512:(m + 1) * 512],
                                                   Q2s[:, m * 512:(m + 1) * 512],
                                                   psd[:, 0:512], ALU.add))
                t_wd_ev[(m, half)] = tev
        t_ffn = self.join([t_wd_ev[(m, 1)] for m in range(DC)])

        tscl = self.dma_many("scld", [
            (self.smalls[:, 44 + i:45 + i],
             self.s_ropeg[rsel + i * 128:rsel + (i + 1) * 128, 64:65])
            for i in range(KQ)], [t_prev], q="sync", slow=True)
        tsc = self.run("scalar", [tscl, t_shift], lambda s: [
            s.activation(self.sc_sel[:, b * KQ + i:b * KQ + i + 1],
                         self.smalls[:, 44 + i:45 + i], AF.Sigmoid,
                         bias=self.negshift[:]) for i in range(KQ)][-1])
        if "d_sc" in self.dbg and b == NB - 1:
            self.dma("dbg", self.dbg["d_sc"], self.sc_sel[:], [tsc], q="gpsimd")
        t_x2ld = self.dma_many("x2e2", [
            (V2s[:, i * D:(i + 1) * D],
             self.s_e[rsel + i * 128:rsel + (i + 1) * 128, :]) for i in range(KQ)],
            [t_ffn], q="sync")
        t_selld2 = self.dma_many("seld3", [
            (O2s[:, i * D:(i + 1) * D],
             self.s_selg[rsel + i * 128:rsel + (i + 1) * 128, :]) for i in range(KQ)],
            [t_ffn], q="sync")
        u_toks = []
        prev_va = None
        for i in range(KQ):
            def tf(t, i=i):
                last = None
                for c in range(4):
                    last = t.transpose(self.ps_c[:, c * 128:(c + 1) * 128],
                                       Q2s[:, c * 512 + i * 128:c * 512 + (i + 1) * 128],
                                       self.ident[:])
                for c in range(4, 8):
                    last = t.transpose(self.ps_d[:, (c - 4) * 128:(c - 3) * 128],
                                       Q2s[:, c * 512 + i * 128:c * 512 + (i + 1) * 128],
                                       self.ident[:])
                return last
            ttr6 = self.run("tensor", [t_ffn] + ([prev_va] if prev_va else []), tf)

            def vu(v, i=i):
                v.tensor_tensor(K2s[:, i * D:i * D + 512], self.ps_c[:],
                                V2s[:, i * D:i * D + 512], ALU.add)
                v.tensor_tensor(K2s[:, i * D + 512:(i + 1) * D], self.ps_d[:],
                                V2s[:, i * D + 512:(i + 1) * D], ALU.add)
                v.drain()
                v.tensor_tensor(K2s[:, i * D:(i + 1) * D],
                                K2s[:, i * D:(i + 1) * D],
                                O2s[:, i * D:(i + 1) * D], ALU.subtract)
                v.drain()
                return v.tensor_scalar(K2s[:, i * D:(i + 1) * D],
                                       K2s[:, i * D:(i + 1) * D],
                                       self.sc_sel[:, b * KQ + i:b * KQ + i + 1],
                                       None, op0=ALU.mult)
            tvu = self.run("vector", [ttr6, t_x2ld, t_selld2, tsc], vu)
            prev_va = tvu
            u_toks.append(tvu)
        t_u = self.join(u_toks)

        t_prev_f = None
        out_toks = []
        for iq in range(TQ):
            tld1 = self.dma_many("f1ld", [
                (V2s[:, 0:1024], self.s_x1[row0 + iq * 128:row0 + (iq + 1) * 128, :]),
                (V2s[:, 1024:2048],
                 self.s_proc[row0 + iq * 128:row0 + (iq + 1) * 128, :])],
                [t_u] + ([t_prev_f] if t_prev_f else []), q="sync")

            def tsc2(t, iq=iq):
                last = None
                for jq in range(KQ):
                    lhs = Pm[:, jq * T + iq * 128:jq * T + (iq + 1) * 128]
                    for nh in range(2):
                        last = t.matmul(
                            self.ps_s[:, nh * 512:(nh + 1) * 512], lhs,
                            K2s[:, jq * D + nh * 512:jq * D + (nh + 1) * 512],
                            start=(jq == 0), stop=(jq == KQ - 1))
                return last
            dep = [t_u]
            if t_prev_f:
                dep.append(t_prev_f)
            tmm = self.run("tensor", dep, tsc2)

            def vf(v, iq=iq):
                v.tensor_tensor(V2s[:, 2048:3072], self.ps_s[:, 0:1024],
                                V2s[:, 0:1024], ALU.add)
                v.drain()
                return v.tensor_tensor(V2s[:, 2048:3072], V2s[:, 2048:3072],
                                       V2s[:, 1024:2048], ALU.add)
            tvf = self.run("vector", [tmm, tld1], vf)
            tout = self.dma("outw", self.out[row0 + iq * 128:row0 + (iq + 1) * 128, :],
                            V2s[:, 2048:3072], [tvf], q="gpsimd")
            t_prev_f = self.join([tvf, tout])
            out_toks.append(tout)
        return self.join(out_toks + [t_prev_f])

    def finish(self, deps):
        def g(gp):
            for name, (sem, cnt, q) in self.streams.items():
                if cnt:
                    gp.wait_ge(sem, cnt)
            return gp.memset(self.smalls[:, 62:63], 0.0)
        self.run("gpsimd", deps, g)

    def finish_dev(self, deps):
        t1 = self.dma("fin", self.out[:, :], self.s_x1[:, :], deps, q="gpsimd")

        def g(gp):
            for name, (sem, cnt, q) in self.streams.items():
                if cnt:
                    gp.wait_ge(sem, cnt)
            return gp.memset(self.smalls[:, 62:63], 0.0)
        self.run("gpsimd", [t1], g)


def build_program(dbg_names=(), phases=("A",)):
    nc = bass.Bass("TRN2", target_bir_lowering=False)
    nc.detect_race_conditions = False
    es = ExitStack()
    with es:
        block = es.enter_context(nc.Block())
        mk = MK(nc, es, block, dbg_names)
        t = mk.setup()
        toks = list(t)
        tA = mk.join(list(t))
        if "A" in phases:
            for b in range(NB):
                tA = mk.phaseA(b, tA)
            toks.append(tA)
        if "B" in phases:
            for b in range(NB):
                tA = mk.phaseB(b, tA)
            toks.append(tA)
        if "C" in phases:
            for b in range(NB):
                tA = mk.phaseC(b, tA)
            toks.append(tA)
        if "D" in phases:
            t_shift, t_psfree = mk.phase_shift(tA)
            tA = t_psfree
            for b in range(NB):
                tA = mk.phaseD(b, tA)
                if "E" in phases:
                    tA = mk.phaseE(b, tA, t_shift)
            toks.append(tA)
            toks.append(t_shift)
        if "E" in phases:
            mk.finish(toks)
        else:
            mk.finish_dev(toks)
    return nc, mk


def host_inputs(xfull, W, core):
    ident, causal, tri, perm = host_consts()
    m = {
        "x_in": np.ascontiguousarray(
            xfull[2 * core:2 * core + 2].reshape(NB * T, D)),
        "ident_in": ident, "causal_in": causal, "tri_in": tri, "perm_in": perm,
    }
    for nm, arr, transpose in [
        ("w_delta", W["L0_attn_W_delta"], True),
        ("w_select", W["L0_attn_W_select"], True),
        ("w_in", W["L0_attn_W_in"], True),
        ("w_gate", W["L0_attn_W_gate"], True),
        ("w_out", W["L0_attn_W_out"], True),
        ("wq1", W["L0_ffn_wq1"], False), ("wk1", W["L0_ffn_wk1"], False),
        ("wv1", W["L0_ffn_wv1"], False), ("wo1", W["L0_ffn_wo1"], False),
        ("wg1", W["L0_ffn_wg1"], False), ("wu1", W["L0_ffn_wu1"], False),
        ("wd1", W["L0_ffn_wd1"], False),
        ("wq2", W["L0_ffn_wq2"], False), ("wk2", W["L0_ffn_wk2"], False),
        ("wv2", W["L0_ffn_wv2"], False), ("wo2", W["L0_ffn_wo2"], False),
        ("wg2", W["L0_ffn_wg2"], False), ("wu2", W["L0_ffn_wu2"], False),
        ("wd2", W["L0_ffn_wd2"], False),
        ("prior_w", W["L0_ffn_prior_w"], False),
    ]:
        a = np.asarray(arr, np.float32)
        m[nm] = host_prep_weight(np.ascontiguousarray(a.T) if transpose else a)
    for nm, key in [("log_a", "L0_attn_log_a"), ("ln1_1", "L0_ffn_ln1_1"),
                    ("ln2_1", "L0_ffn_ln2_1"), ("ln1_2", "L0_ffn_ln1_2"),
                    ("ln2_2", "L0_ffn_ln2_2"), ("prior_nw", "L0_ffn_prior_norm_w")]:
        m[nm] = np.asarray(W[key], np.float32).reshape(1, D)
    return m



_CACHED = {}


def _get_program():
    if "nc" not in _CACHED:
        _CACHED["nc"], _CACHED["mk"] = build_program(
            dbg_names=(), phases=("A", "B", "C", "D", "E"))
    return _CACHED["nc"]


def kernel(**inputs):
    from concourse.bass_utils import run_bass_kernel_spmd
    x = np.asarray(inputs["L0_attn_x"], np.float32)
    nc = _get_program()
    ims = [host_inputs(x, inputs, core=c) for c in range(8)]
    res = run_bass_kernel_spmd(nc, ims, core_ids=list(range(8)))
    out = np.concatenate([res.results[c]["out"] for c in range(8)], 0)
    return out.reshape(16, T, D)
